# version 20
# baseline (speedup 1.0000x reference)
import contextlib
import numpy as np
import concourse.bass as bass
import concourse.mybir as mybir
from concourse.bass_utils import run_bass_kernel_spmd

F32 = mybir.dt.float32
BF16 = mybir.dt.bfloat16
AF = mybir.ActivationFunctionType
ALU = mybir.AluOpType
AX = mybir.AxisListType

C0 = -0.6931471805599453
NCORES = 8
BPC = 2
N = 2048
D = 1024
H = 8
M = 64
DK = 128
HM = H * M
E = 8
F = 4096
SEG = 128
NSEG = N // SEG
GRP = 1024
NGRP = N // GRP
EPS = 1e-5

DO_FFN = True
DO_ATTN = True
N_SEG_EMIT = 99
PHASE_LIMIT = 99
SCAN_OPS = 99


class Prog:
    ENGINES = ("tensor", "vector", "scalar", "sync", "gpsimd")

    def __init__(self, sems):
        self.streams = {e: [] for e in self.ENGINES}
        self.sems = sems
        self.cnt = {k: 0 for k in sems}

    CHAIN = {"vector": "dve", "scalar": "act", "gpsimd": "gch"}

    def op(self, eng, fname, *args, inc=None, **kw):
        is_dma = "dma" in fname
        chain = None if is_dma else self.CHAIN.get(eng)
        if chain is not None:
            self.wait(eng, chain, self.cnt[chain])
            assert inc is None or inc == chain
            inc = chain
        h = self.sems[inc] if inc else None
        amt = 16 if is_dma else 1
        if inc:
            self.cnt[inc] += amt

        def f(e, fname=fname, args=args, kw=kw, h=h, amt=amt):
            ins = getattr(e, fname)(*args, **kw)
            if h is not None:
                ins.then_inc(h, amt)
        self.streams[eng].append(f)
        return self.cnt[inc] if inc else None

    def wait(self, eng, sem, val):
        if val is None or val <= 0:
            return
        h = self.sems[sem]

        def f(e, h=h, val=val):
            e.wait_ge(h, val)
        self.streams[eng].append(f)

    def bar(self, eng, *sem_names):
        for s in sem_names:
            self.wait(eng, s, self.cnt[s])


def build_nc():
    nc = bass.Bass("TRN2", target_bir_lowering=False)

    x_d = nc.dram_tensor("x", [BPC * N, D], F32, kind="ExternalInput").ap()
    wq_d = nc.dram_tensor("wq", [D, D], F32, kind="ExternalInput").ap()
    wk_d = nc.dram_tensor("wk", [D, D], F32, kind="ExternalInput").ap()
    wv_d = nc.dram_tensor("wv", [D, D], F32, kind="ExternalInput").ap()
    wf1_d = nc.dram_tensor("wf1", [D, DK], F32, kind="ExternalInput").ap()
    wf2_d = nc.dram_tensor("wf2", [DK, HM], F32, kind="ExternalInput").ap()
    wg1_d = nc.dram_tensor("wg1", [D, DK], F32, kind="ExternalInput").ap()
    wg2_d = nc.dram_tensor("wg2", [DK, D], F32, kind="ExternalInput").ap()
    wo_d = nc.dram_tensor("wo", [D, D], F32, kind="ExternalInput").ap()
    nw_d = nc.dram_tensor("normw", [D], F32, kind="ExternalInput").ap()
    wr_d = nc.dram_tensor("wr", [D, E], F32, kind="ExternalInput").ap()
    w1_d = nc.dram_tensor("w1", [E, D, F], F32, kind="ExternalInput").ap()
    b1_d = nc.dram_tensor("b1", [E, F], F32, kind="ExternalInput").ap()
    w2_d = nc.dram_tensor("w2", [E, F, D], F32, kind="ExternalInput").ap()
    b2_d = nc.dram_tensor("b2", [E, D], F32, kind="ExternalInput").ap()
    y_d = nc.dram_tensor("y", [BPC * N, D], F32, kind="ExternalOutput").ap()
    w1sc = nc.dram_tensor("w1sc", [E * D, F], BF16).ap()
    w2sc = nc.dram_tensor("w2sc", [E * F, D], BF16).ap()

    es = contextlib.ExitStack()
    sb = lambda name, shape, dt: es.enter_context(nc.sbuf_tensor(name, shape, dt))
    ps = lambda name, shape, dt: es.enter_context(nc.psum_tensor(name, shape, dt))

    x_buf = sb("x_buf", [128, (N // 128) * D], F32)
    ARENA_F32 = 36000
    arena = sb("arena", [128, ARENA_F32], F32)
    ar_f = arena[:]
    ar_b = ar_f.bitcast(BF16)

    class Alloc:
        def __init__(self):
            self.off = 0

        def f32(self, rows, cols):
            o = self.off
            self.off += ((cols * 4 + 127) // 128) * 128
            assert self.off <= ARENA_F32 * 4, f"arena overflow {self.off}"
            return ar_f[0:rows, o // 4: o // 4 + cols]

        def bf(self, rows, cols):
            o = self.off
            self.off += ((cols * 2 + 127) // 128) * 128
            assert self.off <= ARENA_F32 * 4, f"arena overflow {self.off}"
            return ar_b[0:rows, o // 2: o // 2 + cols]

    aa = Alloc()
    a_wq = aa.bf(128, 8 * D)
    a_wk = aa.bf(128, 8 * D)
    a_wv = aa.bf(128, 8 * D)
    a_wo = aa.bf(128, 8 * D)
    a_wf1 = aa.bf(128, 8 * DK)
    a_wf2 = aa.bf(128, HM)
    a_wg1 = aa.bf(128, 8 * DK)
    a_wg2 = aa.bf(128, D)
    a_nw = aa.f32(128, 8)
    XS = []
    for par in range(2):
        XS.append(dict(
            xnt=aa.bf(128, 8 * SEG),
            qt=aa.bf(128, H * SEG),
            kt=aa.bf(128, H * SEG),
            vtm=aa.bf(128, H * DK),
            ktm=aa.bf(128, H * DK),
            sig=aa.bf(128, HM),
            dlt=aa.bf(128, HM),
            stl=aa.bf(128, HM),
            stb=aa.bf(128, HM),
            B=aa.bf(128, HM),
            sg=aa.bf(128, D),
            bca=aa.bf(128, 2 * HM),
            bl=aa.f32(128, 16),
        ))
    a_qraw = aa.bf(128, D)
    a_xact = aa.bf(128, D)
    a_ut = aa.bf(128, DK)
    a_ugt = aa.bf(128, DK)
    a_xc = aa.f32(128, D)
    a_xsq = aa.bf(128, D)
    a_xns = aa.bf(128, D)
    a_xsm = aa.f32(128, 8)
    a_otmp = aa.f32(128, D)
    a_o2 = aa.bf(128, D)
    a_o2t = aa.bf(128, D)
    a_atm = aa.bf(128, HM)
    a_qb = aa.bf(128, HM)
    a_qbt = aa.bf(128, 8 * M)
    a_stT = aa.bf(128, 8 * M)
    a_hk = aa.bf(128, HM)
    a_hkt = aa.bf(128, HM)
    a_hv = aa.bf(128, 8 * DK)
    a_sm1 = aa.f32(128, 16)
    a_nmx = aa.f32(128, 8)
    a_ssum = aa.f32(128, 8)
    a_rsum = aa.f32(128, 8)
    attn_bytes = aa.off

    fa = Alloc()
    f_xnt0 = fa.bf(128, 8 * GRP)
    f_xnt1 = fa.bf(128, 8 * GRP)
    f_ht = fa.bf(128, 32 * GRP)
    f_w1a = fa.bf(128, 8 * 512)
    f_w1b = fa.bf(128, 8 * 512)
    f_w2a = fa.bf(128, D)
    f_w2b = fa.bf(128, D)
    f_w2c = fa.bf(128, D)
    f_w2d = fa.bf(128, D)
    f_b1n = fa.bf(8, F)
    f_b1t = fa.f32(128, 32 * E)
    f_b2a = fa.bf(8, D)
    f_b2s = fa.bf(1, D)
    f_wr = fa.bf(128, 8 * E)
    f_wg0 = fa.f32(128, 8 * E)
    f_wg1 = fa.f32(128, 8 * E)
    f_rt = fa.f32(128, 128)
    f_xns = fa.bf(128, D)
    f_lnc = fa.f32(128, D)
    f_sm = fa.f32(128, 16)
    ffn_bytes = fa.off
    assert max(attn_bytes, ffn_bytes) <= ARENA_F32 * 4, (attn_bytes, ffn_bytes)

    triI = sb("triI", [128, 64], BF16)
    triL = sb("triL", [128, 64], BF16)
    triI8 = sb("triI8", [128, 512], BF16)
    id64 = sb("id64", [128, 64], BF16)
    id128 = sb("id128", [128, 128], BF16)
    ones128 = sb("ones128", [128, 128], BF16)
    onescol = sb("onescol", [128, 1], BF16)
    onesrow = sb("onesrow", [1, 128], BF16)
    c_lin = sb("c_lin", [128, 1], F32)
    c_nlin = sb("c_nlin", [128, 1], F32)
    c_lrev = sb("c_lrev", [128, 1], F32)
    c_c64 = sb("c_c64", [128, 1], F32)
    c_eps = sb("c_eps", [128, 1], F32)
    c_it = sb("c_it", [128, 1], F32)
    c_it2 = sb("c_it2", [128, 1], F32)

    p_a = ps("p_a", [128, 512], F32)
    p_ok = ps("p_ok", [128, 512], F32)
    p_st = ps("p_st", [128, 512], F32)
    p_st2 = ps("p_st2", [128, 512], F32)
    p_ov = ps("p_ov", [128, 1024], F32)
    p_x = ps("p_x", [128, 512], F32)
    p_tp = ps("p_tp", [128, 1024], BF16)
    p_x_bf = p_x[:].bitcast(BF16)
    p_tp_f = p_tp[:].bitcast(F32)

    sems = {}
    for s in ("pe", "dve", "act", "gs", "gch", "xin", "win", "cvi0", "cvi1",
              "cvo0", "cvo1", "w10", "w11", "w20", "w21", "w22", "w23", "yst"):
        sems[s] = es.enter_context(nc.semaphore("s_" + s))
    P = Prog(sems)
    pe_after_b2 = {}

    def emit_consts():
        g = "gpsimd"
        P.op(g, "memset", triI[:], 1.0)
        P.op(g, "memset", triL[:], 1.0)
        P.op(g, "memset", triI8[:], 1.0)
        P.op(g, "memset", id64[:], 0.0)
        P.op(g, "memset", id128[:], 0.0)
        P.op(g, "memset", ones128[:], 1.0)
        P.op(g, "memset", onescol[:], 1.0)
        P.op(g, "memset", onesrow[:], 1.0)
        P.op(g, "memset", c_c64[:], 64.0 * C0)
        P.op(g, "memset", c_eps[:], EPS)
        for base in (0, 64):
            P.op(g, "affine_select", out=triI[base:base + 64, :], in_=triI[base:base + 64, :],
                 compare_op=ALU.is_ge, fill=0.0, base=0, pattern=[[1, 64]], channel_multiplier=-1)
            P.op(g, "affine_select", out=triL[base:base + 64, :], in_=triL[base:base + 64, :],
                 compare_op=ALU.is_gt, fill=0.0, base=0, pattern=[[-1, 64]], channel_multiplier=1)
            P.op(g, "affine_select", out=id64[base:base + 64, :], in_=id64[base:base + 64, :],
                 compare_op=ALU.not_equal, fill=1.0, base=0, pattern=[[-1, 64]], channel_multiplier=1)
            for hh in range(8):
                P.op(g, "affine_select", out=triI8[base:base + 64, hh * 64:hh * 64 + 64],
                     in_=triI8[base:base + 64, hh * 64:hh * 64 + 64],
                     compare_op=ALU.is_ge, fill=0.0, base=0, pattern=[[1, 64]], channel_multiplier=-1)
        P.op(g, "affine_select", out=id128[:], in_=id128[:], compare_op=ALU.not_equal,
             fill=1.0, base=0, pattern=[[-1, 128]], channel_multiplier=1)
        P.op(g, "iota", c_it[:], pattern=[[0, 1]], base=0, channel_multiplier=1,
             allow_small_or_imprecise_dtypes=True)
        v = "vector"
        P.bar(v, "gch")
        P.op(v, "tensor_scalar", c_it2[:], c_it[:], 64.0, None, ALU.is_ge)
        P.op(v, "scalar_tensor_tensor", c_it[:], c_it2[:], -64.0, c_it[:], ALU.mult, ALU.add)
        P.op(v, "tensor_scalar", c_lin[:], c_it[:], C0, C0, ALU.mult, op1=ALU.add)
        P.op(v, "tensor_scalar", c_nlin[:], c_lin[:], -1.0, None, ALU.mult)
        P.op(v, "tensor_scalar", c_lrev[:], c_it[:], -C0, 63.0 * C0, ALU.mult, op1=ALU.add,
             inc="dve")

    def emit_attn_weight_loads(b):
        g = "gpsimd"
        P.bar(g, "pe", "dve", "act", "yst")
        for dst, src in ((a_wq, wq_d), (a_wk, wk_d), (a_wv, wv_d), (a_wo, wo_d)):
            P.op(g, "dma_start", dst.rearrange("p (dt n) -> p dt n", n=D),
                 src.rearrange("(dt p) n -> p dt n", p=128), inc="win")
        P.op(g, "dma_start", a_wf1.rearrange("p (dt u) -> p dt u", u=DK),
             wf1_d.rearrange("(dt p) u -> p dt u", p=128), inc="win")
        P.op(g, "dma_start", a_wf2, wf2_d, inc="win")
        P.op(g, "dma_start", a_wg1.rearrange("p (dt u) -> p dt u", u=DK),
             wg1_d.rearrange("(dt p) u -> p dt u", p=128), inc="win")
        P.op(g, "dma_start", a_wg2, wg2_d, inc="win")
        win_done = P.op(g, "dma_start", a_nw, nw_d.rearrange("(dt p) -> p dt", p=128),
                        allow_slow_non_contiguous=True, inc="win")
        v = "vector"
        P.wait(v, "win", win_done)
        for dt in range(8):
            P.op(v, "tensor_scalar", a_wo[:, dt * D:(dt + 1) * D], a_wo[:, dt * D:(dt + 1) * D],
                 a_nw[:, dt:dt + 1], None, ALU.mult, inc="dve" if dt == 7 else None)

    def emit_conversion():
        g = "gpsimd"
        w1flat = w1_d.rearrange("e d f -> (e d) f")
        w2flat = w2_d.rearrange("e f d -> (e f) d")
        for i in range(8):
            P.op(g, "dma_start", w1sc[i * 1024:(i + 1) * 1024, :],
                 w1flat[i * 1024:(i + 1) * 1024, :], inc="cvo0")
        for i in range(8):
            P.op(g, "dma_start", w2sc[i * 4096:(i + 1) * 4096, :],
                 w2flat[i * 4096:(i + 1) * 4096, :], inc="cvo0")
        return 16

    def emit_attention(b):
        v, t, a, g = "vector", "tensor", "scalar", "gpsimd"
        win_need = P.cnt["win"]
        xin_need = P.cnt["xin"]
        P.wait(v, "win", win_need)
        P.wait(v, "xin", xin_need)
        P.op(v, "memset", a_hk, 0.0)
        P.op(v, "memset", a_hv, 0.0)
        P.wait(t, "win", win_need)
        P.wait(t, "xin", xin_need)
        p_x_b = p_x[:].bitcast(BF16)

        px_state = {}
        XS_sg_gch = [0, 0]
        XS_bca_act = [0, 0]
        XS_bl_act = [0, 0]

        def evac_done(eng_sem):
            px_state["free"] = (eng_sem, P.cnt[eng_sem])

        def px_wait():
            if "free" in px_state:
                P.wait(t, px_state["free"][0], px_state["free"][1])

        def make_x_groups(s):
            S = XS[s % 2]
            xs = {}

            def g1():
                xt = x_buf[:, s * D:(s + 1) * D]
                P.op(v, "tensor_reduce", a_xsm[:, 0:1], xt, axis=AX.X, op=ALU.add)
                P.op(v, "tensor_scalar", a_xsm[:, 1:2], a_xsm[:, 0:1], -1.0 / D, None,
                     ALU.mult)
                P.op(v, "tensor_scalar", a_xc[:], xt, a_xsm[:, 1:2], None, ALU.add)
                P.op(v, "tensor_tensor", a_xsq, a_xc[:], a_xc[:], ALU.mult)
                d1 = P.op(v, "tensor_reduce", a_xsm[:, 2:3], a_xsq, axis=AX.X,
                          op=ALU.add, inc="dve")
                P.wait(a, "dve", d1)
                P.op(a, "activation", a_xsm[:, 3:4], a_xsm[:, 2:3], AF.Ln,
                     bias=c_eps[:], scale=1.0 / D)
                r1 = P.op(a, "activation", a_xsm[:, 4:5], a_xsm[:, 3:4], AF.Exp,
                          scale=-0.5, inc="act")
                P.wait(v, "act", r1)
                xs["apply"] = P.op(v, "tensor_scalar", a_xns, a_xc[:], a_xsm[:, 4:5],
                                   None, ALU.mult, inc="dve")

            def g2():
                P.wait(t, "dve", xs["apply"])
                px_wait()
                for dt in range(8):
                    p1 = P.op(t, "transpose", p_x_b[:, dt * 128:(dt + 1) * 128],
                              a_xns[:, dt * 128:(dt + 1) * 128], id128[:],
                              inc="pe" if dt == 7 else None)
                P.wait(a, "pe", p1)
                P.op(a, "activation", S["xnt"], p_x_b[:, 0:1024], AF.Copy, inc="act")
                evac_done("act")

            def proj(dname, wsb, half):
                def f():
                    dst = a_qraw if dname == "qt_raw" else S[dname]
                    px_wait()
                    for dt in range(8):
                        p1 = P.op(t, "matmul", p_x[:],
                                  S["xnt"][:, dt * SEG:(dt + 1) * SEG],
                                  wsb[:, dt * D + half * 512:
                                      dt * D + half * 512 + 512],
                                  start=(dt == 0), stop=(dt == 7),
                                  inc="pe" if dt == 7 else None)
                    P.wait(a, "pe", p1)
                    P.op(a, "activation", dst[:, half * 512:half * 512 + 512],
                         p_x[:], AF.Copy, inc="act")
                    evac_done("act")
                return f

            def g6():
                px_wait()
                for dt in range(8):
                    p1 = P.op(t, "matmul", p_x[:, 0:SEG],
                              a_wf1[:, dt * DK:(dt + 1) * DK],
                              S["xnt"][:, dt * SEG:(dt + 1) * SEG],
                              start=(dt == 0), stop=(dt == 7),
                              inc="pe" if dt == 7 else None)
                P.wait(a, "pe", p1)
                e2 = P.op(a, "activation", a_ut[:], p_x[:, 0:SEG], AF.Copy,
                          inc="act")
                evac_done("act")
                px_wait()
                P.wait(t, "act", e2)
                p1 = P.op(t, "matmul", p_x[:], a_ut[:], a_wf2[:], start=True,
                          stop=True, inc="pe")
                P.wait(a, "pe", p1)
                xs["ftanh"] = P.op(a, "activation", S["sig"], p_x[:], AF.Tanh,
                                   scale=0.5, inc="act")
                evac_done("act")

            def g6b():
                px_wait()
                for dt in range(8):
                    p1 = P.op(t, "matmul", p_x[:, 0:SEG],
                              a_wg1[:, dt * DK:(dt + 1) * DK],
                              S["xnt"][:, dt * SEG:(dt + 1) * SEG],
                              start=(dt == 0), stop=(dt == 7),
                              inc="pe" if dt == 7 else None)
                P.wait(a, "pe", p1)
                xs["ugt"] = P.op(a, "activation", a_ugt[:], p_x[:, 0:SEG], AF.Copy,
                                 inc="act")
                evac_done("act")

            def g6c(half):
                def f():
                    px_wait()
                    P.wait(t, "act", xs["ugt"])
                    p1 = P.op(t, "matmul", p_x[:], a_ugt[:],
                              a_wg2[:, half * 512:half * 512 + 512], start=True,
                              stop=True, inc="pe")
                    P.wait(a, "pe", p1)
                    c = P.op(a, "activation",
                             S["sg"][:, half * 512:half * 512 + 512],
                             p_x[:], AF.Tanh, scale=0.5, inc="act")
                    if half == 1:
                        xs["gtanh"] = c
                    evac_done("act")
                return f

            def g7qk():
                gp = "gpsimd"
                s1 = P.op(a, "activation", a_xact, a_qraw, AF.Tanh, scale=0.5,
                          inc="act")
                s2 = P.op(a, "activation", a_xns, S["ktm"], AF.Tanh, scale=0.5,
                          inc="act")
                P.wait(gp, "act", s1)
                P.op(gp, "tensor_scalar", a_xact, a_xact, 0.5, 0.5, ALU.mult,
                     op1=ALU.add)
                xs["qsilu"] = P.op(gp, "tensor_tensor", a_qraw, a_qraw, a_xact,
                                   ALU.mult)
                P.wait(gp, "act", s2)
                P.op(gp, "tensor_scalar", a_xns, a_xns, 0.5, 0.5, ALU.mult,
                     op1=ALU.add)
                xs["ksilu"] = P.op(gp, "tensor_tensor", S["ktm"], S["ktm"], a_xns,
                                   ALU.mult)

            def g7fg():
                gp = "gpsimd"
                P.wait(gp, "act", xs["ftanh"])
                xs["sig_gch"] = P.op(gp, "tensor_scalar", S["sig"], S["sig"],
                                     0.5, 0.5, ALU.mult, op1=ALU.add)
                P.wait(gp, "act", xs["gtanh"])
                XS_sg_gch[s % 2] = P.op(gp, "tensor_scalar", S["sg"], S["sg"],
                                        0.5, 0.5, ALU.mult, op1=ALU.add)

            def g8a():
                P.wait(t, "gch", xs["qsilu"])
                px_wait()
                for dt in range(8):
                    p1 = P.op(t, "transpose", p_x_b[:, dt * 128:(dt + 1) * 128],
                              a_qraw[:, dt * 128:(dt + 1) * 128], id128[:],
                              inc="pe" if dt == 7 else None)
                P.wait(a, "pe", p1)
                P.op(a, "activation", S["qt"], p_x_b[:, 0:1024], AF.Copy, inc="act")
                evac_done("act")

            def g8b():
                P.wait(t, "gch", xs["ksilu"])
                px_wait()
                for dt in range(8):
                    p1 = P.op(t, "transpose", p_x_b[:, dt * 128:(dt + 1) * 128],
                              S["ktm"][:, dt * 128:(dt + 1) * 128], id128[:],
                              inc="pe" if dt == 7 else None)
                P.wait(a, "pe", p1)
                P.op(a, "activation", S["kt"], p_x_b[:, 0:1024], AF.Copy, inc="act")
                evac_done("act")

            def g9():
                P.wait(a, "dve", xs["apply"])
                P.wait(a, "gch", xs["sig_gch"])
                l1 = P.op(a, "activation", a_xc[:, 0:512], S["sig"], AF.Ln, inc="act")
                P.wait(v, "act", l1)
                xs["dlt_dve"] = P.op(v, "tensor_scalar", S["dlt"], a_xc[:, 0:512],
                                     -C0, None, ALU.add, inc="dve")
                P.wait(v, "gch", xs["sig_gch"])
                d1 = P.op(v, "tensor_scalar", S["sig"], S["sig"], -1.0, 1.0, ALU.mult,
                          op1=ALU.add, inc="dve")
                P.wait(t, "dve", d1)
                px_wait()
                for pi in range(2):
                    rs2 = slice(pi * 64, pi * 64 + 64)
                    p1 = P.op(t, "matmul", p_x[rs2, :], triI[rs2, :], S["dlt"][rs2, :],
                              start=True, stop=True, tile_position=(pi * 64, pi * 64),
                              inc="pe" if pi == 1 else None)
                P.wait(a, "pe", p1)
                P.op(a, "activation", S["B"], p_x[:], AF.Exp, bias=c_lin[:])
                e1 = P.op(a, "activation", a_xact[:, 0:512], p_x[:], AF.Exp, bias=c_nlin[:],
                          scale=-1.0, inc="act")
                P.wait(v, "act", e1)
                d2 = P.op(v, "tensor_tensor", S["stl"], S["sig"], a_xact[:, 0:512], ALU.mult,
                          inc="dve")
                xs["g9_d2"] = d2
                xs["g9_e1"] = e1

            def g9b():
                P.wait(t, "dve", xs["g9_d2"])
                P.wait(t, "act", xs["g9_e1"])
                px_wait()
                for pi in range(2):
                    rs2 = slice(pi * 64, pi * 64 + 64)
                    p1 = P.op(t, "matmul", p_x[rs2, :], triL[rs2, :], S["dlt"][rs2, :],
                              start=True, stop=True, tile_position=(pi * 64, pi * 64),
                              inc="pe" if pi == 1 else None)
                P.wait(a, "pe", p1)
                P.wait(a, "dve", xs["g9_d2"])
                e2 = P.op(a, "activation", a_xact[:, 0:512], p_x[:], AF.Exp, bias=c_lrev[:],
                          inc="act")
                P.wait(v, "act", e2)
                P.op(v, "tensor_tensor", S["stb"], S["sig"], a_xact[:, 0:512], ALU.mult,
                     inc="dve")
                evac_done("act")

            def g10a():
                P.wait(t, "dve", xs["dlt_dve"])
                for c4 in range(2):
                    rs2 = slice(c4 * 64, c4 * 64 + 64)
                    px_wait()
                    p1 = P.op(t, "matmul", p_x[:], ones128[rs2, :],
                              S["dlt"][rs2, :], start=True, stop=True,
                              tile_position=(c4 * 64, 0), inc="pe")
                    P.wait(a, "pe", p1)
                    XS_bca_act[s % 2] = P.op(a, "activation",
                                             S["bca"][:, c4 * HM:(c4 + 1) * HM],
                                             p_x[:], AF.Exp, bias=c_c64[:],
                                             inc="act")
                    evac_done("act")

            def g10b():
                for c4 in range(2):
                    rs2 = slice(c4 * 64, c4 * 64 + 64)
                    px_wait()
                    for h in range(H):
                        for base in (0, 64):
                            p1 = P.op(t, "matmul", p_x[base:base + 64, h:h + 1],
                                      S["dlt"][rs2, h * 64:(h + 1) * 64],
                                      onescol[rs2, :], start=True, stop=True,
                                      tile_position=(c4 * 64, base),
                                      inc="pe" if (h == 7 and base == 64) else None)
                    P.wait(a, "pe", p1)
                    XS_bl_act[s % 2] = P.op(a, "activation",
                                            S["bl"][:, c4 * 8:c4 * 8 + 8],
                                            p_x[:, 0:8], AF.Exp, bias=c_c64[:],
                                            inc="act")
                    evac_done("act")

            return [g1, g2, proj("qt_raw", a_wq, 0), proj("qt_raw", a_wq, 1),
                    proj("ktm", a_wk, 0), proj("ktm", a_wk, 1), g7qk,
                    g6, g6b, g6c(0), g6c(1), g7fg, g8a, g8b, g9, g9b, g10a,
                    proj("vtm", a_wv, 0), proj("vtm", a_wv, 1), g10b]

        ys = {"state_dve": 0, "post_dve": 0}

        def emit_y(s, inject):
            S = XS[s % 2]
            for c4 in range(2):
                pi = c4
                rs = slice(pi * 64, pi * 64 + 64)
                P.wait(t, "dve", ys["state_dve"])
                P.wait(t, "dve", ys["post_dve"])
                for h in range(H):
                    tc = slice(h * SEG + c4 * 64, h * SEG + c4 * 64 + 64)
                    pat = P.op(t, "matmul", p_a[rs, h * 64:h * 64 + 64],
                               S["kt"][:, tc], S["qt"][:, tc], start=True, stop=True,
                               tile_position=(0, pi * 64),
                               inc="pe" if h == 7 else None)
                P.wait(v, "pe", pat)
                atm_dve = P.op(v, "tensor_tensor", a_atm[rs, :], p_a[rs, :],
                               triI8[rs, :], ALU.mult, inc="dve")
                inject()
                P.wait(t, "dve", atm_dve)
                for h in range(H):
                    tc = slice(h * SEG + c4 * 64, h * SEG + c4 * 64 + 64)
                    oc = slice(h * 64, h * 64 + 64)
                    P.op(t, "matmul", p_ok[rs, oc], a_atm[rs, oc],
                         S["stl"][rs, h * 64:(h + 1) * 64],
                         start=True, stop=False, tile_position=(pi * 64, pi * 64))
                    pok = P.op(t, "matmul", p_ok[rs, oc], S["qt"][:, tc],
                               a_hk[:, oc], start=False, stop=True,
                               tile_position=(0, pi * 64),
                               inc="pe" if h == 7 else None)
                inject()
                for h in range(H):
                    P.op(t, "matmul", p_st[:, h * 64:h * 64 + 64],
                         S["ktm"][rs, h * 128:(h + 1) * 128],
                         S["stb"][rs, h * 64:(h + 1) * 64],
                         start=True, stop=True, tile_position=(pi * 64, 0),
                         inc="pe" if h == 7 else None)
                inject()
                P.wait(v, "pe", pok)
                okb = a_otmp[rs, 0:512]
                qvt = a_otmp[rs, 512:1024]
                okb3 = okb.rearrange("p (h m) -> p h m", m=64)
                qvt3 = qvt.rearrange("p (h m) -> p h m", m=64)
                okb_dve = P.op(v, "tensor_tensor", okb, p_ok[rs, :], S["B"][rs, :],
                               ALU.mult, inc="dve")
                P.op(v, "tensor_reduce", a_nmx[rs, :], okb3, axis=AX.X, op=ALU.max,
                     negate=True)
                sm_dve = P.op(v, "tensor_tensor", qvt3, okb3,
                              a_nmx[rs, :].unsqueeze(2).broadcast_to([64, 8, 64]),
                              ALU.add, inc="dve")
                P.wait(a, "dve", sm_dve)
                sm_act = P.op(a, "activation", okb, qvt, AF.Exp, inc="act")
                P.wait(v, "act", sm_act)
                P.op(v, "tensor_reduce", a_ssum[rs, :], okb3, axis=AX.X, op=ALU.add)
                P.op(v, "reciprocal", a_rsum[rs, :], a_ssum[rs, :])
                P.op(v, "tensor_tensor", qvt3, okb3,
                     a_rsum[rs, :].unsqueeze(2).broadcast_to([64, 8, 64]), ALU.mult)
                qb_dve = P.op(v, "tensor_tensor", a_qb[rs, :], qvt, S["B"][rs, :],
                              ALU.mult, inc="dve")
                P.wait(t, "dve", okb_dve)
                for h in range(H):
                    sbc = S["stb"][rs, h * 64:(h + 1) * 64]
                    vv = S["vtm"][rs, h * 128:(h + 1) * 128]
                    bank2 = p_st2 if h < 4 else p_ok
                    for base in (0, 64):
                        pov_st = P.op(t, "matmul",
                                      bank2[base:base + 64,
                                            (h % 4) * 128:(h % 4 + 1) * 128],
                                      sbc, vv, start=True, stop=True,
                                      tile_position=(pi * 64, base),
                                      inc="pe" if (h == 7 and base == 64) else None)
                inject()
                P.wait(t, "dve", qb_dve)
                for h in range(H):
                    oc = h * 64
                    P.op(t, "transpose", p_tp[rs, oc:oc + 64],
                         a_qb[rs, h * 64:h * 64 + 64], id64[rs, :],
                         tile_position=(pi * 64, pi * 64))
                    ptr = P.op(t, "transpose", p_tp[rs, 512 + oc:512 + oc + 64],
                               S["stl"][rs, h * 64:h * 64 + 64], id64[rs, :],
                               tile_position=(pi * 64, pi * 64),
                               inc="pe" if h == 7 else None)
                P.wait(v, "pe", ptr)
                P.op(v, "tensor_copy", a_qbt[rs, :], p_tp[rs, 0:512])
                tt_dve = P.op(v, "tensor_copy", a_stT[rs, :], p_tp[rs, 512:1024],
                              inc="dve")
                P.wait(t, "dve", tt_dve)
                for h in range(H):
                    oc = h * 64
                    pat2 = P.op(t, "matmul", p_a[rs, h * 64:h * 64 + 64],
                                a_stT[rs, oc:oc + 64], a_qbt[rs, oc:oc + 64],
                                start=True, stop=True,
                                tile_position=(pi * 64, pi * 64),
                                inc="pe" if h == 7 else None)
                P.wait(v, "pe", pat2)
                atm2_dve = P.op(v, "tensor_tensor", a_atm[rs, :], p_a[rs, :],
                                triI8[rs, :], ALU.mult, inc="dve")
                inject()
                P.wait(t, "dve", atm2_dve)
                for h in range(H):
                    oc = h * 64
                    vv = S["vtm"][rs, h * 128:(h + 1) * 128]
                    P.op(t, "matmul", p_ov[rs, h * 128:(h + 1) * 128],
                         a_atm[rs, h * 64:h * 64 + 64], vv,
                         start=True, stop=False, tile_position=(pi * 64, pi * 64))
                    ov_pe = P.op(t, "matmul", p_ov[rs, h * 128:(h + 1) * 128],
                                 a_qbt[rs, oc:oc + 64],
                                 a_hv[rs, h * 128:(h + 1) * 128],
                                 start=False, stop=True,
                                 tile_position=(pi * 64, pi * 64),
                                 inc="pe" if h == 7 else None)
                inject()
                P.wait(v, "pe", ov_pe)
                P.wait(v, "act", XS_bca_act[s % 2])
                P.op(v, "tensor_tensor", a_hkt, a_hk,
                     S["bca"][:, c4 * HM:(c4 + 1) * HM], ALU.mult)
                P.op(v, "tensor_tensor", a_hk, a_hkt, p_st[:], ALU.add)
                P.wait(v, "act", XS_bl_act[s % 2])
                for h in range(H):
                    bank2 = p_st2 if h < 4 else p_ok
                    sd = P.op(v, "scalar_tensor_tensor",
                              a_hv[:, h * 128:(h + 1) * 128],
                              a_hv[:, h * 128:(h + 1) * 128],
                              S["bl"][:, c4 * 8 + h:c4 * 8 + h + 1],
                              bank2[:, (h % 4) * 128:(h % 4 + 1) * 128],
                              ALU.mult, ALU.add, inc="dve" if h == 7 else None)
                ys["state_dve"] = sd

            P.wait(v, "gch", XS_sg_gch[s % 2])
            P.op(v, "tensor_tensor", a_otmp, p_ov[:], S["sg"], ALU.mult)
            P.op(v, "tensor_tensor", a_o2, a_otmp, a_otmp, ALU.mult)
            rm_dve = P.op(v, "tensor_reduce", a_sm1[:, 8:16],
                          a_o2.rearrange("p (h d) -> p h d", d=128), axis=AX.X,
                          op=ALU.add, inc="dve")
            P.wait(a, "dve", rm_dve)
            P.op(a, "activation", a_sm1[:, 0:8], a_sm1[:, 8:16], AF.Ln,
                 bias=c_eps[:], scale=1.0 / 128)
            rm_act = P.op(a, "activation", a_sm1[:, 8:16], a_sm1[:, 0:8], AF.Exp,
                          scale=-0.5, inc="act")
            P.wait(v, "act", rm_act)
            for h in range(H):
                o2_dve = P.op(v, "tensor_scalar", a_o2[:, h * 128:(h + 1) * 128],
                              a_otmp[:, h * 128:(h + 1) * 128],
                              a_sm1[:, 8 + h:9 + h], None, ALU.mult,
                              inc="dve" if h == 7 else None)
            inject()
            P.wait(t, "dve", o2_dve)
            for dt in range(8):
                po2 = P.op(t, "transpose", p_tp[:, dt * 128:(dt + 1) * 128],
                           a_o2[:, dt * 128:(dt + 1) * 128], id128[:],
                           inc="pe" if dt == 7 else None)
            P.wait(v, "pe", po2)
            o2t_dve = P.op(v, "tensor_copy", a_o2t, p_tp[:, 0:1024], inc="dve")
            P.wait(t, "dve", o2t_dve)
            for dt in range(8):
                for dh in range(2):
                    pwo = P.op(t, "matmul", p_ov[:, dh * 512:dh * 512 + 512],
                               a_o2t[:, dt * 128:(dt + 1) * 128],
                               a_wo[:, dt * D + dh * 512: dt * D + dh * 512 + 512],
                               start=(dt == 0), stop=(dt == 7),
                               inc="pe" if (dt == 7 and dh == 1) else None)
            inject()
            P.wait(v, "pe", pwo)
            ys["post_dve"] = P.op(v, "tensor_tensor", x_buf[:, s * D:(s + 1) * D],
                                  x_buf[:, s * D:(s + 1) * D], p_ov[:], ALU.add,
                                  inc="dve")

        for gr in make_x_groups(0):
            gr()
        pend = []

        def inject():
            for _ in range(2):
                if pend:
                    pend.pop(0)()
        for s in range(NSEG):
            if s + 1 < NSEG:
                pend = make_x_groups(s + 1)
                pend.pop(0)()
            else:
                pend = []
            emit_y(s, inject)
            while pend:
                pend.pop(0)()

    def emit_ffn(b):
        v, t, a, g, sy = "vector", "tensor", "scalar", "gpsimd", "sync"
        P.bar(g, "pe", "dve", "act")
        P.bar(v, "pe", "act")
        P.bar(a, "pe", "dve")
        P.bar(t, "dve", "act")
        P.op(g, "dma_start", f_b1n[:], b1_d, inc="gs")
        P.op(g, "dma_start", f_b2a[:], b2_d, inc="gs")
        gsw = P.op(g, "dma_start", f_wr[:].rearrange("p (dt e) -> p dt e", e=E),
                   wr_d.rearrange("(dt p) e -> p dt e", p=128), inc="gs")
        P.wait(t, "gs", gsw)
        P.bar(t, "dve", "act")
        for fb in range(32):
            pb1 = P.op(t, "transpose", p_tp[0:128, fb * 8:fb * 8 + 8],
                       f_b1n[0:8, fb * 128:(fb + 1) * 128], id128[0:8, 0:8],
                       inc="pe" if fb == 31 else None)
        P.wait(v, "pe", pb1)
        b1t_dve = P.op(v, "tensor_copy", f_b1t[:], p_tp[:, 0:256], inc="dve")

        xnts = (f_xnt0, f_xnt1)
        wgs = (f_wg0, f_wg1)
        w1bufs = (f_w1a, f_w1b)
        w2bufs = (f_w2a, f_w2b, f_w2c, f_w2d)
        mm1banks = ((p_ov[:, 0:512], p_ov[:, 512:1024]), (p_st[:], p_st2[:]))
        accb = (p_a[:, 0:512], p_ok[:, 0:512], p_st[:, 0:512], p_st2[:, 0:512],
                p_ov[:, 0:512], p_ov[:, 512:1024], p_x[:, 0:512], p_tp_f[:, 0:512])
        bank_free = {i: ("dve", b1t_dve if i == 7 else 0) for i in range(8)}

        def ln2_router(grp):
            for f in ln2_parts(grp):
                f()

        def ln2_parts(grp):
            gx = xnts[grp % 2]
            gw = wgs[grp % 2]
            g0 = grp * (GRP // 128)
            l4 = [0]
            l5 = [0]
            rstd_act = [0]

            def stats_all():
                for tl in range(8):
                    xt = x_buf[:, (g0 + tl) * D:(g0 + tl + 1) * D]
                    P.wait(v, "pe", l4[0])
                    P.op(v, "tensor_reduce", f_sm[:, 0:1], xt, axis=AX.X, op=ALU.add)
                    P.op(v, "tensor_scalar", f_rt[:, tl:tl + 1], f_sm[:, 0:1],
                         -1.0 / D, None, ALU.mult)
                    P.op(v, "tensor_tensor", f_xns, xt, xt, ALU.mult)
                    P.op(v, "tensor_reduce", f_rt[:, 8 + tl:9 + tl], f_xns,
                         axis=AX.X, op=ALU.add)
                P.op(v, "tensor_scalar", f_rt[:, 8:16], f_rt[:, 8:16], 1.0 / D, None,
                     ALU.mult)
                P.op(v, "tensor_tensor", f_rt[:, 16:24], f_rt[:, 0:8], f_rt[:, 0:8],
                     ALU.mult)
                st = P.op(v, "tensor_sub", f_rt[:, 8:16], f_rt[:, 8:16],
                          f_rt[:, 16:24], inc="dve")
                P.wait(a, "dve", st)
                P.op(a, "activation", f_rt[:, 16:24], f_rt[:, 8:16], AF.Ln,
                     bias=c_eps[:])
                rstd_act[0] = P.op(a, "activation", f_rt[:, 16:24], f_rt[:, 16:24],
                                   AF.Exp, scale=-0.5, inc="act")

            def tile(tl):
                def f():
                    xt = x_buf[:, (g0 + tl) * D:(g0 + tl + 1) * D]
                    P.wait(v, "act", rstd_act[0])
                    P.wait(v, "pe", l4[0])
                    l3 = P.op(v, "tensor_scalar", f_xns, xt, f_rt[:, tl:tl + 1],
                              f_rt[:, 16 + tl:17 + tl], ALU.add, op1=ALU.mult,
                              inc="dve")
                    P.wait(t, "dve", l3)
                    P.wait(t, bank_free[7][0], bank_free[7][1])
                    for dt in range(8):
                        l4[0] = P.op(t, "transpose", p_tp[:, dt * 128:(dt + 1) * 128],
                                     f_xns[:, dt * 128:(dt + 1) * 128], id128[:],
                                     inc="pe" if dt == 7 else None)
                    P.wait(a, "pe", l4[0])
                    l5[0] = P.op(a, "activation",
                                 gx[:, 0:8192].rearrange("p (dt tk) -> p dt tk",
                                                         tk=GRP)
                                 [:, :, tl * 128:tl * 128 + 128],
                                 p_tp[:, 0:1024].rearrange("p (dt tk) -> p dt tk",
                                                           tk=128),
                                 AF.Copy, inc="act")
                    bank_free[7] = ("act", l5[0])
                return f

            def router():
                P.wait(t, "act", l5[0])
                P.wait(t, bank_free[6][0], bank_free[6][1])
                for tl in range(8):
                    for dt in range(8):
                        r1 = P.op(t, "matmul", p_x[:, tl * 8:tl * 8 + 8],
                                  gx[:, dt * GRP + tl * 128:
                                     dt * GRP + tl * 128 + 128],
                                  f_wr[:, dt * 8:(dt + 1) * 8],
                                  start=(dt == 0), stop=(dt == 7),
                                  inc="pe" if (tl == 7 and dt == 7) else None)
                P.wait(v, "pe", r1)
                L3 = p_x[:, 0:64].rearrange("p (tt e) -> p tt e", e=8)
                P.op(v, "tensor_reduce", f_sm[:, 8:16], L3, axis=AX.X, op=ALU.max,
                     negate=True)
                r2 = P.op(v, "tensor_tensor",
                          f_rt[:, 0:64].rearrange("p (tt e) -> p tt e", e=8), L3,
                          f_sm[:, 8:16].unsqueeze(2).broadcast_to([128, 8, 8]),
                          ALU.add, inc="dve")
                bank_free[6] = ("dve", r2)
                P.wait(a, "dve", r2)
                r3 = P.op(a, "activation", f_rt[:, 64:128], f_rt[:, 0:64], AF.Exp,
                          inc="act")
                P.wait(v, "act", r3)
                P.op(v, "tensor_reduce", f_sm[:, 8:16],
                     f_rt[:, 64:128].rearrange("p (tt e) -> p tt e", e=8), axis=AX.X,
                     op=ALU.add)
                P.op(v, "reciprocal", f_sm[:, 0:8], f_sm[:, 8:16])
                P.op(v, "tensor_scalar", f_rt[:, 64:128], f_rt[:, 0:64], 0.0, None,
                     ALU.is_equal)
                P.op(v, "tensor_tensor", gw[:].rearrange("p (tt e) -> p tt e", e=8),
                     f_rt[:, 64:128].rearrange("p (tt e) -> p tt e", e=8),
                     f_sm[:, 0:8].unsqueeze(2).broadcast_to([128, 8, 8]), ALU.mult,
                     inc="dve")

            return [stats_all] + [tile(i) for i in range(8)] + [router]

        pe_after_w1use = {}
        pe_after_w2use = {}
        gelu_after = {}

        ln2_router(0)
        ln2_pend = []
        for grp in range(NGRP):
            if grp + 1 < NGRP:
                ln2_pend = ln2_parts(grp + 1)
            g0 = grp * (GRP // 128)
            gx = xnts[grp % 2]
            gw = wgs[grp % 2]
            for e in range(E):
                eg = (b * NGRP + grp) * E + e
                w1row = w1sc[e * D:(e + 1) * D, :]
                glast = None
                for fbg in range(8):
                    gidx = eg * 8 + fbg
                    par = gidx % 2
                    psem = "w1%d" % par
                    if gidx >= 2:
                        P.wait(g, "pe", pe_after_w1use.get(gidx - 2))
                    ld = P.op(g, "dma_start",
                              w1bufs[par][:].rearrange("p (dt f) -> p dt f", f=512),
                              w1row[:, fbg * 512:(fbg + 1) * 512].rearrange(
                                  "(dt p) f -> p dt f", p=128), inc=psem)
                    P.wait(t, psem, ld)
                    if fbg == 0:
                        for i in (2, 3, 4, 5):
                            P.wait(t, bank_free[i][0], bank_free[i][1])
                        P.wait(t, "dve", b1t_dve)
                    for fi in range(4):
                        fb = fbg * 4 + fi
                        fgi = eg * 32 + fb
                        bsel = mm1banks[fi % 2]
                        if gelu_after.get(fgi - 2) is not None:
                            P.wait(t, "act", gelu_after[fgi - 2])
                        for dt in range(8):
                            for half in range(2):
                                m1 = P.op(t, "matmul", bsel[half],
                                          w1bufs[par][:, dt * 512 + fi * 128:
                                                       dt * 512 + fi * 128 + 128],
                                          gx[:, dt * GRP + half * 512:
                                             dt * GRP + half * 512 + 512],
                                          start=(dt == 0), stop=(dt == 7),
                                          inc="pe" if (dt == 7 and half == 1)
                                          else None)
                        P.wait(a, "pe", m1)
                        for half in range(2):
                            glast = P.op(a, "activation",
                                         f_ht[:, fb * GRP + half * 512:
                                              fb * GRP + half * 512 + 512],
                                         bsel[half], AF.Gelu,
                                         bias=f_b1t[:, fb * 8 + e:fb * 8 + e + 1],
                                         inc="act" if half == 1 else None)
                        gelu_after[fgi] = P.cnt["act"]
                    pe_after_w1use[gidx] = P.cnt["pe"]
                    if ln2_pend:
                        ln2_pend.pop(0)()
                P.wait(g, "gs", gsw)
                P.wait(g, "pe", pe_after_b2.get(eg - 1))
                b2done = P.op(g, "dma_start", f_b2s[:], f_b2a[e:e + 1, :], inc="gs")
                w2row = w2sc[e * F:(e + 1) * F, :]
                P.wait(t, "gs", b2done)
                for h5 in range(2):
                    gl = gelu_after[eg * 32 + 31]
                    for fb in range(32):
                        gidx2 = (eg * 2 + h5) * 32 + fb
                        par = fb % 4
                        psem = "w2%d" % par
                        if gidx2 >= 4:
                            P.wait(g, "pe", pe_after_w2use.get(gidx2 - 4))
                        ld = P.op(g, "dma_start", w2bufs[par][:],
                                  w2row[fb * 128:(fb + 1) * 128, :], inc=psem)
                        P.wait(t, psem, ld)
                        if fb == 0:
                            P.wait(t, "act", gl)
                        for j, i in enumerate((4, 5, 2, 3, 0, 1, 6, 7)):
                            tl, dh = i // 2, i % 2
                            if fb == 0:
                                P.wait(t, bank_free[i][0], bank_free[i][1])
                            m2 = P.op(t, "matmul", accb[i],
                                      f_ht[:, fb * GRP + h5 * 512 + tl * 128:
                                           fb * GRP + h5 * 512 + tl * 128 + 128],
                                      w2bufs[par][:, dh * 512:dh * 512 + 512],
                                      start=(fb == 0), stop=False,
                                      inc="pe" if j == 7 else None)
                        pe_after_w2use[gidx2] = P.cnt["pe"]
                    for tl in range(4):
                        for dh in range(2):
                            m3 = P.op(t, "matmul", accb[tl * 2 + dh], onesrow[:],
                                      f_b2s[0:1, dh * 512:dh * 512 + 512],
                                      start=False, stop=True, tile_position=(0, 0),
                                      inc="pe" if (tl == 3 and dh == 1) else None)
                    pe_after_b2[eg] = P.cnt["pe"]
                    P.wait(v, "pe", m3)
                    for j, i in enumerate((4, 5, 2, 3, 0, 1, 6, 7)):
                        tl, dh = i // 2, i % 2
                        TT = g0 + h5 * 4 + tl
                        xs2 = x_buf[:, TT * D + dh * 512: TT * D + dh * 512 + 512]
                        ev = P.op(v, "scalar_tensor_tensor", xs2, accb[i],
                                  gw[:, (h5 * 4 + tl) * 8 + e:
                                     (h5 * 4 + tl) * 8 + e + 1],
                                  xs2, ALU.mult, ALU.add, inc="dve")
                        bank_free[i] = ("dve", ev)
            P.bar(sy, "dve")
            ydst = y_d[b * N + grp * GRP: b * N + (grp + 1) * GRP, :].rearrange(
                "(tt p) d -> p tt d", p=128)
            P.op(sy, "dma_start", ydst,
                 x_buf[:, g0 * D:(g0 + 8) * D].rearrange("p (tt d) -> p tt d", d=D),
                 inc="yst")

    emit_consts()
    for b in range(BPC):
        emit_attn_weight_loads(b)
        P.bar("sync", "yst")
        P.op("sync", "dma_start", x_buf[:].rearrange("p (tt d) -> p tt d", d=D),
             x_d[b * N:(b + 1) * N, :].rearrange("(tt p) d -> p tt d", p=128), inc="xin")
        if b == 0 and DO_FFN:
            emit_conversion()
        if DO_ATTN:
            emit_attention(b)
        if DO_FFN:
            P.wait("gpsimd", "cvo0", P.cnt["cvo0"])
            emit_ffn(b)
        else:
            P.bar("sync", "dve")
            ydst = y_d[b * N:(b + 1) * N, :].rearrange("(tt p) d -> p tt d", p=128)
            P.op("sync", "dma_start", ydst, x_buf[:].rearrange("p (tt d) -> p tt d", d=D),
                 inc="yst")
    P.wait("sync", "yst", P.cnt["yst"])

    with es, nc.Block() as block:
        for eng in Prog.ENGINES:
            stream = P.streams[eng]

            def make(fns):
                def _(e):
                    for f in fns:
                        f(e)
                return _
            getattr(block, eng)(make(stream))

    return nc


_NC_CACHE = None


def kernel(**inputs):
    global _NC_CACHE
    xs = np.ascontiguousarray(np.asarray(inputs["L0_attn_x"], dtype=np.float32))
    w = {
        "wq": inputs["L0_attn_Wq"], "wk": inputs["L0_attn_Wk"], "wv": inputs["L0_attn_Wv"],
        "wf1": inputs["L0_attn_Wf1"], "wf2": inputs["L0_attn_Wf2"],
        "wg1": inputs["L0_attn_Wg1"], "wg2": inputs["L0_attn_Wg2"],
        "wo": inputs["L0_attn_Wo"], "normw": inputs["L0_attn_norm_w"],
        "wr": inputs["L0_ffn_Wr"], "w1": inputs["L0_ffn_W1"], "b1": inputs["L0_ffn_b1"],
        "w2": inputs["L0_ffn_W2"], "b2": inputs["L0_ffn_b2"],
    }
    w = {k: np.ascontiguousarray(np.asarray(v, dtype=np.float32)) for k, v in w.items()}
    if _NC_CACHE is None:
        _NC_CACHE = build_nc()
    nc = _NC_CACHE
    in_maps = []
    for c in range(NCORES):
        m = {"x": np.ascontiguousarray(xs[c * BPC:(c + 1) * BPC].reshape(BPC * N, D))}
        m.update(w)
        in_maps.append(m)
    res = run_bass_kernel_spmd(nc, in_maps, core_ids=list(range(NCORES)))
    out = np.concatenate([r["y"].reshape(BPC, N, D) for r in res.results], axis=0)
    return out


# version 21
# speedup vs baseline: 1.0150x; 1.0150x over previous
import contextlib
import numpy as np
import concourse.bass as bass
import concourse.mybir as mybir
from concourse.bass_utils import run_bass_kernel_spmd

F32 = mybir.dt.float32
BF16 = mybir.dt.bfloat16
AF = mybir.ActivationFunctionType
ALU = mybir.AluOpType
AX = mybir.AxisListType

C0 = -0.6931471805599453
NCORES = 8
BPC = 2
N = 2048
D = 1024
H = 8
M = 64
DK = 128
HM = H * M
E = 8
F = 4096
SEG = 128
NSEG = N // SEG
GRP = 1024
NGRP = N // GRP
EPS = 1e-5

DO_FFN = True
DO_ATTN = True
N_SEG_EMIT = 99
PHASE_LIMIT = 99
SCAN_OPS = 99


class Prog:
    ENGINES = ("tensor", "vector", "scalar", "sync", "gpsimd")

    def __init__(self, sems):
        self.streams = {e: [] for e in self.ENGINES}
        self.sems = sems
        self.cnt = {k: 0 for k in sems}

    CHAIN = {"vector": "dve", "scalar": "act", "gpsimd": "gch"}

    def op(self, eng, fname, *args, inc=None, **kw):
        is_dma = "dma" in fname
        chain = None if is_dma else self.CHAIN.get(eng)
        if chain is not None:
            self.wait(eng, chain, self.cnt[chain])
            assert inc is None or inc == chain
            inc = chain
        h = self.sems[inc] if inc else None
        amt = 16 if is_dma else 1
        if inc:
            self.cnt[inc] += amt

        def f(e, fname=fname, args=args, kw=kw, h=h, amt=amt):
            ins = getattr(e, fname)(*args, **kw)
            if h is not None:
                ins.then_inc(h, amt)
        self.streams[eng].append(f)
        return self.cnt[inc] if inc else None

    def wait(self, eng, sem, val):
        if val is None or val <= 0:
            return
        h = self.sems[sem]

        def f(e, h=h, val=val):
            e.wait_ge(h, val)
        self.streams[eng].append(f)

    def bar(self, eng, *sem_names):
        for s in sem_names:
            self.wait(eng, s, self.cnt[s])


def build_nc():
    nc = bass.Bass("TRN2", target_bir_lowering=False)

    x_d = nc.dram_tensor("x", [BPC * N, D], F32, kind="ExternalInput").ap()
    wq_d = nc.dram_tensor("wq", [D, D], F32, kind="ExternalInput").ap()
    wk_d = nc.dram_tensor("wk", [D, D], F32, kind="ExternalInput").ap()
    wv_d = nc.dram_tensor("wv", [D, D], F32, kind="ExternalInput").ap()
    wf1_d = nc.dram_tensor("wf1", [D, DK], F32, kind="ExternalInput").ap()
    wf2_d = nc.dram_tensor("wf2", [DK, HM], F32, kind="ExternalInput").ap()
    wg1_d = nc.dram_tensor("wg1", [D, DK], F32, kind="ExternalInput").ap()
    wg2_d = nc.dram_tensor("wg2", [DK, D], F32, kind="ExternalInput").ap()
    wo_d = nc.dram_tensor("wo", [D, D], F32, kind="ExternalInput").ap()
    nw_d = nc.dram_tensor("normw", [D], F32, kind="ExternalInput").ap()
    wr_d = nc.dram_tensor("wr", [D, E], F32, kind="ExternalInput").ap()
    w1_d = nc.dram_tensor("w1", [E, D, F], F32, kind="ExternalInput").ap()
    b1_d = nc.dram_tensor("b1", [E, F], F32, kind="ExternalInput").ap()
    w2_d = nc.dram_tensor("w2", [E, F, D], F32, kind="ExternalInput").ap()
    b2_d = nc.dram_tensor("b2", [E, D], F32, kind="ExternalInput").ap()
    y_d = nc.dram_tensor("y", [BPC * N, D], F32, kind="ExternalOutput").ap()
    w1sc = nc.dram_tensor("w1sc", [E * D, F], BF16).ap()
    w2sc = nc.dram_tensor("w2sc", [E * F, D], BF16).ap()

    es = contextlib.ExitStack()
    sb = lambda name, shape, dt: es.enter_context(nc.sbuf_tensor(name, shape, dt))
    ps = lambda name, shape, dt: es.enter_context(nc.psum_tensor(name, shape, dt))

    x_buf = sb("x_buf", [128, (N // 128) * D], F32)
    ARENA_F32 = 36000
    arena = sb("arena", [128, ARENA_F32], F32)
    ar_f = arena[:]
    ar_b = ar_f.bitcast(BF16)

    class Alloc:
        def __init__(self):
            self.off = 0

        def f32(self, rows, cols):
            o = self.off
            self.off += ((cols * 4 + 127) // 128) * 128
            assert self.off <= ARENA_F32 * 4, f"arena overflow {self.off}"
            return ar_f[0:rows, o // 4: o // 4 + cols]

        def bf(self, rows, cols):
            o = self.off
            self.off += ((cols * 2 + 127) // 128) * 128
            assert self.off <= ARENA_F32 * 4, f"arena overflow {self.off}"
            return ar_b[0:rows, o // 2: o // 2 + cols]

    aa = Alloc()
    a_wq = aa.bf(128, 8 * D)
    a_wk = aa.bf(128, 8 * D)
    a_wv = aa.bf(128, 8 * D)
    a_wo = aa.bf(128, 8 * D)
    a_wf1 = aa.bf(128, 8 * DK)
    a_wf2 = aa.bf(128, HM)
    a_wg1 = aa.bf(128, 8 * DK)
    a_wg2 = aa.bf(128, D)
    a_nw = aa.f32(128, 8)
    XS = []
    for par in range(2):
        XS.append(dict(
            xnt=aa.bf(128, 8 * SEG),
            qt=aa.bf(128, H * SEG),
            kt=aa.bf(128, H * SEG),
            vtm=aa.bf(128, H * DK),
            ktm=aa.bf(128, H * DK),
            sig=aa.bf(128, HM),
            dlt=aa.bf(128, HM),
            stl=aa.bf(128, HM),
            stb=aa.bf(128, HM),
            B=aa.bf(128, HM),
            sg=aa.bf(128, D),
            bca=aa.bf(128, 2 * HM),
            bl=aa.f32(128, 16),
        ))
    a_qraw = aa.bf(128, D)
    a_xact = aa.bf(128, D)
    a_ut = aa.bf(128, DK)
    a_ugt = aa.bf(128, DK)
    a_xc = aa.f32(128, D)
    a_xsq = aa.bf(128, D)
    a_xns = aa.bf(128, D)
    a_xsm = aa.f32(128, 8)
    a_otmp = aa.f32(128, D)
    a_o2 = aa.bf(128, D)
    a_o2t = aa.bf(128, D)
    a_atm = aa.bf(128, HM)
    a_qb = aa.bf(128, HM)
    a_qbt = aa.bf(128, 8 * M)
    a_stT = aa.bf(128, 8 * M)
    a_hk = aa.bf(128, HM)
    a_hkt = aa.bf(128, HM)
    a_hv = aa.bf(128, 8 * DK)
    a_sm1 = aa.f32(128, 16)
    a_nmx = aa.f32(128, 8)
    a_ssum = aa.f32(128, 8)
    a_rsum = aa.f32(128, 8)
    attn_bytes = aa.off

    fa = Alloc()
    f_xnt0 = fa.bf(128, 8 * GRP)
    f_xnt1 = fa.bf(128, 8 * GRP)
    f_ht = fa.bf(128, 32 * GRP)
    f_w1a = fa.bf(128, 8 * 512)
    f_w1b = fa.bf(128, 8 * 512)
    f_w2a = fa.bf(128, D)
    f_w2b = fa.bf(128, D)
    f_w2c = fa.bf(128, D)
    f_w2d = fa.bf(128, D)
    f_b1n = fa.bf(8, F)
    f_b1t = fa.f32(128, 32 * E)
    f_b2a = fa.bf(8, D)
    f_b2s = fa.bf(1, D)
    f_wr = fa.bf(128, 8 * E)
    f_wg0 = fa.f32(128, 8 * E)
    f_wg1 = fa.f32(128, 8 * E)
    f_rt = fa.f32(128, 128)
    f_xns = fa.bf(128, D)
    f_lnc = fa.f32(128, D)
    f_sm = fa.f32(128, 16)
    ffn_bytes = fa.off
    assert max(attn_bytes, ffn_bytes) <= ARENA_F32 * 4, (attn_bytes, ffn_bytes)

    triI = sb("triI", [128, 64], BF16)
    triL = sb("triL", [128, 64], BF16)
    triI8 = sb("triI8", [128, 512], BF16)
    id64 = sb("id64", [128, 64], BF16)
    id128 = sb("id128", [128, 128], BF16)
    ones128 = sb("ones128", [128, 128], BF16)
    onescol = sb("onescol", [128, 1], BF16)
    onesrow = sb("onesrow", [1, 128], BF16)
    c_lin = sb("c_lin", [128, 1], F32)
    c_nlin = sb("c_nlin", [128, 1], F32)
    c_lrev = sb("c_lrev", [128, 1], F32)
    c_c64 = sb("c_c64", [128, 1], F32)
    c_eps = sb("c_eps", [128, 1], F32)
    c_it = sb("c_it", [128, 1], F32)
    c_it2 = sb("c_it2", [128, 1], F32)

    p_a = ps("p_a", [128, 512], F32)
    p_ok = ps("p_ok", [128, 512], F32)
    p_st = ps("p_st", [128, 512], F32)
    p_st2 = ps("p_st2", [128, 512], F32)
    p_ov = ps("p_ov", [128, 1024], F32)
    p_x = ps("p_x", [128, 512], F32)
    p_tp = ps("p_tp", [128, 1024], BF16)
    p_x_bf = p_x[:].bitcast(BF16)
    p_tp_f = p_tp[:].bitcast(F32)

    sems = {}
    for s in ("pe", "dve", "act", "gs", "gch", "xin", "win", "cvi0", "cvi1",
              "cvo0", "cvo1", "w10", "w11", "w20", "w21", "w22", "w23", "yst"):
        sems[s] = es.enter_context(nc.semaphore("s_" + s))
    P = Prog(sems)
    pe_after_b2 = {}

    def emit_consts():
        g = "gpsimd"
        P.op(g, "memset", triI[:], 1.0)
        P.op(g, "memset", triL[:], 1.0)
        P.op(g, "memset", triI8[:], 1.0)
        P.op(g, "memset", id64[:], 0.0)
        P.op(g, "memset", id128[:], 0.0)
        P.op(g, "memset", ones128[:], 1.0)
        P.op(g, "memset", onescol[:], 1.0)
        P.op(g, "memset", onesrow[:], 1.0)
        P.op(g, "memset", c_c64[:], 64.0 * C0)
        P.op(g, "memset", c_eps[:], EPS)
        for base in (0, 64):
            P.op(g, "affine_select", out=triI[base:base + 64, :], in_=triI[base:base + 64, :],
                 compare_op=ALU.is_ge, fill=0.0, base=0, pattern=[[1, 64]], channel_multiplier=-1)
            P.op(g, "affine_select", out=triL[base:base + 64, :], in_=triL[base:base + 64, :],
                 compare_op=ALU.is_gt, fill=0.0, base=0, pattern=[[-1, 64]], channel_multiplier=1)
            P.op(g, "affine_select", out=id64[base:base + 64, :], in_=id64[base:base + 64, :],
                 compare_op=ALU.not_equal, fill=1.0, base=0, pattern=[[-1, 64]], channel_multiplier=1)
            for hh in range(8):
                P.op(g, "affine_select", out=triI8[base:base + 64, hh * 64:hh * 64 + 64],
                     in_=triI8[base:base + 64, hh * 64:hh * 64 + 64],
                     compare_op=ALU.is_ge, fill=0.0, base=0, pattern=[[1, 64]], channel_multiplier=-1)
        P.op(g, "affine_select", out=id128[:], in_=id128[:], compare_op=ALU.not_equal,
             fill=1.0, base=0, pattern=[[-1, 128]], channel_multiplier=1)
        P.op(g, "iota", c_it[:], pattern=[[0, 1]], base=0, channel_multiplier=1,
             allow_small_or_imprecise_dtypes=True)
        v = "vector"
        P.bar(v, "gch")
        P.op(v, "tensor_scalar", c_it2[:], c_it[:], 64.0, None, ALU.is_ge)
        P.op(v, "scalar_tensor_tensor", c_it[:], c_it2[:], -64.0, c_it[:], ALU.mult, ALU.add)
        P.op(v, "tensor_scalar", c_lin[:], c_it[:], C0, C0, ALU.mult, op1=ALU.add)
        P.op(v, "tensor_scalar", c_nlin[:], c_lin[:], -1.0, None, ALU.mult)
        P.op(v, "tensor_scalar", c_lrev[:], c_it[:], -C0, 63.0 * C0, ALU.mult, op1=ALU.add,
             inc="dve")

    def emit_attn_weight_loads(b):
        g = "gpsimd"
        P.bar(g, "pe", "dve", "act", "yst")
        for dst, src in ((a_wq, wq_d), (a_wk, wk_d), (a_wv, wv_d), (a_wo, wo_d)):
            P.op(g, "dma_start", dst.rearrange("p (dt n) -> p dt n", n=D),
                 src.rearrange("(dt p) n -> p dt n", p=128), inc="win")
        P.op(g, "dma_start", a_wf1.rearrange("p (dt u) -> p dt u", u=DK),
             wf1_d.rearrange("(dt p) u -> p dt u", p=128), inc="win")
        P.op(g, "dma_start", a_wf2, wf2_d, inc="win")
        P.op(g, "dma_start", a_wg1.rearrange("p (dt u) -> p dt u", u=DK),
             wg1_d.rearrange("(dt p) u -> p dt u", p=128), inc="win")
        P.op(g, "dma_start", a_wg2, wg2_d, inc="win")
        win_done = P.op(g, "dma_start", a_nw, nw_d.rearrange("(dt p) -> p dt", p=128),
                        allow_slow_non_contiguous=True, inc="win")
        v = "vector"
        P.wait(v, "win", win_done)
        for dt in range(8):
            P.op(v, "tensor_scalar", a_wo[:, dt * D:(dt + 1) * D], a_wo[:, dt * D:(dt + 1) * D],
                 a_nw[:, dt:dt + 1], None, ALU.mult, inc="dve" if dt == 7 else None)

    def emit_conversion():
        g = "gpsimd"
        w1flat = w1_d.rearrange("e d f -> (e d) f")
        w2flat = w2_d.rearrange("e f d -> (e f) d")
        for i in range(8):
            P.op(g, "dma_start", w1sc[i * 1024:(i + 1) * 1024, :],
                 w1flat[i * 1024:(i + 1) * 1024, :], inc="cvo0")
        for i in range(8):
            P.op(g, "dma_start", w2sc[i * 4096:(i + 1) * 4096, :],
                 w2flat[i * 4096:(i + 1) * 4096, :], inc="cvo0")
        return 16

    def emit_attention(b):
        v, t, a, g = "vector", "tensor", "scalar", "gpsimd"
        win_need = P.cnt["win"]
        xin_need = P.cnt["xin"]
        P.wait(v, "win", win_need)
        P.wait(v, "xin", xin_need)
        P.op(v, "memset", a_hk, 0.0)
        P.op(v, "memset", a_hv, 0.0)
        P.wait(t, "win", win_need)
        P.wait(t, "xin", xin_need)
        p_x_b = p_x[:].bitcast(BF16)

        px_state = {}
        XS_sg_gch = [0, 0]
        XS_bca_act = [0, 0]
        XS_bl_act = [0, 0]

        def evac_done(eng_sem):
            px_state["free"] = (eng_sem, P.cnt[eng_sem])

        def px_wait():
            if "free" in px_state:
                P.wait(t, px_state["free"][0], px_state["free"][1])

        def make_x_groups(s):
            S = XS[s % 2]
            xs = {}

            def g1():
                xt = x_buf[:, s * D:(s + 1) * D]
                P.op(v, "tensor_reduce", a_xsm[:, 0:1], xt, axis=AX.X, op=ALU.add)
                P.op(v, "tensor_scalar", a_xsm[:, 1:2], a_xsm[:, 0:1], -1.0 / D, None,
                     ALU.mult)
                P.op(v, "tensor_scalar", a_xc[:], xt, a_xsm[:, 1:2], None, ALU.add)
                P.op(v, "tensor_tensor", a_xsq, a_xc[:], a_xc[:], ALU.mult)
                d1 = P.op(v, "tensor_reduce", a_xsm[:, 2:3], a_xsq, axis=AX.X,
                          op=ALU.add, inc="dve")
                P.wait(a, "dve", d1)
                P.op(a, "activation", a_xsm[:, 3:4], a_xsm[:, 2:3], AF.Ln,
                     bias=c_eps[:], scale=1.0 / D)
                r1 = P.op(a, "activation", a_xsm[:, 4:5], a_xsm[:, 3:4], AF.Exp,
                          scale=-0.5, inc="act")
                P.wait(v, "act", r1)
                xs["apply"] = P.op(v, "tensor_scalar", a_xns, a_xc[:], a_xsm[:, 4:5],
                                   None, ALU.mult, inc="dve")

            def g2():
                P.wait(t, "dve", xs["apply"])
                px_wait()
                for dt in range(8):
                    p1 = P.op(t, "transpose", p_x_b[:, dt * 128:(dt + 1) * 128],
                              a_xns[:, dt * 128:(dt + 1) * 128], id128[:],
                              inc="pe" if dt == 7 else None)
                P.wait(a, "pe", p1)
                P.op(a, "activation", S["xnt"], p_x_b[:, 0:1024], AF.Copy, inc="act")
                evac_done("act")

            def proj(dname, wsb, half):
                def f():
                    dst = a_qraw if dname == "qt_raw" else S[dname]
                    px_wait()
                    for dt in range(8):
                        p1 = P.op(t, "matmul", p_x[:],
                                  S["xnt"][:, dt * SEG:(dt + 1) * SEG],
                                  wsb[:, dt * D + half * 512:
                                      dt * D + half * 512 + 512],
                                  start=(dt == 0), stop=(dt == 7),
                                  inc="pe" if dt == 7 else None)
                    P.wait(a, "pe", p1)
                    P.op(a, "activation", dst[:, half * 512:half * 512 + 512],
                         p_x[:], AF.Copy, inc="act")
                    evac_done("act")
                return f

            def g6():
                px_wait()
                for dt in range(8):
                    p1 = P.op(t, "matmul", p_x[:, 0:SEG],
                              a_wf1[:, dt * DK:(dt + 1) * DK],
                              S["xnt"][:, dt * SEG:(dt + 1) * SEG],
                              start=(dt == 0), stop=(dt == 7),
                              inc="pe" if dt == 7 else None)
                P.wait(a, "pe", p1)
                e2 = P.op(a, "activation", a_ut[:], p_x[:, 0:SEG], AF.Copy,
                          inc="act")
                evac_done("act")
                px_wait()
                P.wait(t, "act", e2)
                p1 = P.op(t, "matmul", p_x[:], a_ut[:], a_wf2[:], start=True,
                          stop=True, inc="pe")
                P.wait(a, "pe", p1)
                xs["ftanh"] = P.op(a, "activation", S["sig"], p_x[:], AF.Tanh,
                                   scale=0.5, inc="act")
                evac_done("act")

            def g6b():
                px_wait()
                for dt in range(8):
                    p1 = P.op(t, "matmul", p_x[:, 0:SEG],
                              a_wg1[:, dt * DK:(dt + 1) * DK],
                              S["xnt"][:, dt * SEG:(dt + 1) * SEG],
                              start=(dt == 0), stop=(dt == 7),
                              inc="pe" if dt == 7 else None)
                P.wait(a, "pe", p1)
                xs["ugt"] = P.op(a, "activation", a_ugt[:], p_x[:, 0:SEG], AF.Copy,
                                 inc="act")
                evac_done("act")

            def g6c(half):
                def f():
                    px_wait()
                    P.wait(t, "act", xs["ugt"])
                    p1 = P.op(t, "matmul", p_x[:], a_ugt[:],
                              a_wg2[:, half * 512:half * 512 + 512], start=True,
                              stop=True, inc="pe")
                    P.wait(a, "pe", p1)
                    c = P.op(a, "activation",
                             S["sg"][:, half * 512:half * 512 + 512],
                             p_x[:], AF.Tanh, scale=0.5, inc="act")
                    if half == 1:
                        xs["gtanh"] = c
                    evac_done("act")
                return f

            def g7qk():
                gp = "gpsimd"
                s1 = P.op(a, "activation", a_xact, a_qraw, AF.Tanh, scale=0.5,
                          inc="act")
                s2 = P.op(a, "activation", a_xns, S["ktm"], AF.Tanh, scale=0.5,
                          inc="act")
                P.wait(gp, "act", s1)
                P.op(gp, "tensor_scalar", a_xact, a_xact, 0.5, 0.5, ALU.mult,
                     op1=ALU.add)
                xs["qsilu"] = P.op(gp, "tensor_tensor", a_qraw, a_qraw, a_xact,
                                   ALU.mult)
                P.wait(gp, "act", s2)
                P.op(gp, "tensor_scalar", a_xns, a_xns, 0.5, 0.5, ALU.mult,
                     op1=ALU.add)
                xs["ksilu"] = P.op(gp, "tensor_tensor", S["ktm"], S["ktm"], a_xns,
                                   ALU.mult)

            def g7fg():
                gp = "gpsimd"
                P.wait(gp, "act", xs["ftanh"])
                xs["sig_gch"] = P.op(gp, "tensor_scalar", S["sig"], S["sig"],
                                     0.5, 0.5, ALU.mult, op1=ALU.add)
                P.wait(gp, "act", xs["gtanh"])
                XS_sg_gch[s % 2] = P.op(gp, "tensor_scalar", S["sg"], S["sg"],
                                        0.5, 0.5, ALU.mult, op1=ALU.add)

            def g8a():
                P.wait(t, "gch", xs["qsilu"])
                px_wait()
                for dt in range(8):
                    p1 = P.op(t, "transpose", p_x_b[:, dt * 128:(dt + 1) * 128],
                              a_qraw[:, dt * 128:(dt + 1) * 128], id128[:],
                              inc="pe" if dt == 7 else None)
                P.wait(a, "pe", p1)
                P.op(a, "activation", S["qt"], p_x_b[:, 0:1024], AF.Copy, inc="act")
                evac_done("act")

            def g8b():
                P.wait(t, "gch", xs["ksilu"])
                px_wait()
                for dt in range(8):
                    p1 = P.op(t, "transpose", p_x_b[:, dt * 128:(dt + 1) * 128],
                              S["ktm"][:, dt * 128:(dt + 1) * 128], id128[:],
                              inc="pe" if dt == 7 else None)
                P.wait(a, "pe", p1)
                P.op(a, "activation", S["kt"], p_x_b[:, 0:1024], AF.Copy, inc="act")
                evac_done("act")

            def g9():
                P.wait(a, "dve", xs["apply"])
                P.wait(a, "gch", xs["sig_gch"])
                l1 = P.op(a, "activation", a_xc[:, 0:512], S["sig"], AF.Ln, inc="act")
                P.wait(v, "act", l1)
                xs["dlt_dve"] = P.op(v, "tensor_scalar", S["dlt"], a_xc[:, 0:512],
                                     -C0, None, ALU.add, inc="dve")
                P.wait(v, "gch", xs["sig_gch"])
                d1 = P.op(v, "tensor_scalar", S["sig"], S["sig"], -1.0, 1.0, ALU.mult,
                          op1=ALU.add, inc="dve")
                P.wait(t, "dve", d1)
                px_wait()
                for pi in range(2):
                    rs2 = slice(pi * 64, pi * 64 + 64)
                    p1 = P.op(t, "matmul", p_x[rs2, :], triI[rs2, :], S["dlt"][rs2, :],
                              start=True, stop=True, tile_position=(pi * 64, pi * 64),
                              inc="pe" if pi == 1 else None)
                P.wait(a, "pe", p1)
                P.op(a, "activation", S["B"], p_x[:], AF.Exp, bias=c_lin[:])
                e1 = P.op(a, "activation", a_xact[:, 0:512], p_x[:], AF.Exp, bias=c_nlin[:],
                          scale=-1.0, inc="act")
                P.wait(v, "act", e1)
                d2 = P.op(v, "tensor_tensor", S["stl"], S["sig"], a_xact[:, 0:512], ALU.mult,
                          inc="dve")
                xs["g9_d2"] = d2
                xs["g9_e1"] = e1

            def g9b():
                P.wait(t, "dve", xs["g9_d2"])
                P.wait(t, "act", xs["g9_e1"])
                px_wait()
                for pi in range(2):
                    rs2 = slice(pi * 64, pi * 64 + 64)
                    p1 = P.op(t, "matmul", p_x[rs2, :], triL[rs2, :], S["dlt"][rs2, :],
                              start=True, stop=True, tile_position=(pi * 64, pi * 64),
                              inc="pe" if pi == 1 else None)
                P.wait(a, "pe", p1)
                P.wait(a, "dve", xs["g9_d2"])
                e2 = P.op(a, "activation", a_xact[:, 0:512], p_x[:], AF.Exp, bias=c_lrev[:],
                          inc="act")
                P.wait(v, "act", e2)
                P.op(v, "tensor_tensor", S["stb"], S["sig"], a_xact[:, 0:512], ALU.mult,
                     inc="dve")
                evac_done("act")

            def g10a():
                P.wait(t, "dve", xs["dlt_dve"])
                for c4 in range(2):
                    rs2 = slice(c4 * 64, c4 * 64 + 64)
                    px_wait()
                    p1 = P.op(t, "matmul", p_x[:], ones128[rs2, :],
                              S["dlt"][rs2, :], start=True, stop=True,
                              tile_position=(c4 * 64, 0), inc="pe")
                    P.wait(a, "pe", p1)
                    XS_bca_act[s % 2] = P.op(a, "activation",
                                             S["bca"][:, c4 * HM:(c4 + 1) * HM],
                                             p_x[:], AF.Exp, bias=c_c64[:],
                                             inc="act")
                    evac_done("act")

            def g10b():
                for c4 in range(2):
                    rs2 = slice(c4 * 64, c4 * 64 + 64)
                    px_wait()
                    for h in range(H):
                        for base in (0, 64):
                            p1 = P.op(t, "matmul", p_x[base:base + 64, h:h + 1],
                                      S["dlt"][rs2, h * 64:(h + 1) * 64],
                                      onescol[rs2, :], start=True, stop=True,
                                      tile_position=(c4 * 64, base),
                                      inc="pe" if (h == 7 and base == 64) else None)
                    P.wait(a, "pe", p1)
                    XS_bl_act[s % 2] = P.op(a, "activation",
                                            S["bl"][:, c4 * 8:c4 * 8 + 8],
                                            p_x[:, 0:8], AF.Exp, bias=c_c64[:],
                                            inc="act")
                    evac_done("act")

            return [g1, g2, proj("qt_raw", a_wq, 0), proj("qt_raw", a_wq, 1),
                    proj("ktm", a_wk, 0), proj("ktm", a_wk, 1), g7qk,
                    g6, g6b, g6c(0), g6c(1), g7fg, g8a, g8b, g9, g9b, g10a,
                    proj("vtm", a_wv, 0), proj("vtm", a_wv, 1), g10b]

        ys = {"state_dve": 0, "post_dve": 0}

        def emit_y(s, inject):
            S = XS[s % 2]
            for c4 in range(2):
                pi = c4
                rs = slice(pi * 64, pi * 64 + 64)
                P.wait(t, "dve", ys["state_dve"])
                P.wait(t, "dve", ys["post_dve"])
                for h in range(H):
                    tc = slice(h * SEG + c4 * 64, h * SEG + c4 * 64 + 64)
                    pat = P.op(t, "matmul", p_a[rs, h * 64:h * 64 + 64],
                               S["kt"][:, tc], S["qt"][:, tc], start=True, stop=True,
                               tile_position=(0, pi * 64),
                               inc="pe" if h == 7 else None)
                P.wait(v, "pe", pat)
                atm_dve = P.op(v, "tensor_tensor", a_atm[rs, :], p_a[rs, :],
                               triI8[rs, :], ALU.mult, inc="dve")
                inject()
                P.wait(t, "dve", atm_dve)
                for h in range(H):
                    tc = slice(h * SEG + c4 * 64, h * SEG + c4 * 64 + 64)
                    oc = slice(h * 64, h * 64 + 64)
                    P.op(t, "matmul", p_ok[rs, oc], a_atm[rs, oc],
                         S["stl"][rs, h * 64:(h + 1) * 64],
                         start=True, stop=False, tile_position=(pi * 64, pi * 64))
                    pok = P.op(t, "matmul", p_ok[rs, oc], S["qt"][:, tc],
                               a_hk[:, oc], start=False, stop=True,
                               tile_position=(0, pi * 64),
                               inc="pe" if h == 7 else None)
                inject()
                for h in range(H):
                    P.op(t, "matmul", p_st[:, h * 64:h * 64 + 64],
                         S["ktm"][rs, h * 128:(h + 1) * 128],
                         S["stb"][rs, h * 64:(h + 1) * 64],
                         start=True, stop=True, tile_position=(pi * 64, 0),
                         inc="pe" if h == 7 else None)
                inject()
                P.wait(v, "pe", pok)
                okb = a_otmp[rs, 0:512]
                qvt = a_otmp[rs, 512:1024]
                okb3 = okb.rearrange("p (h m) -> p h m", m=64)
                qvt3 = qvt.rearrange("p (h m) -> p h m", m=64)
                okb_dve = P.op(v, "tensor_tensor", okb, p_ok[rs, :], S["B"][rs, :],
                               ALU.mult, inc="dve")
                P.op(v, "tensor_reduce", a_nmx[rs, :], okb3, axis=AX.X, op=ALU.max,
                     negate=True)
                sm_dve = P.op(v, "tensor_tensor", qvt3, okb3,
                              a_nmx[rs, :].unsqueeze(2).broadcast_to([64, 8, 64]),
                              ALU.add, inc="dve")
                P.wait(a, "dve", sm_dve)
                sm_act = P.op(a, "activation", okb, qvt, AF.Exp, inc="act")
                P.wait(v, "act", sm_act)
                P.op(v, "tensor_reduce", a_ssum[rs, :], okb3, axis=AX.X, op=ALU.add)
                P.op(v, "reciprocal", a_rsum[rs, :], a_ssum[rs, :])
                P.op(v, "tensor_tensor", qvt3, okb3,
                     a_rsum[rs, :].unsqueeze(2).broadcast_to([64, 8, 64]), ALU.mult)
                qb_dve = P.op(v, "tensor_tensor", a_qb[rs, :], qvt, S["B"][rs, :],
                              ALU.mult, inc="dve")
                P.wait(t, "dve", okb_dve)
                for h in range(H):
                    sbc = S["stb"][rs, h * 64:(h + 1) * 64]
                    vv = S["vtm"][rs, h * 128:(h + 1) * 128]
                    bank2 = p_st2 if h < 4 else p_ok
                    for base in (0, 64):
                        pov_st = P.op(t, "matmul",
                                      bank2[base:base + 64,
                                            (h % 4) * 128:(h % 4 + 1) * 128],
                                      sbc, vv, start=True, stop=True,
                                      tile_position=(pi * 64, base),
                                      inc="pe" if (h == 7 and base == 64) else None)
                inject()
                P.wait(t, "dve", qb_dve)
                for h in range(H):
                    oc = h * 64
                    P.op(t, "transpose", p_tp[rs, oc:oc + 64],
                         a_qb[rs, h * 64:h * 64 + 64], id64[rs, :],
                         tile_position=(pi * 64, pi * 64))
                    ptr = P.op(t, "transpose", p_tp[rs, 512 + oc:512 + oc + 64],
                               S["stl"][rs, h * 64:h * 64 + 64], id64[rs, :],
                               tile_position=(pi * 64, pi * 64),
                               inc="pe" if h == 7 else None)
                P.wait(v, "pe", ptr)
                P.op(v, "tensor_copy", a_qbt[rs, :], p_tp[rs, 0:512])
                tt_dve = P.op(v, "tensor_copy", a_stT[rs, :], p_tp[rs, 512:1024],
                              inc="dve")
                P.wait(t, "dve", tt_dve)
                for h in range(H):
                    oc = h * 64
                    pat2 = P.op(t, "matmul", p_a[rs, h * 64:h * 64 + 64],
                                a_stT[rs, oc:oc + 64], a_qbt[rs, oc:oc + 64],
                                start=True, stop=True,
                                tile_position=(pi * 64, pi * 64),
                                inc="pe" if h == 7 else None)
                P.wait(v, "pe", pat2)
                atm2_dve = P.op(v, "tensor_tensor", a_atm[rs, :], p_a[rs, :],
                                triI8[rs, :], ALU.mult, inc="dve")
                inject()
                P.wait(t, "dve", atm2_dve)
                for h in range(H):
                    oc = h * 64
                    vv = S["vtm"][rs, h * 128:(h + 1) * 128]
                    P.op(t, "matmul", p_ov[rs, h * 128:(h + 1) * 128],
                         a_atm[rs, h * 64:h * 64 + 64], vv,
                         start=True, stop=False, tile_position=(pi * 64, pi * 64))
                    ov_pe = P.op(t, "matmul", p_ov[rs, h * 128:(h + 1) * 128],
                                 a_qbt[rs, oc:oc + 64],
                                 a_hv[rs, h * 128:(h + 1) * 128],
                                 start=False, stop=True,
                                 tile_position=(pi * 64, pi * 64),
                                 inc="pe" if h == 7 else None)
                inject()
                P.wait(v, "pe", ov_pe)
                P.wait(v, "act", XS_bca_act[s % 2])
                P.op(v, "tensor_tensor", a_hkt, a_hk,
                     S["bca"][:, c4 * HM:(c4 + 1) * HM], ALU.mult)
                P.op(v, "tensor_tensor", a_hk, a_hkt, p_st[:], ALU.add)
                P.wait(v, "act", XS_bl_act[s % 2])
                for h in range(H):
                    bank2 = p_st2 if h < 4 else p_ok
                    sd = P.op(v, "scalar_tensor_tensor",
                              a_hv[:, h * 128:(h + 1) * 128],
                              a_hv[:, h * 128:(h + 1) * 128],
                              S["bl"][:, c4 * 8 + h:c4 * 8 + h + 1],
                              bank2[:, (h % 4) * 128:(h % 4 + 1) * 128],
                              ALU.mult, ALU.add, inc="dve" if h == 7 else None)
                ys["state_dve"] = sd

            P.wait(v, "gch", XS_sg_gch[s % 2])
            P.op(v, "tensor_tensor", a_otmp, p_ov[:], S["sg"], ALU.mult)
            P.op(v, "tensor_tensor", a_o2, a_otmp, a_otmp, ALU.mult)
            rm_dve = P.op(v, "tensor_reduce", a_sm1[:, 8:16],
                          a_o2.rearrange("p (h d) -> p h d", d=128), axis=AX.X,
                          op=ALU.add, inc="dve")
            P.wait(a, "dve", rm_dve)
            P.op(a, "activation", a_sm1[:, 0:8], a_sm1[:, 8:16], AF.Ln,
                 bias=c_eps[:], scale=1.0 / 128)
            rm_act = P.op(a, "activation", a_sm1[:, 8:16], a_sm1[:, 0:8], AF.Exp,
                          scale=-0.5, inc="act")
            P.wait(v, "act", rm_act)
            for h in range(H):
                o2_dve = P.op(v, "tensor_scalar", a_o2[:, h * 128:(h + 1) * 128],
                              a_otmp[:, h * 128:(h + 1) * 128],
                              a_sm1[:, 8 + h:9 + h], None, ALU.mult,
                              inc="dve" if h == 7 else None)
            inject(3)
            P.wait(t, "dve", o2_dve)
            for dt in range(8):
                po2 = P.op(t, "transpose", p_tp[:, dt * 128:(dt + 1) * 128],
                           a_o2[:, dt * 128:(dt + 1) * 128], id128[:],
                           inc="pe" if dt == 7 else None)
            P.wait(v, "pe", po2)
            o2t_dve = P.op(v, "tensor_copy", a_o2t, p_tp[:, 0:1024], inc="dve")
            P.wait(t, "dve", o2t_dve)
            for dt in range(8):
                for dh in range(2):
                    pwo = P.op(t, "matmul", p_ov[:, dh * 512:dh * 512 + 512],
                               a_o2t[:, dt * 128:(dt + 1) * 128],
                               a_wo[:, dt * D + dh * 512: dt * D + dh * 512 + 512],
                               start=(dt == 0), stop=(dt == 7),
                               inc="pe" if (dt == 7 and dh == 1) else None)
            inject(3)
            P.wait(v, "pe", pwo)
            ys["post_dve"] = P.op(v, "tensor_tensor", x_buf[:, s * D:(s + 1) * D],
                                  x_buf[:, s * D:(s + 1) * D], p_ov[:], ALU.add,
                                  inc="dve")

        for gr in make_x_groups(0):
            gr()
        pend = []

        def inject(k=1):
            for _ in range(k):
                if pend:
                    pend.pop(0)()
        for s in range(NSEG):
            if s + 1 < NSEG:
                pend = make_x_groups(s + 1)
                pend.pop(0)()
            else:
                pend = []
            emit_y(s, inject)
            while pend:
                pend.pop(0)()

    def emit_ffn(b):
        v, t, a, g, sy = "vector", "tensor", "scalar", "gpsimd", "sync"
        P.bar(g, "pe", "dve", "act")
        P.bar(v, "pe", "act")
        P.bar(a, "pe", "dve")
        P.bar(t, "dve", "act")
        P.op(g, "dma_start", f_b1n[:], b1_d, inc="gs")
        P.op(g, "dma_start", f_b2a[:], b2_d, inc="gs")
        gsw = P.op(g, "dma_start", f_wr[:].rearrange("p (dt e) -> p dt e", e=E),
                   wr_d.rearrange("(dt p) e -> p dt e", p=128), inc="gs")
        P.wait(t, "gs", gsw)
        P.bar(t, "dve", "act")
        for fb in range(32):
            pb1 = P.op(t, "transpose", p_tp[0:128, fb * 8:fb * 8 + 8],
                       f_b1n[0:8, fb * 128:(fb + 1) * 128], id128[0:8, 0:8],
                       inc="pe" if fb == 31 else None)
        P.wait(v, "pe", pb1)
        b1t_dve = P.op(v, "tensor_copy", f_b1t[:], p_tp[:, 0:256], inc="dve")

        xnts = (f_xnt0, f_xnt1)
        wgs = (f_wg0, f_wg1)
        w1bufs = (f_w1a, f_w1b)
        w2bufs = (f_w2a, f_w2b, f_w2c, f_w2d)
        mm1banks = ((p_ov[:, 0:512], p_ov[:, 512:1024]), (p_st[:], p_st2[:]))
        accb = (p_a[:, 0:512], p_ok[:, 0:512], p_st[:, 0:512], p_st2[:, 0:512],
                p_ov[:, 0:512], p_ov[:, 512:1024], p_x[:, 0:512], p_tp_f[:, 0:512])
        bank_free = {i: ("dve", b1t_dve if i == 7 else 0) for i in range(8)}

        def ln2_router(grp):
            for f in ln2_parts(grp):
                f()

        def ln2_parts(grp):
            gx = xnts[grp % 2]
            gw = wgs[grp % 2]
            g0 = grp * (GRP // 128)
            l4 = [0]
            l5 = [0]
            rstd_act = [0]

            def stats_all():
                for tl in range(8):
                    xt = x_buf[:, (g0 + tl) * D:(g0 + tl + 1) * D]
                    P.wait(v, "pe", l4[0])
                    P.op(v, "tensor_reduce", f_sm[:, 0:1], xt, axis=AX.X, op=ALU.add)
                    P.op(v, "tensor_scalar", f_rt[:, tl:tl + 1], f_sm[:, 0:1],
                         -1.0 / D, None, ALU.mult)
                    P.op(v, "tensor_tensor", f_xns, xt, xt, ALU.mult)
                    P.op(v, "tensor_reduce", f_rt[:, 8 + tl:9 + tl], f_xns,
                         axis=AX.X, op=ALU.add)
                P.op(v, "tensor_scalar", f_rt[:, 8:16], f_rt[:, 8:16], 1.0 / D, None,
                     ALU.mult)
                P.op(v, "tensor_tensor", f_rt[:, 16:24], f_rt[:, 0:8], f_rt[:, 0:8],
                     ALU.mult)
                st = P.op(v, "tensor_sub", f_rt[:, 8:16], f_rt[:, 8:16],
                          f_rt[:, 16:24], inc="dve")
                P.wait(a, "dve", st)
                P.op(a, "activation", f_rt[:, 16:24], f_rt[:, 8:16], AF.Ln,
                     bias=c_eps[:])
                rstd_act[0] = P.op(a, "activation", f_rt[:, 16:24], f_rt[:, 16:24],
                                   AF.Exp, scale=-0.5, inc="act")

            def tile(tl):
                def f():
                    xt = x_buf[:, (g0 + tl) * D:(g0 + tl + 1) * D]
                    P.wait(v, "act", rstd_act[0])
                    P.wait(v, "pe", l4[0])
                    l3 = P.op(v, "tensor_scalar", f_xns, xt, f_rt[:, tl:tl + 1],
                              f_rt[:, 16 + tl:17 + tl], ALU.add, op1=ALU.mult,
                              inc="dve")
                    P.wait(t, "dve", l3)
                    P.wait(t, bank_free[7][0], bank_free[7][1])
                    for dt in range(8):
                        l4[0] = P.op(t, "transpose", p_tp[:, dt * 128:(dt + 1) * 128],
                                     f_xns[:, dt * 128:(dt + 1) * 128], id128[:],
                                     inc="pe" if dt == 7 else None)
                    P.wait(a, "pe", l4[0])
                    l5[0] = P.op(a, "activation",
                                 gx[:, 0:8192].rearrange("p (dt tk) -> p dt tk",
                                                         tk=GRP)
                                 [:, :, tl * 128:tl * 128 + 128],
                                 p_tp[:, 0:1024].rearrange("p (dt tk) -> p dt tk",
                                                           tk=128),
                                 AF.Copy, inc="act")
                    bank_free[7] = ("act", l5[0])
                return f

            def router():
                P.wait(t, "act", l5[0])
                P.wait(t, bank_free[6][0], bank_free[6][1])
                for tl in range(8):
                    for dt in range(8):
                        r1 = P.op(t, "matmul", p_x[:, tl * 8:tl * 8 + 8],
                                  gx[:, dt * GRP + tl * 128:
                                     dt * GRP + tl * 128 + 128],
                                  f_wr[:, dt * 8:(dt + 1) * 8],
                                  start=(dt == 0), stop=(dt == 7),
                                  inc="pe" if (tl == 7 and dt == 7) else None)
                P.wait(v, "pe", r1)
                L3 = p_x[:, 0:64].rearrange("p (tt e) -> p tt e", e=8)
                P.op(v, "tensor_reduce", f_sm[:, 8:16], L3, axis=AX.X, op=ALU.max,
                     negate=True)
                r2 = P.op(v, "tensor_tensor",
                          f_rt[:, 0:64].rearrange("p (tt e) -> p tt e", e=8), L3,
                          f_sm[:, 8:16].unsqueeze(2).broadcast_to([128, 8, 8]),
                          ALU.add, inc="dve")
                bank_free[6] = ("dve", r2)
                P.wait(a, "dve", r2)
                r3 = P.op(a, "activation", f_rt[:, 64:128], f_rt[:, 0:64], AF.Exp,
                          inc="act")
                P.wait(v, "act", r3)
                P.op(v, "tensor_reduce", f_sm[:, 8:16],
                     f_rt[:, 64:128].rearrange("p (tt e) -> p tt e", e=8), axis=AX.X,
                     op=ALU.add)
                P.op(v, "reciprocal", f_sm[:, 0:8], f_sm[:, 8:16])
                P.op(v, "tensor_scalar", f_rt[:, 64:128], f_rt[:, 0:64], 0.0, None,
                     ALU.is_equal)
                P.op(v, "tensor_tensor", gw[:].rearrange("p (tt e) -> p tt e", e=8),
                     f_rt[:, 64:128].rearrange("p (tt e) -> p tt e", e=8),
                     f_sm[:, 0:8].unsqueeze(2).broadcast_to([128, 8, 8]), ALU.mult,
                     inc="dve")

            return [stats_all] + [tile(i) for i in range(8)] + [router]

        pe_after_w1use = {}
        pe_after_w2use = {}
        gelu_after = {}

        ln2_router(0)
        ln2_pend = []
        for grp in range(NGRP):
            if grp + 1 < NGRP:
                ln2_pend = ln2_parts(grp + 1)
            g0 = grp * (GRP // 128)
            gx = xnts[grp % 2]
            gw = wgs[grp % 2]
            for e in range(E):
                eg = (b * NGRP + grp) * E + e
                w1row = w1sc[e * D:(e + 1) * D, :]
                glast = None
                for fbg in range(8):
                    gidx = eg * 8 + fbg
                    par = gidx % 2
                    psem = "w1%d" % par
                    if gidx >= 2:
                        P.wait(g, "pe", pe_after_w1use.get(gidx - 2))
                    ld = P.op(g, "dma_start",
                              w1bufs[par][:].rearrange("p (dt f) -> p dt f", f=512),
                              w1row[:, fbg * 512:(fbg + 1) * 512].rearrange(
                                  "(dt p) f -> p dt f", p=128), inc=psem)
                    P.wait(t, psem, ld)
                    if fbg == 0:
                        for i in (2, 3, 4, 5):
                            P.wait(t, bank_free[i][0], bank_free[i][1])
                        P.wait(t, "dve", b1t_dve)
                    for fi in range(4):
                        fb = fbg * 4 + fi
                        fgi = eg * 32 + fb
                        bsel = mm1banks[fi % 2]
                        if gelu_after.get(fgi - 2) is not None:
                            P.wait(t, "act", gelu_after[fgi - 2])
                        for dt in range(8):
                            for half in range(2):
                                m1 = P.op(t, "matmul", bsel[half],
                                          w1bufs[par][:, dt * 512 + fi * 128:
                                                       dt * 512 + fi * 128 + 128],
                                          gx[:, dt * GRP + half * 512:
                                             dt * GRP + half * 512 + 512],
                                          start=(dt == 0), stop=(dt == 7),
                                          inc="pe" if (dt == 7 and half == 1)
                                          else None)
                        P.wait(a, "pe", m1)
                        for half in range(2):
                            glast = P.op(a, "activation",
                                         f_ht[:, fb * GRP + half * 512:
                                              fb * GRP + half * 512 + 512],
                                         bsel[half], AF.Gelu,
                                         bias=f_b1t[:, fb * 8 + e:fb * 8 + e + 1],
                                         inc="act" if half == 1 else None)
                        gelu_after[fgi] = P.cnt["act"]
                    pe_after_w1use[gidx] = P.cnt["pe"]
                    if ln2_pend:
                        ln2_pend.pop(0)()
                P.wait(g, "gs", gsw)
                P.wait(g, "pe", pe_after_b2.get(eg - 1))
                b2done = P.op(g, "dma_start", f_b2s[:], f_b2a[e:e + 1, :], inc="gs")
                w2row = w2sc[e * F:(e + 1) * F, :]
                P.wait(t, "gs", b2done)
                for h5 in range(2):
                    gl = gelu_after[eg * 32 + 31]
                    for fb in range(32):
                        gidx2 = (eg * 2 + h5) * 32 + fb
                        par = fb % 4
                        psem = "w2%d" % par
                        if gidx2 >= 4:
                            P.wait(g, "pe", pe_after_w2use.get(gidx2 - 4))
                        ld = P.op(g, "dma_start", w2bufs[par][:],
                                  w2row[fb * 128:(fb + 1) * 128, :], inc=psem)
                        P.wait(t, psem, ld)
                        if fb == 0:
                            P.wait(t, "act", gl)
                        for j, i in enumerate((4, 5, 2, 3, 0, 1, 6, 7)):
                            tl, dh = i // 2, i % 2
                            if fb == 0:
                                P.wait(t, bank_free[i][0], bank_free[i][1])
                            m2 = P.op(t, "matmul", accb[i],
                                      f_ht[:, fb * GRP + h5 * 512 + tl * 128:
                                           fb * GRP + h5 * 512 + tl * 128 + 128],
                                      w2bufs[par][:, dh * 512:dh * 512 + 512],
                                      start=(fb == 0), stop=False,
                                      inc="pe" if j == 7 else None)
                        pe_after_w2use[gidx2] = P.cnt["pe"]
                    for tl in range(4):
                        for dh in range(2):
                            m3 = P.op(t, "matmul", accb[tl * 2 + dh], onesrow[:],
                                      f_b2s[0:1, dh * 512:dh * 512 + 512],
                                      start=False, stop=True, tile_position=(0, 0),
                                      inc="pe" if (tl == 3 and dh == 1) else None)
                    pe_after_b2[eg] = P.cnt["pe"]
                    P.wait(v, "pe", m3)
                    for j, i in enumerate((4, 5, 2, 3, 0, 1, 6, 7)):
                        tl, dh = i // 2, i % 2
                        TT = g0 + h5 * 4 + tl
                        xs2 = x_buf[:, TT * D + dh * 512: TT * D + dh * 512 + 512]
                        ev = P.op(v, "scalar_tensor_tensor", xs2, accb[i],
                                  gw[:, (h5 * 4 + tl) * 8 + e:
                                     (h5 * 4 + tl) * 8 + e + 1],
                                  xs2, ALU.mult, ALU.add, inc="dve")
                        bank_free[i] = ("dve", ev)
            P.bar(sy, "dve")
            ydst = y_d[b * N + grp * GRP: b * N + (grp + 1) * GRP, :].rearrange(
                "(tt p) d -> p tt d", p=128)
            P.op(sy, "dma_start", ydst,
                 x_buf[:, g0 * D:(g0 + 8) * D].rearrange("p (tt d) -> p tt d", d=D),
                 inc="yst")

    emit_consts()
    for b in range(BPC):
        emit_attn_weight_loads(b)
        P.bar("sync", "yst")
        P.op("sync", "dma_start", x_buf[:].rearrange("p (tt d) -> p tt d", d=D),
             x_d[b * N:(b + 1) * N, :].rearrange("(tt p) d -> p tt d", p=128), inc="xin")
        if b == 0 and DO_FFN:
            emit_conversion()
        if DO_ATTN:
            emit_attention(b)
        if DO_FFN:
            P.wait("gpsimd", "cvo0", P.cnt["cvo0"])
            emit_ffn(b)
        else:
            P.bar("sync", "dve")
            ydst = y_d[b * N:(b + 1) * N, :].rearrange("(tt p) d -> p tt d", p=128)
            P.op("sync", "dma_start", ydst, x_buf[:].rearrange("p (tt d) -> p tt d", d=D),
                 inc="yst")
    P.wait("sync", "yst", P.cnt["yst"])

    with es, nc.Block() as block:
        for eng in Prog.ENGINES:
            stream = P.streams[eng]

            def make(fns):
                def _(e):
                    for f in fns:
                        f(e)
                return _
            getattr(block, eng)(make(stream))

    return nc


_NC_CACHE = None


def kernel(**inputs):
    global _NC_CACHE
    xs = np.ascontiguousarray(np.asarray(inputs["L0_attn_x"], dtype=np.float32))
    w = {
        "wq": inputs["L0_attn_Wq"], "wk": inputs["L0_attn_Wk"], "wv": inputs["L0_attn_Wv"],
        "wf1": inputs["L0_attn_Wf1"], "wf2": inputs["L0_attn_Wf2"],
        "wg1": inputs["L0_attn_Wg1"], "wg2": inputs["L0_attn_Wg2"],
        "wo": inputs["L0_attn_Wo"], "normw": inputs["L0_attn_norm_w"],
        "wr": inputs["L0_ffn_Wr"], "w1": inputs["L0_ffn_W1"], "b1": inputs["L0_ffn_b1"],
        "w2": inputs["L0_ffn_W2"], "b2": inputs["L0_ffn_b2"],
    }
    w = {k: np.ascontiguousarray(np.asarray(v, dtype=np.float32)) for k, v in w.items()}
    if _NC_CACHE is None:
        _NC_CACHE = build_nc()
    nc = _NC_CACHE
    in_maps = []
    for c in range(NCORES):
        m = {"x": np.ascontiguousarray(xs[c * BPC:(c + 1) * BPC].reshape(BPC * N, D))}
        m.update(w)
        in_maps.append(m)
    res = run_bass_kernel_spmd(nc, in_maps, core_ids=list(range(NCORES)))
    out = np.concatenate([r["y"].reshape(BPC, N, D) for r in res.results], axis=0)
    return out


# version 22
# speedup vs baseline: 1.0276x; 1.0124x over previous
import contextlib
import numpy as np
import concourse.bass as bass
import concourse.mybir as mybir
from concourse.bass_utils import run_bass_kernel_spmd

F32 = mybir.dt.float32
BF16 = mybir.dt.bfloat16
AF = mybir.ActivationFunctionType
ALU = mybir.AluOpType
AX = mybir.AxisListType

C0 = -0.6931471805599453
NCORES = 8
BPC = 2
N = 2048
D = 1024
H = 8
M = 64
DK = 128
HM = H * M
E = 8
F = 4096
SEG = 128
NSEG = N // SEG
GRP = 1024
NGRP = N // GRP
EPS = 1e-5

DO_FFN = True
DO_ATTN = True
N_SEG_EMIT = 99
PHASE_LIMIT = 99
SCAN_OPS = 99


class Prog:
    ENGINES = ("tensor", "vector", "scalar", "sync", "gpsimd")

    def __init__(self, sems):
        self.streams = {e: [] for e in self.ENGINES}
        self.sems = sems
        self.cnt = {k: 0 for k in sems}

    CHAIN = {"vector": "dve", "scalar": "act", "gpsimd": "gch"}

    def op(self, eng, fname, *args, inc=None, **kw):
        is_dma = "dma" in fname
        chain = None if is_dma else self.CHAIN.get(eng)
        if chain is not None:
            self.wait(eng, chain, self.cnt[chain])
            assert inc is None or inc == chain
            inc = chain
        h = self.sems[inc] if inc else None
        amt = 16 if is_dma else 1
        if inc:
            self.cnt[inc] += amt

        def f(e, fname=fname, args=args, kw=kw, h=h, amt=amt):
            ins = getattr(e, fname)(*args, **kw)
            if h is not None:
                ins.then_inc(h, amt)
        self.streams[eng].append(f)
        return self.cnt[inc] if inc else None

    def wait(self, eng, sem, val):
        if val is None or val <= 0:
            return
        h = self.sems[sem]

        def f(e, h=h, val=val):
            e.wait_ge(h, val)
        self.streams[eng].append(f)

    def bar(self, eng, *sem_names):
        for s in sem_names:
            self.wait(eng, s, self.cnt[s])


def build_nc():
    nc = bass.Bass("TRN2", target_bir_lowering=False)

    x_d = nc.dram_tensor("x", [BPC * N, D], F32, kind="ExternalInput").ap()
    wq_d = nc.dram_tensor("wq", [D, D], F32, kind="ExternalInput").ap()
    wk_d = nc.dram_tensor("wk", [D, D], F32, kind="ExternalInput").ap()
    wv_d = nc.dram_tensor("wv", [D, D], F32, kind="ExternalInput").ap()
    wf1_d = nc.dram_tensor("wf1", [D, DK], F32, kind="ExternalInput").ap()
    wf2_d = nc.dram_tensor("wf2", [DK, HM], F32, kind="ExternalInput").ap()
    wg1_d = nc.dram_tensor("wg1", [D, DK], F32, kind="ExternalInput").ap()
    wg2_d = nc.dram_tensor("wg2", [DK, D], F32, kind="ExternalInput").ap()
    wo_d = nc.dram_tensor("wo", [D, D], F32, kind="ExternalInput").ap()
    nw_d = nc.dram_tensor("normw", [D], F32, kind="ExternalInput").ap()
    wr_d = nc.dram_tensor("wr", [D, E], F32, kind="ExternalInput").ap()
    w1_d = nc.dram_tensor("w1", [E, D, F], F32, kind="ExternalInput").ap()
    b1_d = nc.dram_tensor("b1", [E, F], F32, kind="ExternalInput").ap()
    w2_d = nc.dram_tensor("w2", [E, F, D], F32, kind="ExternalInput").ap()
    b2_d = nc.dram_tensor("b2", [E, D], F32, kind="ExternalInput").ap()
    y_d = nc.dram_tensor("y", [BPC * N, D], F32, kind="ExternalOutput").ap()
    w1sc = nc.dram_tensor("w1sc", [E * D, F], BF16).ap()
    w2sc = nc.dram_tensor("w2sc", [E * F, D], BF16).ap()

    es = contextlib.ExitStack()
    sb = lambda name, shape, dt: es.enter_context(nc.sbuf_tensor(name, shape, dt))
    ps = lambda name, shape, dt: es.enter_context(nc.psum_tensor(name, shape, dt))

    x_buf = sb("x_buf", [128, (N // 128) * D], F32)
    ARENA_F32 = 36000
    arena = sb("arena", [128, ARENA_F32], F32)
    ar_f = arena[:]
    ar_b = ar_f.bitcast(BF16)

    class Alloc:
        def __init__(self):
            self.off = 0

        def f32(self, rows, cols):
            o = self.off
            self.off += ((cols * 4 + 127) // 128) * 128
            assert self.off <= ARENA_F32 * 4, f"arena overflow {self.off}"
            return ar_f[0:rows, o // 4: o // 4 + cols]

        def bf(self, rows, cols):
            o = self.off
            self.off += ((cols * 2 + 127) // 128) * 128
            assert self.off <= ARENA_F32 * 4, f"arena overflow {self.off}"
            return ar_b[0:rows, o // 2: o // 2 + cols]

    aa = Alloc()
    a_wq = aa.bf(128, 8 * D)
    a_wk = aa.bf(128, 8 * D)
    a_wv = aa.bf(128, 8 * D)
    a_wo = aa.bf(128, 8 * D)
    a_wf1 = aa.bf(128, 8 * DK)
    a_wf2 = aa.bf(128, HM)
    a_wg1 = aa.bf(128, 8 * DK)
    a_wg2 = aa.bf(128, D)
    a_nw = aa.f32(128, 8)
    XS = []
    for par in range(2):
        XS.append(dict(
            xnt=aa.bf(128, 8 * SEG),
            qt=aa.bf(128, H * SEG),
            kt=aa.bf(128, H * SEG),
            vtm=aa.bf(128, H * DK),
            ktm=aa.bf(128, H * DK),
            sig=aa.bf(128, HM),
            dlt=aa.bf(128, HM),
            stl=aa.bf(128, HM),
            stb=aa.bf(128, HM),
            B=aa.bf(128, HM),
            sg=aa.bf(128, D),
            bca=aa.bf(128, 2 * HM),
            bl=aa.f32(128, 16),
        ))
    a_qraw = aa.bf(128, D)
    a_xact = aa.bf(128, D)
    a_ut = aa.bf(128, DK)
    a_ugt = aa.bf(128, DK)
    a_xc = aa.f32(128, D)
    a_xsq = aa.bf(128, D)
    a_xns = aa.bf(128, D)
    a_xsm = aa.f32(128, 8)
    a_otmp = aa.f32(128, D)
    a_o2 = aa.bf(128, D)
    a_o2t = aa.bf(128, D)
    a_atm = aa.bf(128, HM)
    a_qb = aa.bf(128, HM)
    a_qbt = aa.bf(128, 8 * M)
    a_stT = aa.bf(128, 8 * M)
    a_hk = aa.bf(128, HM)
    a_hkt = aa.bf(128, HM)
    a_hv = aa.bf(128, 8 * DK)
    a_sm1 = aa.f32(128, 16)
    a_nmx = aa.f32(128, 8)
    a_ssum = aa.f32(128, 8)
    a_rsum = aa.f32(128, 8)
    attn_bytes = aa.off

    fa = Alloc()
    f_xnt0 = fa.bf(128, 8 * GRP)
    f_xnt1 = fa.bf(128, 8 * GRP)
    f_ht = fa.bf(128, 32 * GRP)
    f_w1a = fa.bf(128, 8 * 512)
    f_w1b = fa.bf(128, 8 * 512)
    f_w2a = fa.bf(128, D)
    f_w2b = fa.bf(128, D)
    f_w2c = fa.bf(128, D)
    f_w2d = fa.bf(128, D)
    f_b1n = fa.bf(8, F)
    f_b1t = fa.f32(128, 32 * E)
    f_b2a = fa.bf(8, D)
    f_b2s = fa.bf(1, D)
    f_wr = fa.bf(128, 8 * E)
    f_wg0 = fa.f32(128, 8 * E)
    f_wg1 = fa.f32(128, 8 * E)
    f_rt = fa.f32(128, 128)
    f_xns = fa.bf(128, D)
    f_lnc = fa.f32(128, D)
    f_sm = fa.f32(128, 16)
    ffn_bytes = fa.off
    assert max(attn_bytes, ffn_bytes) <= ARENA_F32 * 4, (attn_bytes, ffn_bytes)

    triI = sb("triI", [128, 64], BF16)
    triL = sb("triL", [128, 64], BF16)
    triI8 = sb("triI8", [128, 512], BF16)
    id64 = sb("id64", [128, 64], BF16)
    id128 = sb("id128", [128, 128], BF16)
    ones128 = sb("ones128", [128, 128], BF16)
    onescol = sb("onescol", [128, 1], BF16)
    onesrow = sb("onesrow", [1, 128], BF16)
    c_lin = sb("c_lin", [128, 1], F32)
    c_nlin = sb("c_nlin", [128, 1], F32)
    c_lrev = sb("c_lrev", [128, 1], F32)
    c_c64 = sb("c_c64", [128, 1], F32)
    c_eps = sb("c_eps", [128, 1], F32)
    c_it = sb("c_it", [128, 1], F32)
    c_it2 = sb("c_it2", [128, 1], F32)

    p_a = ps("p_a", [128, 512], F32)
    p_ok = ps("p_ok", [128, 512], F32)
    p_st = ps("p_st", [128, 512], F32)
    p_st2 = ps("p_st2", [128, 512], F32)
    p_ov = ps("p_ov", [128, 1024], F32)
    p_x = ps("p_x", [128, 512], F32)
    p_tp = ps("p_tp", [128, 1024], BF16)
    p_x_bf = p_x[:].bitcast(BF16)
    p_tp_f = p_tp[:].bitcast(F32)

    sems = {}
    for s in ("pe", "dve", "act", "gs", "gch", "xin", "win", "cvi0", "cvi1",
              "cvo0", "cvo1", "w10", "w11", "w20", "w21", "w22", "w23", "yst"):
        sems[s] = es.enter_context(nc.semaphore("s_" + s))
    P = Prog(sems)
    pe_after_b2 = {}

    def emit_consts():
        g = "gpsimd"
        P.op(g, "memset", triI[:], 1.0)
        P.op(g, "memset", triL[:], 1.0)
        P.op(g, "memset", triI8[:], 1.0)
        P.op(g, "memset", id64[:], 0.0)
        P.op(g, "memset", id128[:], 0.0)
        P.op(g, "memset", ones128[:], 1.0)
        P.op(g, "memset", onescol[:], 1.0)
        P.op(g, "memset", onesrow[:], 1.0)
        P.op(g, "memset", c_c64[:], 64.0 * C0)
        P.op(g, "memset", c_eps[:], EPS)
        for base in (0, 64):
            P.op(g, "affine_select", out=triI[base:base + 64, :], in_=triI[base:base + 64, :],
                 compare_op=ALU.is_ge, fill=0.0, base=0, pattern=[[1, 64]], channel_multiplier=-1)
            P.op(g, "affine_select", out=triL[base:base + 64, :], in_=triL[base:base + 64, :],
                 compare_op=ALU.is_gt, fill=0.0, base=0, pattern=[[-1, 64]], channel_multiplier=1)
            P.op(g, "affine_select", out=id64[base:base + 64, :], in_=id64[base:base + 64, :],
                 compare_op=ALU.not_equal, fill=1.0, base=0, pattern=[[-1, 64]], channel_multiplier=1)
            for hh in range(8):
                P.op(g, "affine_select", out=triI8[base:base + 64, hh * 64:hh * 64 + 64],
                     in_=triI8[base:base + 64, hh * 64:hh * 64 + 64],
                     compare_op=ALU.is_ge, fill=0.0, base=0, pattern=[[1, 64]], channel_multiplier=-1)
        P.op(g, "affine_select", out=id128[:], in_=id128[:], compare_op=ALU.not_equal,
             fill=1.0, base=0, pattern=[[-1, 128]], channel_multiplier=1)
        P.op(g, "iota", c_it[:], pattern=[[0, 1]], base=0, channel_multiplier=1,
             allow_small_or_imprecise_dtypes=True)
        v = "vector"
        P.bar(v, "gch")
        P.op(v, "tensor_scalar", c_it2[:], c_it[:], 64.0, None, ALU.is_ge)
        P.op(v, "scalar_tensor_tensor", c_it[:], c_it2[:], -64.0, c_it[:], ALU.mult, ALU.add)
        P.op(v, "tensor_scalar", c_lin[:], c_it[:], C0, C0, ALU.mult, op1=ALU.add)
        P.op(v, "tensor_scalar", c_nlin[:], c_lin[:], -1.0, None, ALU.mult)
        P.op(v, "tensor_scalar", c_lrev[:], c_it[:], -C0, 63.0 * C0, ALU.mult, op1=ALU.add,
             inc="dve")

    def emit_attn_weight_loads(b):
        g = "gpsimd"
        P.bar(g, "pe", "dve", "act", "yst")
        for dst, src in ((a_wq, wq_d), (a_wk, wk_d), (a_wv, wv_d), (a_wo, wo_d)):
            P.op(g, "dma_start", dst.rearrange("p (dt n) -> p dt n", n=D),
                 src.rearrange("(dt p) n -> p dt n", p=128), inc="win")
        P.op(g, "dma_start", a_wf1.rearrange("p (dt u) -> p dt u", u=DK),
             wf1_d.rearrange("(dt p) u -> p dt u", p=128), inc="win")
        P.op(g, "dma_start", a_wf2, wf2_d, inc="win")
        P.op(g, "dma_start", a_wg1.rearrange("p (dt u) -> p dt u", u=DK),
             wg1_d.rearrange("(dt p) u -> p dt u", p=128), inc="win")
        P.op(g, "dma_start", a_wg2, wg2_d, inc="win")
        win_done = P.op(g, "dma_start", a_nw, nw_d.rearrange("(dt p) -> p dt", p=128),
                        allow_slow_non_contiguous=True, inc="win")
        v = "vector"
        P.wait(v, "win", win_done)
        for dt in range(8):
            P.op(v, "tensor_scalar", a_wo[:, dt * D:(dt + 1) * D], a_wo[:, dt * D:(dt + 1) * D],
                 a_nw[:, dt:dt + 1], None, ALU.mult, inc="dve" if dt == 7 else None)

    def emit_conversion():
        g = "gpsimd"
        w1flat = w1_d.rearrange("e d f -> (e d) f")
        w2flat = w2_d.rearrange("e f d -> (e f) d")
        for i in range(8):
            P.op(g, "dma_start", w1sc[i * 1024:(i + 1) * 1024, :],
                 w1flat[i * 1024:(i + 1) * 1024, :], inc="cvo0")
        for i in range(8):
            P.op(g, "dma_start", w2sc[i * 4096:(i + 1) * 4096, :],
                 w2flat[i * 4096:(i + 1) * 4096, :], inc="cvo0")
        return 16

    def emit_attention(b):
        v, t, a, g = "vector", "tensor", "scalar", "gpsimd"
        win_need = P.cnt["win"]
        xin_need = P.cnt["xin"]
        P.wait(v, "win", win_need)
        P.wait(v, "xin", xin_need)
        P.op(v, "memset", a_hk, 0.0)
        P.op(v, "memset", a_hv, 0.0)
        P.wait(t, "win", win_need)
        P.wait(t, "xin", xin_need)
        p_x_b = p_x[:].bitcast(BF16)

        px_state = {}
        XS_sg_gch = [0, 0]
        XS_bca_act = [0, 0]
        XS_bl_act = [0, 0]

        def evac_done(eng_sem):
            px_state["free"] = (eng_sem, P.cnt[eng_sem])

        def px_wait():
            if "free" in px_state:
                P.wait(t, px_state["free"][0], px_state["free"][1])

        def make_x_groups(s):
            S = XS[s % 2]
            xs = {}

            def g1():
                xt = x_buf[:, s * D:(s + 1) * D]
                P.op(v, "tensor_reduce", a_xsm[:, 0:1], xt, axis=AX.X, op=ALU.add)
                P.op(v, "tensor_scalar", a_xsm[:, 1:2], a_xsm[:, 0:1], -1.0 / D, None,
                     ALU.mult)
                P.op(v, "tensor_scalar", a_xc[:], xt, a_xsm[:, 1:2], None, ALU.add)
                P.op(v, "tensor_tensor", a_xsq, a_xc[:], a_xc[:], ALU.mult)
                d1 = P.op(v, "tensor_reduce", a_xsm[:, 2:3], a_xsq, axis=AX.X,
                          op=ALU.add, inc="dve")
                P.wait(a, "dve", d1)
                P.op(a, "activation", a_xsm[:, 3:4], a_xsm[:, 2:3], AF.Ln,
                     bias=c_eps[:], scale=1.0 / D)
                r1 = P.op(a, "activation", a_xsm[:, 4:5], a_xsm[:, 3:4], AF.Exp,
                          scale=-0.5, inc="act")
                P.wait(v, "act", r1)
                xs["apply"] = P.op(v, "tensor_scalar", a_xns, a_xc[:], a_xsm[:, 4:5],
                                   None, ALU.mult, inc="dve")

            def g2():
                P.wait(t, "dve", xs["apply"])
                px_wait()
                for dt in range(8):
                    p1 = P.op(t, "transpose", p_x_b[:, dt * 128:(dt + 1) * 128],
                              a_xns[:, dt * 128:(dt + 1) * 128], id128[:],
                              inc="pe" if dt == 7 else None)
                P.wait(a, "pe", p1)
                P.op(a, "activation", S["xnt"], p_x_b[:, 0:1024], AF.Copy, inc="act")
                evac_done("act")

            def proj(dname, wsb, half):
                def f():
                    dst = a_qraw if dname == "qt_raw" else S[dname]
                    px_wait()
                    for dt in range(8):
                        p1 = P.op(t, "matmul", p_x[:],
                                  S["xnt"][:, dt * SEG:(dt + 1) * SEG],
                                  wsb[:, dt * D + half * 512:
                                      dt * D + half * 512 + 512],
                                  start=(dt == 0), stop=(dt == 7),
                                  inc="pe" if dt == 7 else None)
                    P.wait(a, "pe", p1)
                    P.op(a, "activation", dst[:, half * 512:half * 512 + 512],
                         p_x[:], AF.Copy, inc="act")
                    evac_done("act")
                return f

            def g6():
                px_wait()
                for dt in range(8):
                    p1 = P.op(t, "matmul", p_x[:, 0:SEG],
                              a_wf1[:, dt * DK:(dt + 1) * DK],
                              S["xnt"][:, dt * SEG:(dt + 1) * SEG],
                              start=(dt == 0), stop=(dt == 7),
                              inc="pe" if dt == 7 else None)
                P.wait(a, "pe", p1)
                e2 = P.op(a, "activation", a_ut[:], p_x[:, 0:SEG], AF.Copy,
                          inc="act")
                evac_done("act")
                px_wait()
                P.wait(t, "act", e2)
                p1 = P.op(t, "matmul", p_x[:], a_ut[:], a_wf2[:], start=True,
                          stop=True, inc="pe")
                P.wait(a, "pe", p1)
                xs["ftanh"] = P.op(a, "activation", S["sig"], p_x[:], AF.Tanh,
                                   scale=0.5, inc="act")
                evac_done("act")

            def g6b():
                px_wait()
                for dt in range(8):
                    p1 = P.op(t, "matmul", p_x[:, 0:SEG],
                              a_wg1[:, dt * DK:(dt + 1) * DK],
                              S["xnt"][:, dt * SEG:(dt + 1) * SEG],
                              start=(dt == 0), stop=(dt == 7),
                              inc="pe" if dt == 7 else None)
                P.wait(a, "pe", p1)
                xs["ugt"] = P.op(a, "activation", a_ugt[:], p_x[:, 0:SEG], AF.Copy,
                                 inc="act")
                evac_done("act")

            def g6c(half):
                def f():
                    px_wait()
                    P.wait(t, "act", xs["ugt"])
                    p1 = P.op(t, "matmul", p_x[:], a_ugt[:],
                              a_wg2[:, half * 512:half * 512 + 512], start=True,
                              stop=True, inc="pe")
                    P.wait(a, "pe", p1)
                    c = P.op(a, "activation",
                             S["sg"][:, half * 512:half * 512 + 512],
                             p_x[:], AF.Tanh, scale=0.5, inc="act")
                    if half == 1:
                        xs["gtanh"] = c
                    evac_done("act")
                return f

            def g7qk():
                gp = "gpsimd"
                s1 = P.op(a, "activation", a_xact, a_qraw, AF.Tanh, scale=0.5,
                          inc="act")
                s2 = P.op(a, "activation", a_xns, S["ktm"], AF.Tanh, scale=0.5,
                          inc="act")
                P.wait(gp, "act", s1)
                P.op(gp, "tensor_scalar", a_xact, a_xact, 0.5, 0.5, ALU.mult,
                     op1=ALU.add)
                xs["qsilu"] = P.op(gp, "tensor_tensor", a_qraw, a_qraw, a_xact,
                                   ALU.mult)
                P.wait(gp, "act", s2)
                P.op(gp, "tensor_scalar", a_xns, a_xns, 0.5, 0.5, ALU.mult,
                     op1=ALU.add)
                xs["ksilu"] = P.op(gp, "tensor_tensor", S["ktm"], S["ktm"], a_xns,
                                   ALU.mult)

            def g7fg():
                gp = "gpsimd"
                P.wait(gp, "act", xs["ftanh"])
                xs["sig_gch"] = P.op(gp, "tensor_scalar", S["sig"], S["sig"],
                                     0.5, 0.5, ALU.mult, op1=ALU.add)
                P.wait(gp, "act", xs["gtanh"])
                XS_sg_gch[s % 2] = P.op(gp, "tensor_scalar", S["sg"], S["sg"],
                                        0.5, 0.5, ALU.mult, op1=ALU.add)

            def g8a():
                P.wait(t, "gch", xs["qsilu"])
                px_wait()
                for dt in range(8):
                    p1 = P.op(t, "transpose", p_x_b[:, dt * 128:(dt + 1) * 128],
                              a_qraw[:, dt * 128:(dt + 1) * 128], id128[:],
                              inc="pe" if dt == 7 else None)
                P.wait(a, "pe", p1)
                P.op(a, "activation", S["qt"], p_x_b[:, 0:1024], AF.Copy, inc="act")
                evac_done("act")

            def g8b():
                P.wait(t, "gch", xs["ksilu"])
                px_wait()
                for dt in range(8):
                    p1 = P.op(t, "transpose", p_x_b[:, dt * 128:(dt + 1) * 128],
                              S["ktm"][:, dt * 128:(dt + 1) * 128], id128[:],
                              inc="pe" if dt == 7 else None)
                P.wait(a, "pe", p1)
                P.op(a, "activation", S["kt"], p_x_b[:, 0:1024], AF.Copy, inc="act")
                evac_done("act")

            def g9prep():
                P.wait(a, "dve", xs["apply"])
                P.wait(a, "gch", xs["sig_gch"])
                l1 = P.op(a, "activation", a_xc[:, 0:512], S["sig"], AF.Ln, inc="act")
                P.wait(v, "act", l1)
                xs["dlt_dve"] = P.op(v, "tensor_scalar", S["dlt"], a_xc[:, 0:512],
                                     -C0, None, ALU.add, inc="dve")
                P.wait(v, "gch", xs["sig_gch"])
                xs["g9_d1"] = P.op(v, "tensor_scalar", S["sig"], S["sig"], -1.0, 1.0,
                                   ALU.mult, op1=ALU.add, inc="dve")

            def g9():
                P.wait(t, "dve", xs["g9_d1"])
                px_wait()
                for pi in range(2):
                    rs2 = slice(pi * 64, pi * 64 + 64)
                    p1 = P.op(t, "matmul", p_x[rs2, :], triI[rs2, :], S["dlt"][rs2, :],
                              start=True, stop=True, tile_position=(pi * 64, pi * 64),
                              inc="pe" if pi == 1 else None)
                P.wait(a, "pe", p1)
                P.op(a, "activation", S["B"], p_x[:], AF.Exp, bias=c_lin[:])
                e1 = P.op(a, "activation", a_xact[:, 0:512], p_x[:], AF.Exp, bias=c_nlin[:],
                          scale=-1.0, inc="act")
                P.wait(v, "act", e1)
                d2 = P.op(v, "tensor_tensor", S["stl"], S["sig"], a_xact[:, 0:512], ALU.mult,
                          inc="dve")
                xs["g9_d2"] = d2
                xs["g9_e1"] = e1

            def g9b():
                P.wait(t, "dve", xs["g9_d2"])
                P.wait(t, "act", xs["g9_e1"])
                px_wait()
                for pi in range(2):
                    rs2 = slice(pi * 64, pi * 64 + 64)
                    p1 = P.op(t, "matmul", p_x[rs2, :], triL[rs2, :], S["dlt"][rs2, :],
                              start=True, stop=True, tile_position=(pi * 64, pi * 64),
                              inc="pe" if pi == 1 else None)
                P.wait(a, "pe", p1)
                P.wait(a, "dve", xs["g9_d2"])
                e2 = P.op(a, "activation", a_xact[:, 0:512], p_x[:], AF.Exp, bias=c_lrev[:],
                          inc="act")
                P.wait(v, "act", e2)
                P.op(v, "tensor_tensor", S["stb"], S["sig"], a_xact[:, 0:512], ALU.mult,
                     inc="dve")
                evac_done("act")

            def g10a():
                P.wait(t, "dve", xs["dlt_dve"])
                for c4 in range(2):
                    rs2 = slice(c4 * 64, c4 * 64 + 64)
                    px_wait()
                    p1 = P.op(t, "matmul", p_x[:], ones128[rs2, :],
                              S["dlt"][rs2, :], start=True, stop=True,
                              tile_position=(c4 * 64, 0), inc="pe")
                    P.wait(a, "pe", p1)
                    XS_bca_act[s % 2] = P.op(a, "activation",
                                             S["bca"][:, c4 * HM:(c4 + 1) * HM],
                                             p_x[:], AF.Exp, bias=c_c64[:],
                                             inc="act")
                    evac_done("act")

            def g10b():
                for c4 in range(2):
                    rs2 = slice(c4 * 64, c4 * 64 + 64)
                    px_wait()
                    for h in range(H):
                        for base in (0, 64):
                            p1 = P.op(t, "matmul", p_x[base:base + 64, h:h + 1],
                                      S["dlt"][rs2, h * 64:(h + 1) * 64],
                                      onescol[rs2, :], start=True, stop=True,
                                      tile_position=(c4 * 64, base),
                                      inc="pe" if (h == 7 and base == 64) else None)
                    P.wait(a, "pe", p1)
                    XS_bl_act[s % 2] = P.op(a, "activation",
                                            S["bl"][:, c4 * 8:c4 * 8 + 8],
                                            p_x[:, 0:8], AF.Exp, bias=c_c64[:],
                                            inc="act")
                    evac_done("act")

            return [g1, g2, proj("qt_raw", a_wq, 0), proj("qt_raw", a_wq, 1),
                    proj("ktm", a_wk, 0), proj("ktm", a_wk, 1), g7qk,
                    g6, g6b, g6c(0), g6c(1), g7fg, g9prep, g8a, g8b, g9, g9b,
                    g10a, proj("vtm", a_wv, 0), proj("vtm", a_wv, 1), g10b]

        ys = {"state_dve": 0, "post_dve": 0}

        def emit_y(s, inject):
            S = XS[s % 2]
            for c4 in range(2):
                pi = c4
                rs = slice(pi * 64, pi * 64 + 64)
                P.wait(t, "dve", ys["state_dve"])
                P.wait(t, "dve", ys["post_dve"])
                for h in range(H):
                    tc = slice(h * SEG + c4 * 64, h * SEG + c4 * 64 + 64)
                    pat = P.op(t, "matmul", p_a[rs, h * 64:h * 64 + 64],
                               S["kt"][:, tc], S["qt"][:, tc], start=True, stop=True,
                               tile_position=(0, pi * 64),
                               inc="pe" if h == 7 else None)
                P.wait(v, "pe", pat)
                atm_dve = P.op(v, "tensor_tensor", a_atm[rs, :], p_a[rs, :],
                               triI8[rs, :], ALU.mult, inc="dve")
                inject()
                P.wait(t, "dve", atm_dve)
                for h in range(H):
                    tc = slice(h * SEG + c4 * 64, h * SEG + c4 * 64 + 64)
                    oc = slice(h * 64, h * 64 + 64)
                    P.op(t, "matmul", p_ok[rs, oc], a_atm[rs, oc],
                         S["stl"][rs, h * 64:(h + 1) * 64],
                         start=True, stop=False, tile_position=(pi * 64, pi * 64))
                    pok = P.op(t, "matmul", p_ok[rs, oc], S["qt"][:, tc],
                               a_hk[:, oc], start=False, stop=True,
                               tile_position=(0, pi * 64),
                               inc="pe" if h == 7 else None)
                inject()
                for h in range(H):
                    P.op(t, "matmul", p_st[:, h * 64:h * 64 + 64],
                         S["ktm"][rs, h * 128:(h + 1) * 128],
                         S["stb"][rs, h * 64:(h + 1) * 64],
                         start=True, stop=True, tile_position=(pi * 64, 0),
                         inc="pe" if h == 7 else None)
                inject()
                P.wait(v, "pe", pok)
                okb = a_otmp[rs, 0:512]
                qvt = a_otmp[rs, 512:1024]
                okb3 = okb.rearrange("p (h m) -> p h m", m=64)
                qvt3 = qvt.rearrange("p (h m) -> p h m", m=64)
                okb_dve = P.op(v, "tensor_tensor", okb, p_ok[rs, :], S["B"][rs, :],
                               ALU.mult, inc="dve")
                P.op(v, "tensor_reduce", a_nmx[rs, :], okb3, axis=AX.X, op=ALU.max,
                     negate=True)
                sm_dve = P.op(v, "tensor_tensor", qvt3, okb3,
                              a_nmx[rs, :].unsqueeze(2).broadcast_to([64, 8, 64]),
                              ALU.add, inc="dve")
                P.wait(a, "dve", sm_dve)
                sm_act = P.op(a, "activation", okb, qvt, AF.Exp, inc="act")
                P.wait(v, "act", sm_act)
                P.op(v, "tensor_reduce", a_ssum[rs, :], okb3, axis=AX.X, op=ALU.add)
                P.op(v, "reciprocal", a_rsum[rs, :], a_ssum[rs, :])
                P.op(v, "tensor_tensor", qvt3, okb3,
                     a_rsum[rs, :].unsqueeze(2).broadcast_to([64, 8, 64]), ALU.mult)
                qb_dve = P.op(v, "tensor_tensor", a_qb[rs, :], qvt, S["B"][rs, :],
                              ALU.mult, inc="dve")
                P.wait(t, "dve", okb_dve)
                for h in range(H):
                    sbc = S["stb"][rs, h * 64:(h + 1) * 64]
                    vv = S["vtm"][rs, h * 128:(h + 1) * 128]
                    bank2 = p_st2 if h < 4 else p_ok
                    for base in (0, 64):
                        pov_st = P.op(t, "matmul",
                                      bank2[base:base + 64,
                                            (h % 4) * 128:(h % 4 + 1) * 128],
                                      sbc, vv, start=True, stop=True,
                                      tile_position=(pi * 64, base),
                                      inc="pe" if (h == 7 and base == 64) else None)
                inject()
                P.wait(t, "dve", qb_dve)
                for h in range(H):
                    oc = h * 64
                    P.op(t, "transpose", p_tp[rs, oc:oc + 64],
                         a_qb[rs, h * 64:h * 64 + 64], id64[rs, :],
                         tile_position=(pi * 64, pi * 64))
                    ptr = P.op(t, "transpose", p_tp[rs, 512 + oc:512 + oc + 64],
                               S["stl"][rs, h * 64:h * 64 + 64], id64[rs, :],
                               tile_position=(pi * 64, pi * 64),
                               inc="pe" if h == 7 else None)
                P.wait(v, "pe", ptr)
                P.op(v, "tensor_copy", a_qbt[rs, :], p_tp[rs, 0:512])
                tt_dve = P.op(v, "tensor_copy", a_stT[rs, :], p_tp[rs, 512:1024],
                              inc="dve")
                P.wait(t, "dve", tt_dve)
                for h in range(H):
                    oc = h * 64
                    pat2 = P.op(t, "matmul", p_a[rs, h * 64:h * 64 + 64],
                                a_stT[rs, oc:oc + 64], a_qbt[rs, oc:oc + 64],
                                start=True, stop=True,
                                tile_position=(pi * 64, pi * 64),
                                inc="pe" if h == 7 else None)
                P.wait(v, "pe", pat2)
                atm2_dve = P.op(v, "tensor_tensor", a_atm[rs, :], p_a[rs, :],
                                triI8[rs, :], ALU.mult, inc="dve")
                inject()
                P.wait(t, "dve", atm2_dve)
                for h in range(H):
                    oc = h * 64
                    vv = S["vtm"][rs, h * 128:(h + 1) * 128]
                    P.op(t, "matmul", p_ov[rs, h * 128:(h + 1) * 128],
                         a_atm[rs, h * 64:h * 64 + 64], vv,
                         start=True, stop=False, tile_position=(pi * 64, pi * 64))
                    ov_pe = P.op(t, "matmul", p_ov[rs, h * 128:(h + 1) * 128],
                                 a_qbt[rs, oc:oc + 64],
                                 a_hv[rs, h * 128:(h + 1) * 128],
                                 start=False, stop=True,
                                 tile_position=(pi * 64, pi * 64),
                                 inc="pe" if h == 7 else None)
                inject()
                P.wait(v, "pe", ov_pe)
                P.wait(v, "act", XS_bca_act[s % 2])
                P.op(v, "tensor_tensor", a_hkt, a_hk,
                     S["bca"][:, c4 * HM:(c4 + 1) * HM], ALU.mult)
                P.op(v, "tensor_tensor", a_hk, a_hkt, p_st[:], ALU.add)
                P.wait(v, "act", XS_bl_act[s % 2])
                for h in range(H):
                    bank2 = p_st2 if h < 4 else p_ok
                    sd = P.op(v, "scalar_tensor_tensor",
                              a_hv[:, h * 128:(h + 1) * 128],
                              a_hv[:, h * 128:(h + 1) * 128],
                              S["bl"][:, c4 * 8 + h:c4 * 8 + h + 1],
                              bank2[:, (h % 4) * 128:(h % 4 + 1) * 128],
                              ALU.mult, ALU.add, inc="dve" if h == 7 else None)
                ys["state_dve"] = sd

            P.wait(v, "gch", XS_sg_gch[s % 2])
            P.op(v, "tensor_tensor", a_otmp, p_ov[:], S["sg"], ALU.mult)
            P.op(v, "tensor_tensor", a_o2, a_otmp, a_otmp, ALU.mult)
            rm_dve = P.op(v, "tensor_reduce", a_sm1[:, 8:16],
                          a_o2.rearrange("p (h d) -> p h d", d=128), axis=AX.X,
                          op=ALU.add, inc="dve")
            P.wait(a, "dve", rm_dve)
            P.op(a, "activation", a_sm1[:, 0:8], a_sm1[:, 8:16], AF.Ln,
                 bias=c_eps[:], scale=1.0 / 128)
            rm_act = P.op(a, "activation", a_sm1[:, 8:16], a_sm1[:, 0:8], AF.Exp,
                          scale=-0.5, inc="act")
            P.wait(v, "act", rm_act)
            for h in range(H):
                o2_dve = P.op(v, "tensor_scalar", a_o2[:, h * 128:(h + 1) * 128],
                              a_otmp[:, h * 128:(h + 1) * 128],
                              a_sm1[:, 8 + h:9 + h], None, ALU.mult,
                              inc="dve" if h == 7 else None)
            inject(3)
            P.wait(t, "dve", o2_dve)
            for dt in range(8):
                po2 = P.op(t, "transpose", p_tp[:, dt * 128:(dt + 1) * 128],
                           a_o2[:, dt * 128:(dt + 1) * 128], id128[:],
                           inc="pe" if dt == 7 else None)
            P.wait(v, "pe", po2)
            o2t_dve = P.op(v, "tensor_copy", a_o2t, p_tp[:, 0:1024], inc="dve")
            P.wait(t, "dve", o2t_dve)
            for dt in range(8):
                for dh in range(2):
                    pwo = P.op(t, "matmul", p_ov[:, dh * 512:dh * 512 + 512],
                               a_o2t[:, dt * 128:(dt + 1) * 128],
                               a_wo[:, dt * D + dh * 512: dt * D + dh * 512 + 512],
                               start=(dt == 0), stop=(dt == 7),
                               inc="pe" if (dt == 7 and dh == 1) else None)
            inject(3)
            P.wait(v, "pe", pwo)
            ys["post_dve"] = P.op(v, "tensor_tensor", x_buf[:, s * D:(s + 1) * D],
                                  x_buf[:, s * D:(s + 1) * D], p_ov[:], ALU.add,
                                  inc="dve")

        for gr in make_x_groups(0):
            gr()
        pend = []

        def inject(k=1):
            for _ in range(k):
                if pend:
                    pend.pop(0)()
        for s in range(NSEG):
            if s + 1 < NSEG:
                pend = make_x_groups(s + 1)
                pend.pop(0)()
            else:
                pend = []
            emit_y(s, inject)
            while pend:
                pend.pop(0)()

    def emit_ffn(b):
        v, t, a, g, sy = "vector", "tensor", "scalar", "gpsimd", "sync"
        P.bar(g, "pe", "dve", "act")
        P.bar(v, "pe", "act")
        P.bar(a, "pe", "dve")
        P.bar(t, "dve", "act")
        P.op(g, "dma_start", f_b1n[:], b1_d, inc="gs")
        P.op(g, "dma_start", f_b2a[:], b2_d, inc="gs")
        gsw = P.op(g, "dma_start", f_wr[:].rearrange("p (dt e) -> p dt e", e=E),
                   wr_d.rearrange("(dt p) e -> p dt e", p=128), inc="gs")
        P.wait(t, "gs", gsw)
        P.bar(t, "dve", "act")
        for fb in range(32):
            pb1 = P.op(t, "transpose", p_tp[0:128, fb * 8:fb * 8 + 8],
                       f_b1n[0:8, fb * 128:(fb + 1) * 128], id128[0:8, 0:8],
                       inc="pe" if fb == 31 else None)
        P.wait(v, "pe", pb1)
        b1t_dve = P.op(v, "tensor_copy", f_b1t[:], p_tp[:, 0:256], inc="dve")

        xnts = (f_xnt0, f_xnt1)
        wgs = (f_wg0, f_wg1)
        w1bufs = (f_w1a, f_w1b)
        w2bufs = (f_w2a, f_w2b, f_w2c, f_w2d)
        mm1banks = ((p_ov[:, 0:512], p_ov[:, 512:1024]), (p_st[:], p_st2[:]))
        accb = (p_a[:, 0:512], p_ok[:, 0:512], p_st[:, 0:512], p_st2[:, 0:512],
                p_ov[:, 0:512], p_ov[:, 512:1024], p_x[:, 0:512], p_tp_f[:, 0:512])
        bank_free = {i: ("dve", b1t_dve if i == 7 else 0) for i in range(8)}

        def ln2_router(grp):
            for f in ln2_parts(grp):
                f()

        def ln2_parts(grp):
            gx = xnts[grp % 2]
            gw = wgs[grp % 2]
            g0 = grp * (GRP // 128)
            l4 = [0]
            l5 = [0]
            rstd_act = [0]

            def stats_all():
                for tl in range(8):
                    xt = x_buf[:, (g0 + tl) * D:(g0 + tl + 1) * D]
                    P.wait(v, "pe", l4[0])
                    P.op(v, "tensor_reduce", f_sm[:, 0:1], xt, axis=AX.X, op=ALU.add)
                    P.op(v, "tensor_scalar", f_rt[:, tl:tl + 1], f_sm[:, 0:1],
                         -1.0 / D, None, ALU.mult)
                    P.op(v, "tensor_tensor", f_xns, xt, xt, ALU.mult)
                    P.op(v, "tensor_reduce", f_rt[:, 8 + tl:9 + tl], f_xns,
                         axis=AX.X, op=ALU.add)
                P.op(v, "tensor_scalar", f_rt[:, 8:16], f_rt[:, 8:16], 1.0 / D, None,
                     ALU.mult)
                P.op(v, "tensor_tensor", f_rt[:, 16:24], f_rt[:, 0:8], f_rt[:, 0:8],
                     ALU.mult)
                st = P.op(v, "tensor_sub", f_rt[:, 8:16], f_rt[:, 8:16],
                          f_rt[:, 16:24], inc="dve")
                P.wait(a, "dve", st)
                P.op(a, "activation", f_rt[:, 16:24], f_rt[:, 8:16], AF.Ln,
                     bias=c_eps[:])
                rstd_act[0] = P.op(a, "activation", f_rt[:, 16:24], f_rt[:, 16:24],
                                   AF.Exp, scale=-0.5, inc="act")

            def tile(tl):
                def f():
                    xt = x_buf[:, (g0 + tl) * D:(g0 + tl + 1) * D]
                    P.wait(v, "act", rstd_act[0])
                    P.wait(v, "pe", l4[0])
                    l3 = P.op(v, "tensor_scalar", f_xns, xt, f_rt[:, tl:tl + 1],
                              f_rt[:, 16 + tl:17 + tl], ALU.add, op1=ALU.mult,
                              inc="dve")
                    P.wait(t, "dve", l3)
                    P.wait(t, bank_free[7][0], bank_free[7][1])
                    for dt in range(8):
                        l4[0] = P.op(t, "transpose", p_tp[:, dt * 128:(dt + 1) * 128],
                                     f_xns[:, dt * 128:(dt + 1) * 128], id128[:],
                                     inc="pe" if dt == 7 else None)
                    P.wait(a, "pe", l4[0])
                    l5[0] = P.op(a, "activation",
                                 gx[:, 0:8192].rearrange("p (dt tk) -> p dt tk",
                                                         tk=GRP)
                                 [:, :, tl * 128:tl * 128 + 128],
                                 p_tp[:, 0:1024].rearrange("p (dt tk) -> p dt tk",
                                                           tk=128),
                                 AF.Copy, inc="act")
                    bank_free[7] = ("act", l5[0])
                return f

            def router():
                P.wait(t, "act", l5[0])
                P.wait(t, bank_free[6][0], bank_free[6][1])
                for tl in range(8):
                    for dt in range(8):
                        r1 = P.op(t, "matmul", p_x[:, tl * 8:tl * 8 + 8],
                                  gx[:, dt * GRP + tl * 128:
                                     dt * GRP + tl * 128 + 128],
                                  f_wr[:, dt * 8:(dt + 1) * 8],
                                  start=(dt == 0), stop=(dt == 7),
                                  inc="pe" if (tl == 7 and dt == 7) else None)
                P.wait(v, "pe", r1)
                L3 = p_x[:, 0:64].rearrange("p (tt e) -> p tt e", e=8)
                P.op(v, "tensor_reduce", f_sm[:, 8:16], L3, axis=AX.X, op=ALU.max,
                     negate=True)
                r2 = P.op(v, "tensor_tensor",
                          f_rt[:, 0:64].rearrange("p (tt e) -> p tt e", e=8), L3,
                          f_sm[:, 8:16].unsqueeze(2).broadcast_to([128, 8, 8]),
                          ALU.add, inc="dve")
                bank_free[6] = ("dve", r2)
                P.wait(a, "dve", r2)
                r3 = P.op(a, "activation", f_rt[:, 64:128], f_rt[:, 0:64], AF.Exp,
                          inc="act")
                P.wait(v, "act", r3)
                P.op(v, "tensor_reduce", f_sm[:, 8:16],
                     f_rt[:, 64:128].rearrange("p (tt e) -> p tt e", e=8), axis=AX.X,
                     op=ALU.add)
                P.op(v, "reciprocal", f_sm[:, 0:8], f_sm[:, 8:16])
                P.op(v, "tensor_scalar", f_rt[:, 64:128], f_rt[:, 0:64], 0.0, None,
                     ALU.is_equal)
                P.op(v, "tensor_tensor", gw[:].rearrange("p (tt e) -> p tt e", e=8),
                     f_rt[:, 64:128].rearrange("p (tt e) -> p tt e", e=8),
                     f_sm[:, 0:8].unsqueeze(2).broadcast_to([128, 8, 8]), ALU.mult,
                     inc="dve")

            return [stats_all] + [tile(i) for i in range(8)] + [router]

        pe_after_w1use = {}
        pe_after_w2use = {}
        gelu_after = {}

        ln2_router(0)
        ln2_pend = []
        for grp in range(NGRP):
            if grp + 1 < NGRP:
                ln2_pend = ln2_parts(grp + 1)
            g0 = grp * (GRP // 128)
            gx = xnts[grp % 2]
            gw = wgs[grp % 2]
            for e in range(E):
                eg = (b * NGRP + grp) * E + e
                w1row = w1sc[e * D:(e + 1) * D, :]
                glast = None
                for fbg in range(8):
                    gidx = eg * 8 + fbg
                    par = gidx % 2
                    psem = "w1%d" % par
                    if gidx >= 2:
                        P.wait(g, "pe", pe_after_w1use.get(gidx - 2))
                    ld = P.op(g, "dma_start",
                              w1bufs[par][:].rearrange("p (dt f) -> p dt f", f=512),
                              w1row[:, fbg * 512:(fbg + 1) * 512].rearrange(
                                  "(dt p) f -> p dt f", p=128), inc=psem)
                    P.wait(t, psem, ld)
                    if fbg == 0:
                        for i in (2, 3, 4, 5):
                            P.wait(t, bank_free[i][0], bank_free[i][1])
                        P.wait(t, "dve", b1t_dve)
                    for fi in range(4):
                        fb = fbg * 4 + fi
                        fgi = eg * 32 + fb
                        bsel = mm1banks[fi % 2]
                        if gelu_after.get(fgi - 2) is not None:
                            P.wait(t, "act", gelu_after[fgi - 2])
                        for dt in range(8):
                            for half in range(2):
                                m1 = P.op(t, "matmul", bsel[half],
                                          w1bufs[par][:, dt * 512 + fi * 128:
                                                       dt * 512 + fi * 128 + 128],
                                          gx[:, dt * GRP + half * 512:
                                             dt * GRP + half * 512 + 512],
                                          start=(dt == 0), stop=(dt == 7),
                                          inc="pe" if (dt == 7 and half == 1)
                                          else None)
                        P.wait(a, "pe", m1)
                        for half in range(2):
                            glast = P.op(a, "activation",
                                         f_ht[:, fb * GRP + half * 512:
                                              fb * GRP + half * 512 + 512],
                                         bsel[half], AF.Gelu,
                                         bias=f_b1t[:, fb * 8 + e:fb * 8 + e + 1],
                                         inc="act" if half == 1 else None)
                        gelu_after[fgi] = P.cnt["act"]
                    pe_after_w1use[gidx] = P.cnt["pe"]
                    if ln2_pend:
                        ln2_pend.pop(0)()
                P.wait(g, "gs", gsw)
                P.wait(g, "pe", pe_after_b2.get(eg - 1))
                b2done = P.op(g, "dma_start", f_b2s[:], f_b2a[e:e + 1, :], inc="gs")
                w2row = w2sc[e * F:(e + 1) * F, :]
                P.wait(t, "gs", b2done)
                for h5 in range(2):
                    gl = gelu_after[eg * 32 + 31]
                    for fb in range(32):
                        gidx2 = (eg * 2 + h5) * 32 + fb
                        par = fb % 4
                        psem = "w2%d" % par
                        if gidx2 >= 4:
                            P.wait(g, "pe", pe_after_w2use.get(gidx2 - 4))
                        ld = P.op(g, "dma_start", w2bufs[par][:],
                                  w2row[fb * 128:(fb + 1) * 128, :], inc=psem)
                        P.wait(t, psem, ld)
                        if fb == 0:
                            P.wait(t, "act", gl)
                        for j, i in enumerate((4, 5, 2, 3, 0, 1, 6, 7)):
                            tl, dh = i // 2, i % 2
                            if fb == 0:
                                P.wait(t, bank_free[i][0], bank_free[i][1])
                            m2 = P.op(t, "matmul", accb[i],
                                      f_ht[:, fb * GRP + h5 * 512 + tl * 128:
                                           fb * GRP + h5 * 512 + tl * 128 + 128],
                                      w2bufs[par][:, dh * 512:dh * 512 + 512],
                                      start=(fb == 0), stop=False,
                                      inc="pe" if j == 7 else None)
                        pe_after_w2use[gidx2] = P.cnt["pe"]
                    for tl in range(4):
                        for dh in range(2):
                            m3 = P.op(t, "matmul", accb[tl * 2 + dh], onesrow[:],
                                      f_b2s[0:1, dh * 512:dh * 512 + 512],
                                      start=False, stop=True, tile_position=(0, 0),
                                      inc="pe" if (tl == 3 and dh == 1) else None)
                    pe_after_b2[eg] = P.cnt["pe"]
                    P.wait(v, "pe", m3)
                    for j, i in enumerate((4, 5, 2, 3, 0, 1, 6, 7)):
                        tl, dh = i // 2, i % 2
                        TT = g0 + h5 * 4 + tl
                        xs2 = x_buf[:, TT * D + dh * 512: TT * D + dh * 512 + 512]
                        ev = P.op(v, "scalar_tensor_tensor", xs2, accb[i],
                                  gw[:, (h5 * 4 + tl) * 8 + e:
                                     (h5 * 4 + tl) * 8 + e + 1],
                                  xs2, ALU.mult, ALU.add, inc="dve")
                        bank_free[i] = ("dve", ev)
            P.bar(sy, "dve")
            ydst = y_d[b * N + grp * GRP: b * N + (grp + 1) * GRP, :].rearrange(
                "(tt p) d -> p tt d", p=128)
            P.op(sy, "dma_start", ydst,
                 x_buf[:, g0 * D:(g0 + 8) * D].rearrange("p (tt d) -> p tt d", d=D),
                 inc="yst")

    emit_consts()
    for b in range(BPC):
        emit_attn_weight_loads(b)
        P.bar("sync", "yst")
        P.op("sync", "dma_start", x_buf[:].rearrange("p (tt d) -> p tt d", d=D),
             x_d[b * N:(b + 1) * N, :].rearrange("(tt p) d -> p tt d", p=128), inc="xin")
        if b == 0 and DO_FFN:
            emit_conversion()
        if DO_ATTN:
            emit_attention(b)
        if DO_FFN:
            P.wait("gpsimd", "cvo0", P.cnt["cvo0"])
            emit_ffn(b)
        else:
            P.bar("sync", "dve")
            ydst = y_d[b * N:(b + 1) * N, :].rearrange("(tt p) d -> p tt d", p=128)
            P.op("sync", "dma_start", ydst, x_buf[:].rearrange("p (tt d) -> p tt d", d=D),
                 inc="yst")
    P.wait("sync", "yst", P.cnt["yst"])

    with es, nc.Block() as block:
        for eng in Prog.ENGINES:
            stream = P.streams[eng]

            def make(fns):
                def _(e):
                    for f in fns:
                        f(e)
                return _
            getattr(block, eng)(make(stream))

    return nc


_NC_CACHE = None


def kernel(**inputs):
    global _NC_CACHE
    xs = np.ascontiguousarray(np.asarray(inputs["L0_attn_x"], dtype=np.float32))
    w = {
        "wq": inputs["L0_attn_Wq"], "wk": inputs["L0_attn_Wk"], "wv": inputs["L0_attn_Wv"],
        "wf1": inputs["L0_attn_Wf1"], "wf2": inputs["L0_attn_Wf2"],
        "wg1": inputs["L0_attn_Wg1"], "wg2": inputs["L0_attn_Wg2"],
        "wo": inputs["L0_attn_Wo"], "normw": inputs["L0_attn_norm_w"],
        "wr": inputs["L0_ffn_Wr"], "w1": inputs["L0_ffn_W1"], "b1": inputs["L0_ffn_b1"],
        "w2": inputs["L0_ffn_W2"], "b2": inputs["L0_ffn_b2"],
    }
    w = {k: np.ascontiguousarray(np.asarray(v, dtype=np.float32)) for k, v in w.items()}
    if _NC_CACHE is None:
        _NC_CACHE = build_nc()
    nc = _NC_CACHE
    in_maps = []
    for c in range(NCORES):
        m = {"x": np.ascontiguousarray(xs[c * BPC:(c + 1) * BPC].reshape(BPC * N, D))}
        m.update(w)
        in_maps.append(m)
    res = run_bass_kernel_spmd(nc, in_maps, core_ids=list(range(NCORES)))
    out = np.concatenate([r["y"].reshape(BPC, N, D) for r in res.results], axis=0)
    return out


# version 23 (final)
# speedup vs baseline: 1.0323x; 1.0046x over previous
import contextlib
import numpy as np
import concourse.bass as bass
import concourse.mybir as mybir
from concourse.bass_utils import run_bass_kernel_spmd

F32 = mybir.dt.float32
BF16 = mybir.dt.bfloat16
AF = mybir.ActivationFunctionType
ALU = mybir.AluOpType
AX = mybir.AxisListType

C0 = -0.6931471805599453
NCORES = 8
BPC = 2
N = 2048
D = 1024
H = 8
M = 64
DK = 128
HM = H * M
E = 8
F = 4096
SEG = 128
NSEG = N // SEG
GRP = 1024
NGRP = N // GRP
EPS = 1e-5

DO_FFN = True
DO_ATTN = True
N_SEG_EMIT = 99
PHASE_LIMIT = 99
SCAN_OPS = 99


class Prog:
    ENGINES = ("tensor", "vector", "scalar", "sync", "gpsimd")

    def __init__(self, sems):
        self.streams = {e: [] for e in self.ENGINES}
        self.sems = sems
        self.cnt = {k: 0 for k in sems}

    CHAIN = {"vector": "dve", "scalar": "act", "gpsimd": "gch"}

    def op(self, eng, fname, *args, inc=None, **kw):
        is_dma = "dma" in fname
        chain = None if is_dma else self.CHAIN.get(eng)
        if chain is not None:
            self.wait(eng, chain, self.cnt[chain])
            assert inc is None or inc == chain
            inc = chain
        h = self.sems[inc] if inc else None
        amt = 16 if is_dma else 1
        if inc:
            self.cnt[inc] += amt

        def f(e, fname=fname, args=args, kw=kw, h=h, amt=amt):
            ins = getattr(e, fname)(*args, **kw)
            if h is not None:
                ins.then_inc(h, amt)
        self.streams[eng].append(f)
        return self.cnt[inc] if inc else None

    def wait(self, eng, sem, val):
        if val is None or val <= 0:
            return
        h = self.sems[sem]

        def f(e, h=h, val=val):
            e.wait_ge(h, val)
        self.streams[eng].append(f)

    def bar(self, eng, *sem_names):
        for s in sem_names:
            self.wait(eng, s, self.cnt[s])


def build_nc():
    nc = bass.Bass("TRN2", target_bir_lowering=False)

    x_d = nc.dram_tensor("x", [BPC * N, D], F32, kind="ExternalInput").ap()
    wq_d = nc.dram_tensor("wq", [D, D], F32, kind="ExternalInput").ap()
    wk_d = nc.dram_tensor("wk", [D, D], F32, kind="ExternalInput").ap()
    wv_d = nc.dram_tensor("wv", [D, D], F32, kind="ExternalInput").ap()
    wf1_d = nc.dram_tensor("wf1", [D, DK], F32, kind="ExternalInput").ap()
    wf2_d = nc.dram_tensor("wf2", [DK, HM], F32, kind="ExternalInput").ap()
    wg1_d = nc.dram_tensor("wg1", [D, DK], F32, kind="ExternalInput").ap()
    wg2_d = nc.dram_tensor("wg2", [DK, D], F32, kind="ExternalInput").ap()
    wo_d = nc.dram_tensor("wo", [D, D], F32, kind="ExternalInput").ap()
    nw_d = nc.dram_tensor("normw", [D], F32, kind="ExternalInput").ap()
    wr_d = nc.dram_tensor("wr", [D, E], F32, kind="ExternalInput").ap()
    w1_d = nc.dram_tensor("w1", [E, D, F], F32, kind="ExternalInput").ap()
    b1_d = nc.dram_tensor("b1", [E, F], F32, kind="ExternalInput").ap()
    w2_d = nc.dram_tensor("w2", [E, F, D], F32, kind="ExternalInput").ap()
    b2_d = nc.dram_tensor("b2", [E, D], F32, kind="ExternalInput").ap()
    y_d = nc.dram_tensor("y", [BPC * N, D], F32, kind="ExternalOutput").ap()
    w1sc = nc.dram_tensor("w1sc", [E * D, F], BF16).ap()
    w2sc = nc.dram_tensor("w2sc", [E * F, D], BF16).ap()

    es = contextlib.ExitStack()
    sb = lambda name, shape, dt: es.enter_context(nc.sbuf_tensor(name, shape, dt))
    ps = lambda name, shape, dt: es.enter_context(nc.psum_tensor(name, shape, dt))

    x_buf = sb("x_buf", [128, (N // 128) * D], F32)
    ARENA_F32 = 36000
    arena = sb("arena", [128, ARENA_F32], F32)
    ar_f = arena[:]
    ar_b = ar_f.bitcast(BF16)

    class Alloc:
        def __init__(self):
            self.off = 0

        def f32(self, rows, cols):
            o = self.off
            self.off += ((cols * 4 + 127) // 128) * 128
            assert self.off <= ARENA_F32 * 4, f"arena overflow {self.off}"
            return ar_f[0:rows, o // 4: o // 4 + cols]

        def bf(self, rows, cols):
            o = self.off
            self.off += ((cols * 2 + 127) // 128) * 128
            assert self.off <= ARENA_F32 * 4, f"arena overflow {self.off}"
            return ar_b[0:rows, o // 2: o // 2 + cols]

    aa = Alloc()
    a_wq = aa.bf(128, 8 * D)
    a_wk = aa.bf(128, 8 * D)
    a_wv = aa.bf(128, 8 * D)
    a_wo = aa.bf(128, 8 * D)
    a_wf1 = aa.bf(128, 8 * DK)
    a_wf2 = aa.bf(128, HM)
    a_wg1 = aa.bf(128, 8 * DK)
    a_wg2 = aa.bf(128, D)
    a_nw = aa.f32(128, 8)
    XS = []
    for par in range(2):
        XS.append(dict(
            xnt=aa.bf(128, 8 * SEG),
            qt=aa.bf(128, H * SEG),
            kt=aa.bf(128, H * SEG),
            vtm=aa.bf(128, H * DK),
            ktm=aa.bf(128, H * DK),
            sig=aa.bf(128, HM),
            dlt=aa.bf(128, HM),
            stl=aa.bf(128, HM),
            stb=aa.bf(128, HM),
            B=aa.bf(128, HM),
            sg=aa.bf(128, D),
            bca=aa.bf(128, 2 * HM),
            bl=aa.f32(128, 16),
        ))
    a_qraw = aa.bf(128, D)
    a_xact = aa.bf(128, D)
    a_ut = aa.bf(128, DK)
    a_ugt = aa.bf(128, DK)
    a_xc = aa.f32(128, D)
    a_xsq = aa.bf(128, D)
    a_xns = aa.bf(128, D)
    a_xsm = aa.f32(128, 8)
    a_otmp = aa.f32(128, D)
    a_o2 = aa.bf(128, D)
    a_o2t = aa.bf(128, D)
    a_atm = aa.bf(128, HM)
    a_qb = aa.bf(128, HM)
    a_qbt = aa.bf(128, 8 * M)
    a_stT = aa.bf(128, 8 * M)
    a_hk = aa.bf(128, HM)
    a_hkt = aa.bf(128, HM)
    a_hv = aa.bf(128, 8 * DK)
    a_sm1 = aa.f32(128, 16)
    a_nmx = aa.f32(128, 8)
    a_ssum = aa.f32(128, 8)
    a_rsum = aa.f32(128, 8)
    attn_bytes = aa.off

    fa = Alloc()
    f_xnt0 = fa.bf(128, 8 * GRP)
    f_xnt1 = fa.bf(128, 8 * GRP)
    f_ht = fa.bf(128, 32 * GRP)
    f_w1a = fa.bf(128, 8 * 512)
    f_w1b = fa.bf(128, 8 * 512)
    f_w2a = fa.bf(128, D)
    f_w2b = fa.bf(128, D)
    f_w2c = fa.bf(128, D)
    f_w2d = fa.bf(128, D)
    f_b1n = fa.bf(8, F)
    f_b1t = fa.f32(128, 32 * E)
    f_b2a = fa.bf(8, D)
    f_b2s = fa.bf(1, D)
    f_wr = fa.bf(128, 8 * E)
    f_wg0 = fa.f32(128, 8 * E)
    f_wg1 = fa.f32(128, 8 * E)
    f_rt = fa.f32(128, 128)
    f_xns = fa.bf(128, D)
    f_lnc = fa.f32(128, D)
    f_sm = fa.f32(128, 16)
    ffn_bytes = fa.off
    assert max(attn_bytes, ffn_bytes) <= ARENA_F32 * 4, (attn_bytes, ffn_bytes)

    triI = sb("triI", [128, 64], BF16)
    triL = sb("triL", [128, 64], BF16)
    triI8 = sb("triI8", [128, 512], BF16)
    id64 = sb("id64", [128, 64], BF16)
    id128 = sb("id128", [128, 128], BF16)
    ones128 = sb("ones128", [128, 128], BF16)
    onescol = sb("onescol", [128, 1], BF16)
    onesrow = sb("onesrow", [1, 128], BF16)
    c_lin = sb("c_lin", [128, 1], F32)
    c_nlin = sb("c_nlin", [128, 1], F32)
    c_lrev = sb("c_lrev", [128, 1], F32)
    c_c64 = sb("c_c64", [128, 1], F32)
    c_eps = sb("c_eps", [128, 1], F32)
    c_it = sb("c_it", [128, 1], F32)
    c_it2 = sb("c_it2", [128, 1], F32)

    p_a = ps("p_a", [128, 512], F32)
    p_ok = ps("p_ok", [128, 512], F32)
    p_st = ps("p_st", [128, 512], F32)
    p_st2 = ps("p_st2", [128, 512], F32)
    p_ov = ps("p_ov", [128, 1024], F32)
    p_x = ps("p_x", [128, 512], F32)
    p_tp = ps("p_tp", [128, 1024], BF16)
    p_x_bf = p_x[:].bitcast(BF16)
    p_tp_f = p_tp[:].bitcast(F32)

    sems = {}
    for s in ("pe", "dve", "act", "gs", "gch", "xin", "win", "cvi0", "cvi1",
              "cvo0", "cvo1", "w10", "w11", "w20", "w21", "w22", "w23", "yst"):
        sems[s] = es.enter_context(nc.semaphore("s_" + s))
    P = Prog(sems)
    pe_after_b2 = {}

    def emit_consts():
        g = "gpsimd"
        P.op(g, "memset", triI[:], 1.0)
        P.op(g, "memset", triL[:], 1.0)
        P.op(g, "memset", triI8[:], 1.0)
        P.op(g, "memset", id64[:], 0.0)
        P.op(g, "memset", id128[:], 0.0)
        P.op(g, "memset", ones128[:], 1.0)
        P.op(g, "memset", onescol[:], 1.0)
        P.op(g, "memset", onesrow[:], 1.0)
        P.op(g, "memset", c_c64[:], 64.0 * C0)
        P.op(g, "memset", c_eps[:], EPS)
        for base in (0, 64):
            P.op(g, "affine_select", out=triI[base:base + 64, :], in_=triI[base:base + 64, :],
                 compare_op=ALU.is_ge, fill=0.0, base=0, pattern=[[1, 64]], channel_multiplier=-1)
            P.op(g, "affine_select", out=triL[base:base + 64, :], in_=triL[base:base + 64, :],
                 compare_op=ALU.is_gt, fill=0.0, base=0, pattern=[[-1, 64]], channel_multiplier=1)
            P.op(g, "affine_select", out=id64[base:base + 64, :], in_=id64[base:base + 64, :],
                 compare_op=ALU.not_equal, fill=1.0, base=0, pattern=[[-1, 64]], channel_multiplier=1)
            for hh in range(8):
                P.op(g, "affine_select", out=triI8[base:base + 64, hh * 64:hh * 64 + 64],
                     in_=triI8[base:base + 64, hh * 64:hh * 64 + 64],
                     compare_op=ALU.is_ge, fill=0.0, base=0, pattern=[[1, 64]], channel_multiplier=-1)
        P.op(g, "affine_select", out=id128[:], in_=id128[:], compare_op=ALU.not_equal,
             fill=1.0, base=0, pattern=[[-1, 128]], channel_multiplier=1)
        P.op(g, "iota", c_it[:], pattern=[[0, 1]], base=0, channel_multiplier=1,
             allow_small_or_imprecise_dtypes=True)
        v = "vector"
        P.bar(v, "gch")
        P.op(v, "tensor_scalar", c_it2[:], c_it[:], 64.0, None, ALU.is_ge)
        P.op(v, "scalar_tensor_tensor", c_it[:], c_it2[:], -64.0, c_it[:], ALU.mult, ALU.add)
        P.op(v, "tensor_scalar", c_lin[:], c_it[:], C0, C0, ALU.mult, op1=ALU.add)
        P.op(v, "tensor_scalar", c_nlin[:], c_lin[:], -1.0, None, ALU.mult)
        P.op(v, "tensor_scalar", c_lrev[:], c_it[:], -C0, 63.0 * C0, ALU.mult, op1=ALU.add,
             inc="dve")

    def emit_attn_weight_loads(b):
        g = "gpsimd"
        P.bar(g, "pe", "dve", "act", "yst")
        for dst, src in ((a_wq, wq_d), (a_wk, wk_d), (a_wv, wv_d), (a_wo, wo_d)):
            P.op(g, "dma_start", dst.rearrange("p (dt n) -> p dt n", n=D),
                 src.rearrange("(dt p) n -> p dt n", p=128), inc="win")
        P.op(g, "dma_start", a_wf1.rearrange("p (dt u) -> p dt u", u=DK),
             wf1_d.rearrange("(dt p) u -> p dt u", p=128), inc="win")
        P.op(g, "dma_start", a_wf2, wf2_d, inc="win")
        P.op(g, "dma_start", a_wg1.rearrange("p (dt u) -> p dt u", u=DK),
             wg1_d.rearrange("(dt p) u -> p dt u", p=128), inc="win")
        P.op(g, "dma_start", a_wg2, wg2_d, inc="win")
        win_done = P.op(g, "dma_start", a_nw, nw_d.rearrange("(dt p) -> p dt", p=128),
                        allow_slow_non_contiguous=True, inc="win")
        v = "vector"
        P.wait(v, "win", win_done)
        for dt in range(8):
            P.op(v, "tensor_scalar", a_wo[:, dt * D:(dt + 1) * D], a_wo[:, dt * D:(dt + 1) * D],
                 a_nw[:, dt:dt + 1], None, ALU.mult, inc="dve" if dt == 7 else None)

    def emit_conversion():
        g = "gpsimd"
        w1flat = w1_d.rearrange("e d f -> (e d) f")
        w2flat = w2_d.rearrange("e f d -> (e f) d")
        for i in range(8):
            P.op(g, "dma_start", w1sc[i * 1024:(i + 1) * 1024, :],
                 w1flat[i * 1024:(i + 1) * 1024, :], inc="cvo0")
        for i in range(8):
            P.op(g, "dma_start", w2sc[i * 4096:(i + 1) * 4096, :],
                 w2flat[i * 4096:(i + 1) * 4096, :], inc="cvo0")
        return 16

    def emit_attention(b):
        v, t, a, g = "vector", "tensor", "scalar", "gpsimd"
        win_need = P.cnt["win"]
        xin_need = P.cnt["xin"]
        P.wait(v, "win", win_need)
        P.wait(v, "xin", xin_need)
        P.op(v, "memset", a_hk, 0.0)
        P.op(v, "memset", a_hv, 0.0)
        P.wait(t, "win", win_need)
        P.wait(t, "xin", xin_need)
        p_x_b = p_x[:].bitcast(BF16)

        px_state = {}
        XS_sg_gch = [0, 0]
        XS_bca_act = [0, 0]
        XS_bl_act = [0, 0]

        def evac_done(eng_sem):
            px_state["free"] = (eng_sem, P.cnt[eng_sem])

        def px_wait():
            if "free" in px_state:
                P.wait(t, px_state["free"][0], px_state["free"][1])

        def make_x_groups(s):
            S = XS[s % 2]
            xs = {}

            def g1():
                xt = x_buf[:, s * D:(s + 1) * D]
                P.op(v, "tensor_reduce", a_xsm[:, 0:1], xt, axis=AX.X, op=ALU.add)
                P.op(v, "tensor_scalar", a_xsm[:, 1:2], a_xsm[:, 0:1], -1.0 / D, None,
                     ALU.mult)
                P.op(v, "tensor_scalar", a_xc[:], xt, a_xsm[:, 1:2], None, ALU.add)
                P.op(v, "tensor_tensor", a_xsq, a_xc[:], a_xc[:], ALU.mult)
                d1 = P.op(v, "tensor_reduce", a_xsm[:, 2:3], a_xsq, axis=AX.X,
                          op=ALU.add, inc="dve")
                P.wait(a, "dve", d1)
                P.op(a, "activation", a_xsm[:, 3:4], a_xsm[:, 2:3], AF.Ln,
                     bias=c_eps[:], scale=1.0 / D)
                r1 = P.op(a, "activation", a_xsm[:, 4:5], a_xsm[:, 3:4], AF.Exp,
                          scale=-0.5, inc="act")
                P.wait(v, "act", r1)
                xs["apply"] = P.op(v, "tensor_scalar", a_xns, a_xc[:], a_xsm[:, 4:5],
                                   None, ALU.mult, inc="dve")

            def g2():
                P.wait(t, "dve", xs["apply"])
                px_wait()
                for dt in range(8):
                    p1 = P.op(t, "transpose", p_x_b[:, dt * 128:(dt + 1) * 128],
                              a_xns[:, dt * 128:(dt + 1) * 128], id128[:],
                              inc="pe" if dt == 7 else None)
                P.wait(a, "pe", p1)
                P.op(a, "activation", S["xnt"], p_x_b[:, 0:1024], AF.Copy, inc="act")
                evac_done("act")

            def proj(dname, wsb, half):
                def f():
                    dst = a_qraw if dname == "qt_raw" else S[dname]
                    px_wait()
                    for dt in range(8):
                        p1 = P.op(t, "matmul", p_x[:],
                                  S["xnt"][:, dt * SEG:(dt + 1) * SEG],
                                  wsb[:, dt * D + half * 512:
                                      dt * D + half * 512 + 512],
                                  start=(dt == 0), stop=(dt == 7),
                                  inc="pe" if dt == 7 else None)
                    P.wait(a, "pe", p1)
                    P.op(a, "activation", dst[:, half * 512:half * 512 + 512],
                         p_x[:], AF.Copy, inc="act")
                    evac_done("act")
                return f

            def g6():
                px_wait()
                for dt in range(8):
                    p1 = P.op(t, "matmul", p_x[:, 0:SEG],
                              a_wf1[:, dt * DK:(dt + 1) * DK],
                              S["xnt"][:, dt * SEG:(dt + 1) * SEG],
                              start=(dt == 0), stop=(dt == 7),
                              inc="pe" if dt == 7 else None)
                P.wait(a, "pe", p1)
                e2 = P.op(a, "activation", a_ut[:], p_x[:, 0:SEG], AF.Copy,
                          inc="act")
                evac_done("act")
                px_wait()
                P.wait(t, "act", e2)
                p1 = P.op(t, "matmul", p_x[:], a_ut[:], a_wf2[:], start=True,
                          stop=True, inc="pe")
                P.wait(a, "pe", p1)
                xs["ftanh"] = P.op(a, "activation", S["sig"], p_x[:], AF.Tanh,
                                   scale=0.5, inc="act")
                evac_done("act")

            def g6b():
                px_wait()
                for dt in range(8):
                    p1 = P.op(t, "matmul", p_x[:, 0:SEG],
                              a_wg1[:, dt * DK:(dt + 1) * DK],
                              S["xnt"][:, dt * SEG:(dt + 1) * SEG],
                              start=(dt == 0), stop=(dt == 7),
                              inc="pe" if dt == 7 else None)
                P.wait(a, "pe", p1)
                xs["ugt"] = P.op(a, "activation", a_ugt[:], p_x[:, 0:SEG], AF.Copy,
                                 inc="act")
                evac_done("act")

            def g6c(half):
                def f():
                    px_wait()
                    P.wait(t, "act", xs["ugt"])
                    p1 = P.op(t, "matmul", p_x[:], a_ugt[:],
                              a_wg2[:, half * 512:half * 512 + 512], start=True,
                              stop=True, inc="pe")
                    P.wait(a, "pe", p1)
                    c = P.op(a, "activation",
                             S["sg"][:, half * 512:half * 512 + 512],
                             p_x[:], AF.Tanh, scale=0.5, inc="act")
                    if half == 1:
                        xs["gtanh"] = c
                    evac_done("act")
                return f

            def g7qk():
                gp = "gpsimd"
                s1 = P.op(a, "activation", a_xact, a_qraw, AF.Tanh, scale=0.5,
                          inc="act")
                s2 = P.op(a, "activation", a_xns, S["ktm"], AF.Tanh, scale=0.5,
                          inc="act")
                P.wait(gp, "act", s1)
                P.op(gp, "tensor_scalar", a_xact, a_xact, 0.5, 0.5, ALU.mult,
                     op1=ALU.add)
                xs["qsilu"] = P.op(gp, "tensor_tensor", a_qraw, a_qraw, a_xact,
                                   ALU.mult)
                P.wait(gp, "act", s2)
                P.op(gp, "tensor_scalar", a_xns, a_xns, 0.5, 0.5, ALU.mult,
                     op1=ALU.add)
                xs["ksilu"] = P.op(gp, "tensor_tensor", S["ktm"], S["ktm"], a_xns,
                                   ALU.mult)

            def g7fg():
                gp = "gpsimd"
                P.wait(gp, "act", xs["ftanh"])
                xs["sig_gch"] = P.op(gp, "tensor_scalar", S["sig"], S["sig"],
                                     0.5, 0.5, ALU.mult, op1=ALU.add)
                P.wait(gp, "act", xs["gtanh"])
                XS_sg_gch[s % 2] = P.op(gp, "tensor_scalar", S["sg"], S["sg"],
                                        0.5, 0.5, ALU.mult, op1=ALU.add)

            def g8a():
                P.wait(t, "gch", xs["qsilu"])
                px_wait()
                for dt in range(8):
                    p1 = P.op(t, "transpose", p_x_b[:, dt * 128:(dt + 1) * 128],
                              a_qraw[:, dt * 128:(dt + 1) * 128], id128[:],
                              inc="pe" if dt == 7 else None)
                P.wait(a, "pe", p1)
                P.op(a, "activation", S["qt"], p_x_b[:, 0:1024], AF.Copy, inc="act")
                evac_done("act")

            def g8b():
                P.wait(t, "gch", xs["ksilu"])
                px_wait()
                for dt in range(8):
                    p1 = P.op(t, "transpose", p_x_b[:, dt * 128:(dt + 1) * 128],
                              S["ktm"][:, dt * 128:(dt + 1) * 128], id128[:],
                              inc="pe" if dt == 7 else None)
                P.wait(a, "pe", p1)
                P.op(a, "activation", S["kt"], p_x_b[:, 0:1024], AF.Copy, inc="act")
                evac_done("act")

            def g9prep():
                P.wait(a, "dve", xs["apply"])
                P.wait(a, "gch", xs["sig_gch"])
                l1 = P.op(a, "activation", a_xc[:, 0:512], S["sig"], AF.Ln, inc="act")
                P.wait(v, "act", l1)
                xs["dlt_dve"] = P.op(v, "tensor_scalar", S["dlt"], a_xc[:, 0:512],
                                     -C0, None, ALU.add, inc="dve")
                P.wait(v, "gch", xs["sig_gch"])
                xs["g9_d1"] = P.op(v, "tensor_scalar", S["sig"], S["sig"], -1.0, 1.0,
                                   ALU.mult, op1=ALU.add, inc="dve")

            def g9():
                P.wait(t, "dve", xs["g9_d1"])
                px_wait()
                for pi in range(2):
                    rs2 = slice(pi * 64, pi * 64 + 64)
                    p1 = P.op(t, "matmul", p_x[rs2, :], triI[rs2, :], S["dlt"][rs2, :],
                              start=True, stop=True, tile_position=(pi * 64, pi * 64),
                              inc="pe" if pi == 1 else None)
                P.wait(a, "pe", p1)
                P.op(a, "activation", S["B"], p_x[:], AF.Exp, bias=c_lin[:])
                e1 = P.op(a, "activation", a_xact[:, 0:512], p_x[:], AF.Exp, bias=c_nlin[:],
                          scale=-1.0, inc="act")
                P.wait(v, "act", e1)
                d2 = P.op(v, "tensor_tensor", S["stl"], S["sig"], a_xact[:, 0:512], ALU.mult,
                          inc="dve")
                xs["g9_d2"] = d2
                xs["g9_e1"] = e1

            def g9b():
                P.wait(t, "dve", xs["g9_d2"])
                P.wait(t, "act", xs["g9_e1"])
                px_wait()
                for pi in range(2):
                    rs2 = slice(pi * 64, pi * 64 + 64)
                    p1 = P.op(t, "matmul", p_x[rs2, :], triL[rs2, :], S["dlt"][rs2, :],
                              start=True, stop=True, tile_position=(pi * 64, pi * 64),
                              inc="pe" if pi == 1 else None)
                P.wait(a, "pe", p1)
                P.wait(a, "dve", xs["g9_d2"])
                e2 = P.op(a, "activation", a_xact[:, 0:512], p_x[:], AF.Exp, bias=c_lrev[:],
                          inc="act")
                P.wait(v, "act", e2)
                P.op(v, "tensor_tensor", S["stb"], S["sig"], a_xact[:, 0:512], ALU.mult,
                     inc="dve")
                evac_done("act")

            def g10a():
                P.wait(t, "dve", xs["dlt_dve"])
                for c4 in range(2):
                    rs2 = slice(c4 * 64, c4 * 64 + 64)
                    px_wait()
                    p1 = P.op(t, "matmul", p_x[:], ones128[rs2, :],
                              S["dlt"][rs2, :], start=True, stop=True,
                              tile_position=(c4 * 64, 0), inc="pe")
                    P.wait(a, "pe", p1)
                    XS_bca_act[s % 2] = P.op(a, "activation",
                                             S["bca"][:, c4 * HM:(c4 + 1) * HM],
                                             p_x[:], AF.Exp, bias=c_c64[:],
                                             inc="act")
                    evac_done("act")

            def g10b():
                for c4 in range(2):
                    rs2 = slice(c4 * 64, c4 * 64 + 64)
                    px_wait()
                    for h in range(H):
                        for base in (0, 64):
                            p1 = P.op(t, "matmul", p_x[base:base + 64, h:h + 1],
                                      S["dlt"][rs2, h * 64:(h + 1) * 64],
                                      onescol[rs2, :], start=True, stop=True,
                                      tile_position=(c4 * 64, base),
                                      inc="pe" if (h == 7 and base == 64) else None)
                    P.wait(a, "pe", p1)
                    XS_bl_act[s % 2] = P.op(a, "activation",
                                            S["bl"][:, c4 * 8:c4 * 8 + 8],
                                            p_x[:, 0:8], AF.Exp, bias=c_c64[:],
                                            inc="act")
                    evac_done("act")

            return [g1, g2, proj("qt_raw", a_wq, 0), proj("qt_raw", a_wq, 1),
                    proj("ktm", a_wk, 0), proj("ktm", a_wk, 1), g7qk,
                    g6, g6b, g6c(0), g6c(1), g7fg, g9prep, g8a, g8b, g9, g9b,
                    g10a, proj("vtm", a_wv, 0), proj("vtm", a_wv, 1), g10b]

        ys = {"state_dve": 0, "post_dve": 0}

        def emit_y(s, inject):
            S = XS[s % 2]
            for c4 in range(2):
                pi = c4
                rs = slice(pi * 64, pi * 64 + 64)
                P.wait(t, "dve", ys["state_dve"])
                P.wait(t, "dve", ys["post_dve"])
                for h in range(H):
                    tc = slice(h * SEG + c4 * 64, h * SEG + c4 * 64 + 64)
                    pat = P.op(t, "matmul", p_a[rs, h * 64:h * 64 + 64],
                               S["kt"][:, tc], S["qt"][:, tc], start=True, stop=True,
                               tile_position=(0, pi * 64),
                               inc="pe" if h == 7 else None)
                P.wait(v, "pe", pat)
                atm_dve = P.op(v, "tensor_tensor", a_atm[rs, :], p_a[rs, :],
                               triI8[rs, :], ALU.mult, inc="dve")
                inject()
                P.wait(t, "dve", atm_dve)
                for h in range(H):
                    tc = slice(h * SEG + c4 * 64, h * SEG + c4 * 64 + 64)
                    oc = slice(h * 64, h * 64 + 64)
                    P.op(t, "matmul", p_ok[rs, oc], a_atm[rs, oc],
                         S["stl"][rs, h * 64:(h + 1) * 64],
                         start=True, stop=False, tile_position=(pi * 64, pi * 64))
                    pok = P.op(t, "matmul", p_ok[rs, oc], S["qt"][:, tc],
                               a_hk[:, oc], start=False, stop=True,
                               tile_position=(0, pi * 64),
                               inc="pe" if h == 7 else None)
                inject()
                for h in range(H):
                    P.op(t, "matmul", p_st[:, h * 64:h * 64 + 64],
                         S["ktm"][rs, h * 128:(h + 1) * 128],
                         S["stb"][rs, h * 64:(h + 1) * 64],
                         start=True, stop=True, tile_position=(pi * 64, 0),
                         inc="pe" if h == 7 else None)
                inject()
                P.wait(v, "pe", pok)
                okb = a_otmp[rs, 0:512]
                qvt = a_otmp[rs, 512:1024]
                okb3 = okb.rearrange("p (h m) -> p h m", m=64)
                qvt3 = qvt.rearrange("p (h m) -> p h m", m=64)
                okb_dve = P.op(v, "tensor_tensor", okb, p_ok[rs, :], S["B"][rs, :],
                               ALU.mult, inc="dve")
                P.op(v, "tensor_reduce", a_nmx[rs, :], okb3, axis=AX.X, op=ALU.max,
                     negate=True)
                sm_dve = P.op(v, "tensor_tensor", qvt3, okb3,
                              a_nmx[rs, :].unsqueeze(2).broadcast_to([64, 8, 64]),
                              ALU.add, inc="dve")
                P.wait(a, "dve", sm_dve)
                sm_act = P.op(a, "activation", okb, qvt, AF.Exp, inc="act")
                P.wait(v, "act", sm_act)
                P.op(v, "tensor_reduce", a_ssum[rs, :], okb3, axis=AX.X, op=ALU.add)
                P.op(v, "reciprocal", a_rsum[rs, :], a_ssum[rs, :])
                P.op(v, "tensor_tensor", qvt3, okb3,
                     a_rsum[rs, :].unsqueeze(2).broadcast_to([64, 8, 64]), ALU.mult)
                qb_dve = P.op(v, "tensor_tensor", a_qb[rs, :], qvt, S["B"][rs, :],
                              ALU.mult, inc="dve")
                P.wait(t, "dve", okb_dve)
                for h in range(H):
                    sbc = S["stb"][rs, h * 64:(h + 1) * 64]
                    vv = S["vtm"][rs, h * 128:(h + 1) * 128]
                    bank2 = p_st2 if h < 4 else p_ok
                    for base in (0, 64):
                        pov_st = P.op(t, "matmul",
                                      bank2[base:base + 64,
                                            (h % 4) * 128:(h % 4 + 1) * 128],
                                      sbc, vv, start=True, stop=True,
                                      tile_position=(pi * 64, base),
                                      inc="pe" if (h == 7 and base == 64) else None)
                inject()
                P.wait(t, "dve", qb_dve)
                for h in range(H):
                    oc = h * 64
                    P.op(t, "transpose", p_tp[rs, oc:oc + 64],
                         a_qb[rs, h * 64:h * 64 + 64], id64[rs, :],
                         tile_position=(pi * 64, pi * 64))
                    ptr = P.op(t, "transpose", p_tp[rs, 512 + oc:512 + oc + 64],
                               S["stl"][rs, h * 64:h * 64 + 64], id64[rs, :],
                               tile_position=(pi * 64, pi * 64),
                               inc="pe" if h == 7 else None)
                P.wait(v, "pe", ptr)
                P.op(v, "tensor_copy", a_qbt[rs, :], p_tp[rs, 0:512])
                tt_dve = P.op(v, "tensor_copy", a_stT[rs, :], p_tp[rs, 512:1024],
                              inc="dve")
                P.wait(t, "dve", tt_dve)
                for h in range(H):
                    oc = h * 64
                    pat2 = P.op(t, "matmul", p_a[rs, h * 64:h * 64 + 64],
                                a_stT[rs, oc:oc + 64], a_qbt[rs, oc:oc + 64],
                                start=True, stop=True,
                                tile_position=(pi * 64, pi * 64),
                                inc="pe" if h == 7 else None)
                P.wait(v, "pe", pat2)
                atm2_dve = P.op(v, "tensor_tensor", a_atm[rs, :], p_a[rs, :],
                                triI8[rs, :], ALU.mult, inc="dve")
                inject()
                P.wait(t, "dve", atm2_dve)
                for h in range(H):
                    oc = h * 64
                    vv = S["vtm"][rs, h * 128:(h + 1) * 128]
                    P.op(t, "matmul", p_ov[rs, h * 128:(h + 1) * 128],
                         a_atm[rs, h * 64:h * 64 + 64], vv,
                         start=True, stop=False, tile_position=(pi * 64, pi * 64))
                    ov_pe = P.op(t, "matmul", p_ov[rs, h * 128:(h + 1) * 128],
                                 a_qbt[rs, oc:oc + 64],
                                 a_hv[rs, h * 128:(h + 1) * 128],
                                 start=False, stop=True,
                                 tile_position=(pi * 64, pi * 64),
                                 inc="pe" if h == 7 else None)
                inject()
                P.wait(v, "pe", ov_pe)
                P.wait(v, "act", XS_bca_act[s % 2])
                P.op(v, "tensor_tensor", a_hkt, a_hk,
                     S["bca"][:, c4 * HM:(c4 + 1) * HM], ALU.mult)
                P.op(v, "tensor_tensor", a_hk, a_hkt, p_st[:], ALU.add)
                P.wait(v, "act", XS_bl_act[s % 2])
                for h in range(H):
                    bank2 = p_st2 if h < 4 else p_ok
                    sd = P.op(v, "scalar_tensor_tensor",
                              a_hv[:, h * 128:(h + 1) * 128],
                              a_hv[:, h * 128:(h + 1) * 128],
                              S["bl"][:, c4 * 8 + h:c4 * 8 + h + 1],
                              bank2[:, (h % 4) * 128:(h % 4 + 1) * 128],
                              ALU.mult, ALU.add, inc="dve" if h == 7 else None)
                ys["state_dve"] = sd

            P.wait(v, "gch", XS_sg_gch[s % 2])
            P.op(v, "tensor_tensor", a_otmp, p_ov[:], S["sg"], ALU.mult)
            P.op(v, "tensor_tensor", a_o2, a_otmp, a_otmp, ALU.mult)
            rm_dve = P.op(v, "tensor_reduce", a_sm1[:, 8:16],
                          a_o2.rearrange("p (h d) -> p h d", d=128), axis=AX.X,
                          op=ALU.add, inc="dve")
            P.wait(a, "dve", rm_dve)
            P.op(a, "activation", a_sm1[:, 0:8], a_sm1[:, 8:16], AF.Ln,
                 bias=c_eps[:], scale=1.0 / 128)
            rm_act = P.op(a, "activation", a_sm1[:, 8:16], a_sm1[:, 0:8], AF.Exp,
                          scale=-0.5, inc="act")
            P.wait(v, "act", rm_act)
            for h in range(H):
                o2_dve = P.op(v, "tensor_scalar", a_o2[:, h * 128:(h + 1) * 128],
                              a_otmp[:, h * 128:(h + 1) * 128],
                              a_sm1[:, 8 + h:9 + h], None, ALU.mult,
                              inc="dve" if h == 7 else None)
            inject(2)
            P.wait(t, "dve", o2_dve)
            for dt in range(8):
                po2 = P.op(t, "transpose", p_tp[:, dt * 128:(dt + 1) * 128],
                           a_o2[:, dt * 128:(dt + 1) * 128], id128[:],
                           inc="pe" if dt == 7 else None)
            P.wait(v, "pe", po2)
            o2t_dve = P.op(v, "tensor_copy", a_o2t, p_tp[:, 0:1024], inc="dve")
            inject(2)
            P.wait(t, "dve", o2t_dve)
            for dt in range(8):
                for dh in range(2):
                    pwo = P.op(t, "matmul", p_ov[:, dh * 512:dh * 512 + 512],
                               a_o2t[:, dt * 128:(dt + 1) * 128],
                               a_wo[:, dt * D + dh * 512: dt * D + dh * 512 + 512],
                               start=(dt == 0), stop=(dt == 7),
                               inc="pe" if (dt == 7 and dh == 1) else None)
            inject(2)
            P.wait(v, "pe", pwo)
            ys["post_dve"] = P.op(v, "tensor_tensor", x_buf[:, s * D:(s + 1) * D],
                                  x_buf[:, s * D:(s + 1) * D], p_ov[:], ALU.add,
                                  inc="dve")

        for gr in make_x_groups(0):
            gr()
        pend = []

        def inject(k=1):
            for _ in range(k):
                if pend:
                    pend.pop(0)()
        for s in range(NSEG):
            if s + 1 < NSEG:
                pend = make_x_groups(s + 1)
                pend.pop(0)()
            else:
                pend = []
            emit_y(s, inject)
            while pend:
                pend.pop(0)()

    def emit_ffn(b):
        v, t, a, g, sy = "vector", "tensor", "scalar", "gpsimd", "sync"
        P.bar(g, "pe", "dve", "act")
        P.bar(v, "pe", "act")
        P.bar(a, "pe", "dve")
        P.bar(t, "dve", "act")
        P.op(g, "dma_start", f_b1n[:], b1_d, inc="gs")
        P.op(g, "dma_start", f_b2a[:], b2_d, inc="gs")
        gsw = P.op(g, "dma_start", f_wr[:].rearrange("p (dt e) -> p dt e", e=E),
                   wr_d.rearrange("(dt p) e -> p dt e", p=128), inc="gs")
        P.wait(t, "gs", gsw)
        P.bar(t, "dve", "act")
        for fb in range(32):
            pb1 = P.op(t, "transpose", p_tp[0:128, fb * 8:fb * 8 + 8],
                       f_b1n[0:8, fb * 128:(fb + 1) * 128], id128[0:8, 0:8],
                       inc="pe" if fb == 31 else None)
        P.wait(v, "pe", pb1)
        b1t_dve = P.op(v, "tensor_copy", f_b1t[:], p_tp[:, 0:256], inc="dve")

        xnts = (f_xnt0, f_xnt1)
        wgs = (f_wg0, f_wg1)
        w1bufs = (f_w1a, f_w1b)
        w2bufs = (f_w2a, f_w2b, f_w2c, f_w2d)
        mm1banks = ((p_ov[:, 0:512], p_ov[:, 512:1024]), (p_st[:], p_st2[:]))
        accb = (p_a[:, 0:512], p_ok[:, 0:512], p_st[:, 0:512], p_st2[:, 0:512],
                p_ov[:, 0:512], p_ov[:, 512:1024], p_x[:, 0:512], p_tp_f[:, 0:512])
        bank_free = {i: ("dve", b1t_dve if i == 7 else 0) for i in range(8)}

        def ln2_router(grp):
            for f in ln2_parts(grp):
                f()

        def ln2_parts(grp):
            gx = xnts[grp % 2]
            gw = wgs[grp % 2]
            g0 = grp * (GRP // 128)
            l4 = [0]
            l5 = [0]
            rstd_act = [0]

            def stats_all():
                for tl in range(8):
                    xt = x_buf[:, (g0 + tl) * D:(g0 + tl + 1) * D]
                    P.wait(v, "pe", l4[0])
                    P.op(v, "tensor_reduce", f_sm[:, 0:1], xt, axis=AX.X, op=ALU.add)
                    P.op(v, "tensor_scalar", f_rt[:, tl:tl + 1], f_sm[:, 0:1],
                         -1.0 / D, None, ALU.mult)
                    P.op(v, "tensor_tensor", f_xns, xt, xt, ALU.mult)
                    P.op(v, "tensor_reduce", f_rt[:, 8 + tl:9 + tl], f_xns,
                         axis=AX.X, op=ALU.add)
                P.op(v, "tensor_scalar", f_rt[:, 8:16], f_rt[:, 8:16], 1.0 / D, None,
                     ALU.mult)
                P.op(v, "tensor_tensor", f_rt[:, 16:24], f_rt[:, 0:8], f_rt[:, 0:8],
                     ALU.mult)
                st = P.op(v, "tensor_sub", f_rt[:, 8:16], f_rt[:, 8:16],
                          f_rt[:, 16:24], inc="dve")
                P.wait(a, "dve", st)
                P.op(a, "activation", f_rt[:, 16:24], f_rt[:, 8:16], AF.Ln,
                     bias=c_eps[:])
                rstd_act[0] = P.op(a, "activation", f_rt[:, 16:24], f_rt[:, 16:24],
                                   AF.Exp, scale=-0.5, inc="act")

            def tile(tl):
                def f():
                    xt = x_buf[:, (g0 + tl) * D:(g0 + tl + 1) * D]
                    P.wait(v, "act", rstd_act[0])
                    P.wait(v, "pe", l4[0])
                    l3 = P.op(v, "tensor_scalar", f_xns, xt, f_rt[:, tl:tl + 1],
                              f_rt[:, 16 + tl:17 + tl], ALU.add, op1=ALU.mult,
                              inc="dve")
                    P.wait(t, "dve", l3)
                    P.wait(t, bank_free[7][0], bank_free[7][1])
                    for dt in range(8):
                        l4[0] = P.op(t, "transpose", p_tp[:, dt * 128:(dt + 1) * 128],
                                     f_xns[:, dt * 128:(dt + 1) * 128], id128[:],
                                     inc="pe" if dt == 7 else None)
                    P.wait(a, "pe", l4[0])
                    l5[0] = P.op(a, "activation",
                                 gx[:, 0:8192].rearrange("p (dt tk) -> p dt tk",
                                                         tk=GRP)
                                 [:, :, tl * 128:tl * 128 + 128],
                                 p_tp[:, 0:1024].rearrange("p (dt tk) -> p dt tk",
                                                           tk=128),
                                 AF.Copy, inc="act")
                    bank_free[7] = ("act", l5[0])
                return f

            def router():
                P.wait(t, "act", l5[0])
                P.wait(t, bank_free[6][0], bank_free[6][1])
                for tl in range(8):
                    for dt in range(8):
                        r1 = P.op(t, "matmul", p_x[:, tl * 8:tl * 8 + 8],
                                  gx[:, dt * GRP + tl * 128:
                                     dt * GRP + tl * 128 + 128],
                                  f_wr[:, dt * 8:(dt + 1) * 8],
                                  start=(dt == 0), stop=(dt == 7),
                                  inc="pe" if (tl == 7 and dt == 7) else None)
                P.wait(v, "pe", r1)
                L3 = p_x[:, 0:64].rearrange("p (tt e) -> p tt e", e=8)
                P.op(v, "tensor_reduce", f_sm[:, 8:16], L3, axis=AX.X, op=ALU.max,
                     negate=True)
                r2 = P.op(v, "tensor_tensor",
                          f_rt[:, 0:64].rearrange("p (tt e) -> p tt e", e=8), L3,
                          f_sm[:, 8:16].unsqueeze(2).broadcast_to([128, 8, 8]),
                          ALU.add, inc="dve")
                bank_free[6] = ("dve", r2)
                P.wait(a, "dve", r2)
                r3 = P.op(a, "activation", f_rt[:, 64:128], f_rt[:, 0:64], AF.Exp,
                          inc="act")
                P.wait(v, "act", r3)
                P.op(v, "tensor_reduce", f_sm[:, 8:16],
                     f_rt[:, 64:128].rearrange("p (tt e) -> p tt e", e=8), axis=AX.X,
                     op=ALU.add)
                P.op(v, "reciprocal", f_sm[:, 0:8], f_sm[:, 8:16])
                P.op(v, "tensor_scalar", f_rt[:, 64:128], f_rt[:, 0:64], 0.0, None,
                     ALU.is_equal)
                P.op(v, "tensor_tensor", gw[:].rearrange("p (tt e) -> p tt e", e=8),
                     f_rt[:, 64:128].rearrange("p (tt e) -> p tt e", e=8),
                     f_sm[:, 0:8].unsqueeze(2).broadcast_to([128, 8, 8]), ALU.mult,
                     inc="dve")

            return [stats_all] + [tile(i) for i in range(8)] + [router]

        pe_after_w1use = {}
        pe_after_w2use = {}
        gelu_after = {}

        ln2_router(0)
        ln2_pend = []
        for grp in range(NGRP):
            if grp + 1 < NGRP:
                ln2_pend = ln2_parts(grp + 1)
            g0 = grp * (GRP // 128)
            gx = xnts[grp % 2]
            gw = wgs[grp % 2]
            for e in range(E):
                eg = (b * NGRP + grp) * E + e
                w1row = w1sc[e * D:(e + 1) * D, :]
                glast = None
                for fbg in range(8):
                    gidx = eg * 8 + fbg
                    par = gidx % 2
                    psem = "w1%d" % par
                    if gidx >= 2:
                        P.wait(g, "pe", pe_after_w1use.get(gidx - 2))
                    ld = P.op(g, "dma_start",
                              w1bufs[par][:].rearrange("p (dt f) -> p dt f", f=512),
                              w1row[:, fbg * 512:(fbg + 1) * 512].rearrange(
                                  "(dt p) f -> p dt f", p=128), inc=psem)
                    P.wait(t, psem, ld)
                    if fbg == 0:
                        for i in (2, 3, 4, 5):
                            P.wait(t, bank_free[i][0], bank_free[i][1])
                        P.wait(t, "dve", b1t_dve)
                    for fi in range(4):
                        fb = fbg * 4 + fi
                        fgi = eg * 32 + fb
                        bsel = mm1banks[fi % 2]
                        if gelu_after.get(fgi - 2) is not None:
                            P.wait(t, "act", gelu_after[fgi - 2])
                        for dt in range(8):
                            for half in range(2):
                                m1 = P.op(t, "matmul", bsel[half],
                                          w1bufs[par][:, dt * 512 + fi * 128:
                                                       dt * 512 + fi * 128 + 128],
                                          gx[:, dt * GRP + half * 512:
                                             dt * GRP + half * 512 + 512],
                                          start=(dt == 0), stop=(dt == 7),
                                          inc="pe" if (dt == 7 and half == 1)
                                          else None)
                        P.wait(a, "pe", m1)
                        for half in range(2):
                            glast = P.op(a, "activation",
                                         f_ht[:, fb * GRP + half * 512:
                                              fb * GRP + half * 512 + 512],
                                         bsel[half], AF.Gelu,
                                         bias=f_b1t[:, fb * 8 + e:fb * 8 + e + 1],
                                         inc="act" if half == 1 else None)
                        gelu_after[fgi] = P.cnt["act"]
                    pe_after_w1use[gidx] = P.cnt["pe"]
                    if ln2_pend:
                        ln2_pend.pop(0)()
                P.wait(g, "gs", gsw)
                P.wait(g, "pe", pe_after_b2.get(eg - 1))
                b2done = P.op(g, "dma_start", f_b2s[:], f_b2a[e:e + 1, :], inc="gs")
                w2row = w2sc[e * F:(e + 1) * F, :]
                P.wait(t, "gs", b2done)
                for h5 in range(2):
                    gl = gelu_after[eg * 32 + 31]
                    for fb in range(32):
                        gidx2 = (eg * 2 + h5) * 32 + fb
                        par = fb % 4
                        psem = "w2%d" % par
                        if gidx2 >= 4:
                            P.wait(g, "pe", pe_after_w2use.get(gidx2 - 4))
                        ld = P.op(g, "dma_start", w2bufs[par][:],
                                  w2row[fb * 128:(fb + 1) * 128, :], inc=psem)
                        P.wait(t, psem, ld)
                        if fb == 0:
                            P.wait(t, "act", gl)
                        for j, i in enumerate((4, 5, 2, 3, 0, 1, 6, 7)):
                            tl, dh = i // 2, i % 2
                            if fb == 0:
                                P.wait(t, bank_free[i][0], bank_free[i][1])
                            m2 = P.op(t, "matmul", accb[i],
                                      f_ht[:, fb * GRP + h5 * 512 + tl * 128:
                                           fb * GRP + h5 * 512 + tl * 128 + 128],
                                      w2bufs[par][:, dh * 512:dh * 512 + 512],
                                      start=(fb == 0), stop=False,
                                      inc="pe" if j == 7 else None)
                        pe_after_w2use[gidx2] = P.cnt["pe"]
                    for tl in range(4):
                        for dh in range(2):
                            m3 = P.op(t, "matmul", accb[tl * 2 + dh], onesrow[:],
                                      f_b2s[0:1, dh * 512:dh * 512 + 512],
                                      start=False, stop=True, tile_position=(0, 0),
                                      inc="pe" if (tl == 3 and dh == 1) else None)
                    pe_after_b2[eg] = P.cnt["pe"]
                    P.wait(v, "pe", m3)
                    for j, i in enumerate((4, 5, 2, 3, 0, 1, 6, 7)):
                        tl, dh = i // 2, i % 2
                        TT = g0 + h5 * 4 + tl
                        xs2 = x_buf[:, TT * D + dh * 512: TT * D + dh * 512 + 512]
                        ev = P.op(v, "scalar_tensor_tensor", xs2, accb[i],
                                  gw[:, (h5 * 4 + tl) * 8 + e:
                                     (h5 * 4 + tl) * 8 + e + 1],
                                  xs2, ALU.mult, ALU.add, inc="dve")
                        bank_free[i] = ("dve", ev)
            P.bar(sy, "dve")
            ydst = y_d[b * N + grp * GRP: b * N + (grp + 1) * GRP, :].rearrange(
                "(tt p) d -> p tt d", p=128)
            P.op(sy, "dma_start", ydst,
                 x_buf[:, g0 * D:(g0 + 8) * D].rearrange("p (tt d) -> p tt d", d=D),
                 inc="yst")

    emit_consts()
    for b in range(BPC):
        emit_attn_weight_loads(b)
        P.bar("sync", "yst")
        P.op("sync", "dma_start", x_buf[:].rearrange("p (tt d) -> p tt d", d=D),
             x_d[b * N:(b + 1) * N, :].rearrange("(tt p) d -> p tt d", p=128), inc="xin")
        if b == 0 and DO_FFN:
            emit_conversion()
        if DO_ATTN:
            emit_attention(b)
        if DO_FFN:
            P.wait("gpsimd", "cvo0", P.cnt["cvo0"])
            emit_ffn(b)
        else:
            P.bar("sync", "dve")
            ydst = y_d[b * N:(b + 1) * N, :].rearrange("(tt p) d -> p tt d", p=128)
            P.op("sync", "dma_start", ydst, x_buf[:].rearrange("p (tt d) -> p tt d", d=D),
                 inc="yst")
    P.wait("sync", "yst", P.cnt["yst"])

    with es, nc.Block() as block:
        for eng in Prog.ENGINES:
            stream = P.streams[eng]

            def make(fns):
                def _(e):
                    for f in fns:
                        f(e)
                return _
            getattr(block, eng)(make(stream))

    return nc


_NC_CACHE = None


def kernel(**inputs):
    global _NC_CACHE
    xs = np.ascontiguousarray(np.asarray(inputs["L0_attn_x"], dtype=np.float32))
    w = {
        "wq": inputs["L0_attn_Wq"], "wk": inputs["L0_attn_Wk"], "wv": inputs["L0_attn_Wv"],
        "wf1": inputs["L0_attn_Wf1"], "wf2": inputs["L0_attn_Wf2"],
        "wg1": inputs["L0_attn_Wg1"], "wg2": inputs["L0_attn_Wg2"],
        "wo": inputs["L0_attn_Wo"], "normw": inputs["L0_attn_norm_w"],
        "wr": inputs["L0_ffn_Wr"], "w1": inputs["L0_ffn_W1"], "b1": inputs["L0_ffn_b1"],
        "w2": inputs["L0_ffn_W2"], "b2": inputs["L0_ffn_b2"],
    }
    w = {k: np.ascontiguousarray(np.asarray(v, dtype=np.float32)) for k, v in w.items()}
    if _NC_CACHE is None:
        _NC_CACHE = build_nc()
    nc = _NC_CACHE
    in_maps = []
    for c in range(NCORES):
        m = {"x": np.ascontiguousarray(xs[c * BPC:(c + 1) * BPC].reshape(BPC * N, D))}
        m.update(w)
        in_maps.append(m)
    res = run_bass_kernel_spmd(nc, in_maps, core_ids=list(range(NCORES)))
    out = np.concatenate([r["y"].reshape(BPC, N, D) for r in res.results], axis=0)
    return out
